# version 01
# baseline (speedup 1.0000x reference)
import os
import numpy as np
import ml_dtypes
import concourse.bass as bass
import concourse.mybir as mybir
from concourse.bass_utils import run_bass_kernel_spmd
from concourse.alu_op_type import AluOpType as ALU
from contextlib import ExitStack

F32 = mybir.dt.float32
BF16 = mybir.dt.bfloat16
AF = mybir.ActivationFunctionType

NC = 8
H = 2048
HCN = 16
TOK = 1024
S = 8192
NH = 16
HD = 128
E = 16
ICN = 11
SCN = 22
LH = 2
QTN = 16
CSTAB = 12.0
NEGB = -30000.0

QSECT = NH * HD * TOK
SEC = 3 * QSECT
KOFF = QSECT
VOFF = 2 * QSECT
CTXSECT = LH * HD * S


class Ctr:
    __slots__ = ("h", "n")
    def __init__(self):
        self.h = None
        self.n = 0
    def inc(self, k=1):
        self.n += k
        return self.n


class KB:
    ENGS = ("tensor", "scalar", "vector", "gpsimd", "sync")

    def __init__(self, nc):
        self.nc = nc
        self.q = {e: [] for e in self.ENGS}
        self.sP = Ctr(); self.sA = Ctr(); self.sV = Ctr(); self.sC = Ctr()
        self.all_ctrs = [self.sP, self.sA, self.sV, self.sC]
        self.rt = {}

    def em(self, eng, fn):
        self.q[eng].append(fn)

    def wait(self, eng, ctr, v):
        if v:
            self.em(eng, lambda e, c=ctr, vv=v: e.wait_ge(c.h, vv))

    def dma(self, eng, ctr, out, in_, **kw):
        v = ctr.inc(16)
        self.em(eng, lambda e, o=out, i=in_, c=ctr, k=kw:
                e.dma_start(out=o, in_=i, **k).then_inc(c.h, 16))
        return v

    def dma_dyn(self, eng, ctr, out, tensor, off_static, dyn_key, apdims, **kw):
        v = ctr.inc(16)
        def fn(e, o=out, t=tensor, os_=off_static, dk=dyn_key, ad=apdims, c=ctr, en=eng, k=kw):
            src = bass.AP(t, self.rt[en][dk] + os_, [list(d) for d in ad])
            e.dma_start(out=o, in_=src, **k).then_inc(c.h, 16)
        self.em(eng, fn)
        return v

    def op(self, eng, ctr, fn):
        if eng == "vector":
            c = self.sV
            v = c.inc(1)
            self.em(eng, lambda e, f=fn, cc=c: f(e).then_inc(cc.h, 1))
            self.em(eng, lambda e, cc=c, vv=v: e.wait_ge(cc.h, vv))
            return v
        if ctr is None:
            self.em(eng, fn)
            return None
        v = ctr.inc(1)
        self.em(eng, lambda e, f=fn, c=ctr: f(e).then_inc(c.h, 1))
        return v

    def barrier(self):
        vals = [(c, c.n) for c in self.all_ctrs if c.n]
        for eng in self.ENGS:
            for c, v in vals:
                self.wait(eng, c, v)

    def init_rt(self):
        for eng in ("sync", "gpsimd"):
            def fn(e, en=eng):
                pid = e.partition_id()
                self.rt[en] = {
                    "kv": pid * 262144,
                    "clo": pid * 512,
                    "chi": pid * (-512) + 7680,
                }
            self.em(eng, fn)

    def run(self, block):
        for eng in self.ENGS:
            thunks = self.q[eng]
            def make(ts):
                def body(e):
                    for t in ts:
                        t(e)
                return body
            getattr(block, eng)(make(thunks))


def build_program(dump=(), upto="out", n_exp=E):
    nc = bass.Bass("TRN2", target_bir_lowering=False, debug=False, num_devices=NC)

    xin = nc.dram_tensor("xin", [TOK, H], F32, kind="ExternalInput").ap()
    out = nc.dram_tensor("out", [TOK, H], F32, kind="ExternalOutput").ap()
    wqkv = nc.dram_tensor("wqkv", [48, 128, 2048], F32, kind="ExternalInput").ap()
    wdense = nc.dram_tensor("wdense", [16, 128, 2048], F32, kind="ExternalInput").ap()
    wg = nc.dram_tensor("wg", [E, ICN, 128, 2048], F32, kind="ExternalInput").ap()
    wu = nc.dram_tensor("wu", [E, ICN, 128, 2048], F32, kind="ExternalInput").ap()
    wd = nc.dram_tensor("wd", [E, HCN, 128, 1408], F32, kind="ExternalInput").ap()
    wsg = nc.dram_tensor("wsg", [SCN, 128, 2048], F32, kind="ExternalInput").ap()
    wsu = nc.dram_tensor("wsu", [SCN, 128, 2048], F32, kind="ExternalInput").ap()
    wsd = nc.dram_tensor("wsd", [HCN, 128, 2816], F32, kind="ExternalInput").ap()
    gwt = nc.dram_tensor("gwt", [HCN, 128, 16], F32, kind="ExternalInput").ap()
    escb = nc.dram_tensor("escb", [1, 128], F32, kind="ExternalInput").ap()
    nramp = nc.dram_tensor("nramp", [1, 128], F32, kind="ExternalInput").ap()
    tablep = nc.dram_tensor("tablep", [32, 16], F32, kind="ExternalInput").ap()
    onehot = nc.dram_tensor("onehot", [32, 768], F32, kind="ExternalInput").ap()
    e31 = nc.dram_tensor("e31", [32, 1], F32, kind="ExternalInput").ap()
    identf_d = nc.dram_tensor("identf", [128, 128], F32, kind="ExternalInput").ap()
    cqr = nc.dram_tensor("cqr", [128, 1024], F32, kind="ExternalInput").ap()
    sqr = nc.dram_tensor("sqr", [128, 1024], F32, kind="ExternalInput").ap()
    ckr = nc.dram_tensor("ckr", [128, 1024], F32, kind="ExternalInput").ap()
    skr = nc.dram_tensor("skr", [128, 1024], F32, kind="ExternalInput").ap()

    kvl = nc.dram_tensor("kvl", [SEC], BF16, kind="Internal").ap()
    kvgt = nc.dram_tensor("kvg", [NC * SEC], BF16, kind="Internal")
    ctxl = nc.dram_tensor("ctxl", [CTXSECT], BF16, kind="Internal").ap()
    ctxgt = nc.dram_tensor("ctxg", [NC * CTXSECT], BF16, kind="Internal")
    biasd = nc.dram_tensor("biasd", [LH * 1408], F32, kind="Internal").ap()
    kvh = nc.dram_tensor("kvh", [LH * 8 * 3 * 128 * 1024], BF16, kind="Internal").ap()
    ctxh = nc.dram_tensor("ctxh", [2048 * 1024], BF16, kind="Internal").ap()
    combd = nc.dram_tensor("combd", [E, TOK], F32, kind="Internal").ap()

    dbg = {}
    for name in dump:
        dbg[name] = nc.dram_tensor("dbg_" + name, [H, TOK], F32, kind="ExternalOutput").ap()

    es = ExitStack()
    with es:
        sb = lambda name, shape, dt: es.enter_context(nc.sbuf_tensor(name, shape, dt))
        xres = [sb(f"xres{i}", [128, 1024], F32) for i in range(HCN)]
        yb = [sb(f"yb{i}", [128, 1024], BF16) for i in range(HCN)]
        inter = sb("inter", [128, 11264], BF16)
        wblk = [sb(f"wblk{i}", [128, 2048], BF16) for i in range(4)]
        cmb = [sb(f"cmb{i}", [128, 1024], F32) for i in range(2)]
        gsb = [sb(f"gsb{i}", [128, 512], F32) for i in range(2)]
        Asb = sb("Asb", [128, 1024], F32)
        Bsb = sb("Bsb", [128, 1024], F32)
        sq2 = [sb(f"sq{i}", [128, 1024], BF16) for i in range(2)]
        identf = sb("identf_sb", [128, 128], F32)
        identb = sb("identb_sb", [128, 128], BF16)
        gwb = sb("gwb", [128, 256], BF16)
        tps = sb("tps", [32, 16], F32)
        ohsb = sb("ohsb", [32, 768], F32)
        e31sb = sb("e31sb", [32, 1], F32)
        biasrow = sb("biasrow", [2, 1408], F32)
        onescolb = sb("onescolb", [128, 1], BF16)
        onesrf = sb("onesrf", [128, 128], F32)
        onesr = lambda b: onesrf.ap()[b:b + 1, 0:128]
        cstC = sb("cstC", [128, 1], F32)
        cstEps = sb("cstEps", [128, 1], F32)
        rows = [sb(f"row{i}", [1, 1024], F32) for i in range(5)]
        R_M, R_V, R_A, R_B, R_T = [r.ap() for r in rows]
        R_DR0, R_DR1 = R_M, R_V
        R_CR0, R_CR1 = R_M, R_V
        farb = sb("farb", [128, 2], F32)
        bbrep = sb("bbrep", [128, 128], F32)
        nrrep = sb("nrrep", [128, 128], F32)
        ga = sb("ga", [128, 128], F32)
        gb = sb("gb", [128, 128], F32)
        gc = sb("gc", [128, 128], F32)
        gd = sb("gd", [128, 128], F32)
        ge = sb("ge", [128, 128], F32)
        gsm = sb("gsm", [128, 32], F32)
        r16 = sb("r16", [16, 1024], F32)
        qsb = sb("qsb", [128, 1024], BF16)
        ksb = sb("ksb", [128, 1024], BF16)
        vsb = sb("vsb", [128, 1024], BF16)
        vt = [sb(f"vt{i}", [128, 128], BF16) for i in range(2)]
        ctxsb = [sb(f"ctxsb{i}", [128, 512], BF16) for i in range(2)]
        outh = [Asb, Bsb]

        w0b = wblk[0].ap()
        qtl = [w0b[:, 0:512], w0b[:, 512:1024]]
        kblk = [w0b[:, 1024:1536], w0b[:, 1536:2048]]
        w1b = wblk[1].ap()
        vch = [w1b[:, i * 128:(i + 1) * 128] for i in range(8)]
        psb = [w1b[:, 1024:1536], w1b[:, 1536:2048]]
        w2f = wblk[2].ap().bitcast(F32)
        tmpf = [w2f[:, 0:512], w2f[:, 512:1024]]
        w3f = wblk[3].ap().bitcast(F32)
        bcsb = [w3f[:, 0:512], w3f[:, 512:1024]]
        interf = inter.ap().bitcast(F32)
        cqrep = interf[:, 0:1024]
        sqrep = interf[:, 1024:2048]
        ckrep = interf[:, 2048:3072]
        skrep = interf[:, 3072:4096]
        re1 = interf[:, 4096:4608]
        re2 = interf[:, 4608:5120]
        re2s = interf[:, 5120:5632]
        interb = inter.ap()
        def ichunk(kk, t):
            return interb[:, kk * 1024 + t * 512: kk * 1024 + (t + 1) * 512]

        pb = [es.enter_context(nc.psum_tensor(f"pb{i}", [128, 512], F32)) for i in range(7)]
        pt0 = es.enter_context(nc.psum_tensor("pt0", [128, 128], BF16))

        k = KB(nc)
        for nm in ("sP", "sA", "sV", "sC"):
            getattr(k, nm).h = es.enter_context(nc.semaphore(nm))

        def newsem(nm):
            c = Ctr()
            c.h = es.enter_context(nc.semaphore(nm))
            k.all_ctrs.append(c)
            return c
        s_wb = [newsem(f"s_wb{i}") for i in range(4)]
        s_qtl = [newsem(f"s_qtl{i}") for i in range(2)]
        s_kb = [newsem(f"s_kb{i}") for i in range(2)]
        s_v = [newsem(f"s_v{i}") for i in range(8)]
        s_tb = [newsem(f"s_tb{i}") for i in range(2)]
        s_cr = [newsem(f"s_cr{i}") for i in range(2)]
        s_cst = [newsem(f"s_cst{i}") for i in range(2)]
        s_vst = [newsem(f"s_vst{i}") for i in range(2)]
        s_out = [newsem(f"s_out{i}") for i in range(2)]
        s_pc = newsem("s_pc")
        s_gw = newsem("s_gw")
        s_rt = newsem("s_rt")
        s_sw = newsem("s_sw")
        s_qs = newsem("s_qs")
        s_ks = newsem("s_ks")
        s_dld = newsem("s_dld")
        s_cmw = newsem("s_cmw")
        s_bw = newsem("s_bw")
        s_xl = newsem("s_xl")
        s_kvh = [newsem(f"s_kvh{i}") for i in range(2)]
        s_cxh = newsem("s_cxh")
        s_dbg = newsem("s_dbg")
        k.init_rt()

        def mm(psum_ap, lhsT, rhs, start, stop, inc=False):
            return k.op("tensor", k.sP if inc else None,
                        lambda e, p=psum_ap, l=lhsT, r=rhs, s1=start, s2=stop:
                        e.matmul(p, l, r, start=s1, stop=s2, skip_group_check=True))

        def evac(dst, src, func=AF.Copy, bias=0.0, scale=1.0):
            return k.op("scalar", k.sA,
                        lambda e, d=dst, s=src, f=func, b=bias, sc=scale:
                        e.activation(d, s, f, bias=b, scale=sc))

        def ph_load_x():
            tr_done = {}
            last_pv = {}
            for tc in range(8):
                if tc >= 1:
                    k.wait("sync", k.sP, last_pv[tc - 1])
                k.dma("sync", s_xl, outh[0][:], xin[tc * 128:(tc + 1) * 128, 0:1024])
                lv1 = k.dma("sync", s_xl, outh[1][:], xin[tc * 128:(tc + 1) * 128, 1024:2048])
                k.wait("tensor", s_xl, lv1)
                for hc in range(HCN):
                    src = outh[hc // 8][:, (hc % 8) * 128:(hc % 8 + 1) * 128]
                    bank = pb[hc % 2].ap()[:, 0:128]
                    prev = tr_done.get((tc, hc - 2)) or tr_done.get((tc - 1, hc + 14))
                    k.wait("tensor", k.sA, prev)
                    pv = k.op("tensor", k.sP,
                              lambda e, b=bank, s=src: e.transpose(b, s, identf[:]))
                    last_pv[tc] = pv
                    k.wait("scalar", k.sP, pv)
                    tr_done[(tc, hc)] = evac(xres[hc][:, tc * 128:(tc + 1) * 128], bank)

        def ph_prep():
            sv = {}
            k.dma("sync", s_pc, identf[:], identf_d)
            k.dma("sync", s_pc, tps[:], tablep)
            k.dma("sync", s_pc, ohsb[:], onehot)
            k.dma("sync", s_pc, e31sb[:], e31)
            k.dma("sync", s_pc, R_CR0[:, 0:128], escb)
            pcv = k.dma("sync", s_pc, R_CR1[:, 0:128], nramp)
            for key in ("identf", "tps", "oh", "e31", "escb", "nramp"):
                sv[key] = pcv
            gwv = None
            for c in range(HCN):
                gwv = k.dma("gpsimd", s_gw, gwb[:, c * 16:(c + 1) * 16], gwt[c])
            k.op("vector", None, lambda e: e.memset(onescolb[:], 1.0))
            k.op("vector", None, lambda e: e.memset(cstC[:], -CSTAB))
            k.op("vector", None, lambda e: e.memset(cstEps[:], 1e-5))
            brm = k.op("vector", k.sV, lambda e: e.memset(biasrow[:], NEGB))
            vmem = k.op("vector", k.sV, lambda e: e.memset(onesrf[:], 1.0))
            k.wait("vector", s_pc, sv["identf"])
            k.op("vector", k.sV, lambda e: e.tensor_copy(identb[:], identf[:]))
            k.wait("tensor", s_pc, sv["tps"])
            k.wait("tensor", s_pc, sv["oh"])
            p1 = mm(pb[0].ap()[0:2, :], tps.ap()[0:32, 0:2], ohsb.ap()[0:32, 0:512],
                    True, True, inc=True)
            p2 = mm(pb[1].ap()[0:2, 0:256], tps.ap()[0:32, 0:2], ohsb.ap()[0:32, 512:768],
                    True, True, inc=True)
            k.wait("scalar", k.sP, p1)
            k.wait("scalar", k.sV, brm)
            evac(biasrow.ap()[0:2, 638:1150], pb[0].ap()[0:2, :])
            k.wait("scalar", k.sP, p2)
            a2 = evac(biasrow.ap()[0:2, 1150:1406], pb[1].ap()[0:2, 0:256])
            k.wait("vector", k.sA, a2)
            a2v = k.op("vector", k.sV, lambda e: e.tensor_scalar(
                biasrow.ap()[0:2, 638:1406], biasrow.ap()[0:2, 638:1406], -CSTAB, None, ALU.add))
            k.wait("sync", k.sV, a2v)
            k.dma("sync", s_bw, bass.AP(biasd.tensor, 0, [[1408, 2], [1, 1408]]), biasrow[:])
            k.wait("tensor", s_pc, sv["e31"])
            p3 = mm(pb[2].ap()[0:1, 0:2], e31sb[:], tps.ap()[0:32, 0:2], True, True, inc=True)
            k.wait("scalar", k.sP, p3)
            a3 = evac(R_T[:, 0:2], pb[2].ap()[0:1, 0:2])
            k.wait("tensor", k.sA, a3)
            k.wait("tensor", k.sV, vmem)
            p4 = mm(pb[3].ap()[:, 0:2], onesr(0), R_T[:, 0:2], True, True, inc=True)
            k.wait("scalar", k.sP, p4)
            a4 = evac(farb[:], pb[3].ap()[:, 0:2])
            k.wait("vector", k.sA, a4)
            k.op("vector", k.sV, lambda e: e.tensor_scalar(
                farb[:], farb[:], -CSTAB, None, ALU.add))
            k.wait("tensor", s_pc, sv["escb"])
            p5 = mm(pb[4].ap()[:, 0:128], onesr(0), R_CR0[:, 0:128], True, True, inc=True)
            k.wait("scalar", k.sP, p5)
            evac(bbrep[:], pb[4].ap()[:, 0:128])
            k.wait("tensor", s_pc, sv["nramp"])
            p6 = mm(pb[5].ap()[:, 0:128], onesr(0), R_CR1[:, 0:128], True, True, inc=True)
            k.wait("scalar", k.sP, p6)
            evac(nrrep[:], pb[5].ap()[:, 0:128])
            k.wait("vector", s_gw, gwv)

        def ph_ln():
            cast_done = {}
            for c in range(HCN):
                cast_done[c] = k.op("vector", k.sV,
                                    lambda e, c_=c: e.tensor_copy(yb[c_][:], xres[c_][:]))
            sqmm = {}
            last = None
            for c in range(HCN):
                if c >= 2:
                    k.wait("scalar", k.sP, sqmm[c - 2])
                sq_done = k.op("scalar", k.sA,
                               lambda e, c_=c: e.activation(sq2[c_ % 2][:], xres[c_][:], AF.Square))
                k.wait("tensor", k.sV, cast_done[c])
                k.wait("tensor", k.sA, sq_done)
                for t in range(2):
                    mm(pb[t].ap()[0:1, :], onescolb[:], yb[c][:, t * 512:(t + 1) * 512],
                       c == 0, c == HCN - 1)
                for t in range(2):
                    last = mm(pb[2 + t].ap()[0:1, :], onescolb[:],
                              sq2[c % 2][:, t * 512:(t + 1) * 512],
                              c == 0, c == HCN - 1, inc=(t == 1))
                sqmm[c] = last
            k.wait("vector", k.sP, last)
            def VS(fn):
                v = k.op("vector", k.sV, fn)
                k.wait("vector", k.sV, v)
                return v
            for t in range(2):
                sl = slice(t * 512, (t + 1) * 512)
                VS(lambda e, t_=t, s_=sl: e.tensor_scalar(
                    R_M[:, s_], pb[t_].ap()[0:1, :], 1.0 / H, None, ALU.mult))
                VS(lambda e, t_=t, s_=sl: e.tensor_scalar(
                    R_V[:, s_], pb[2 + t_].ap()[0:1, :], 1.0 / H, None, ALU.mult))
            VS(lambda e: e.tensor_mul(R_T, R_M, R_M))
            vrow = VS(lambda e: e.tensor_sub(R_V, R_V, R_T))
            k.wait("scalar", k.sV, vrow)
            asq = k.op("scalar", k.sA,
                       lambda e: e.activation(R_T, R_V, AF.Sqrt, bias=cstEps.ap()[0:1, 0:1]))
            k.wait("vector", k.sA, asq)
            VS(lambda e: e.reciprocal(R_A, R_T))
            VS(lambda e: e.tensor_mul(R_B, R_M, R_A))
            brow = VS(lambda e: e.tensor_scalar(R_B, R_B, -1.0, None, ALU.mult))
            k.wait("tensor", k.sV, brow)
            for r, rb, dstt in ((R_A, 0, Asb), (R_B, 0, Bsb)):
                for t in range(2):
                    pv = mm(pb[6].ap(), onesr(rb), r[:, t * 512:(t + 1) * 512],
                            True, True, inc=True)
                    k.wait("scalar", k.sP, pv)
                    ev = evac(dstt[:, t * 512:(t + 1) * 512], pb[6].ap())
                    k.wait("tensor", k.sA, ev)
            k.wait("vector", k.sA, ev)
            y_done = {}
            for c in range(HCN):
                mv = k.op("vector", k.sV, lambda e, c_=c: e.tensor_mul(cmb[0][:], xres[c_][:], Asb[:]))
                k.wait("vector", k.sV, mv)
                y_done[c] = k.op("vector", k.sV,
                                 lambda e, c_=c: e.tensor_add(yb[c_][:], cmb[0][:], Bsb[:]))
            return y_done

        def ph_router(y_done):
            lg = None
            for c in range(HCN):
                k.wait("tensor", k.sV, y_done[c])
                for t in range(2):
                    lg = mm(pb[2 + t].ap()[0:16, :], gwb[:, c * 16:(c + 1) * 16],
                            yb[c][:, t * 512:(t + 1) * 512], c == 0, c == HCN - 1,
                            inc=(c == HCN - 1 and t == 1))
            k.wait("scalar", k.sP, lg)
            evac(r16.ap()[0:16, 0:512], pb[2].ap()[0:16, :])
            ev1 = evac(r16.ap()[0:16, 512:1024], pb[3].ap()[0:16, :])
            if "gdbg" in dbg:
                k.wait("sync", k.sA, ev1)
                s1 = k.dma("sync", s_dbg, dbg["gdbg"][11 * 128:11 * 128 + 16, :], r16[:])
                s2 = k.dma("sync", s_dbg, dbg["gdbg"][12 * 128:13 * 128, 0:128], bbrep[:])
                k.wait("tensor", s_dbg, s2)
            sig = {}
            k.wait("tensor", k.sA, ev1)
            for t8 in range(8):
                bank = pb[4 + (t8 % 2)].ap()[:, 0:16]
                if t8 >= 2:
                    k.wait("tensor", k.sA, sig[t8 - 2])
                pv = k.op("tensor", k.sP, lambda e, b=bank, t_=t8: e.transpose(
                    b, r16.ap()[0:16, t_ * 128:(t_ + 1) * 128], identf.ap()[0:16, 0:16]))
                k.wait("scalar", k.sP, pv)
                sig[t8] = k.op("scalar", k.sA, lambda e, b=bank, t_=t8: e.activation(
                    gb[:, t_ * 16:(t_ + 1) * 16], b, AF.Sigmoid))
            k.wait("vector", k.sA, sig[7])
            def V(fn):
                v = k.op("vector", k.sV, fn)
                k.wait("vector", k.sV, v)
                return v
            k.snap(0, gb[:])
            V(lambda e: e.tensor_add(gc[:], gb[:], bbrep[:]))
            k.snap(1, gc[:])
            g4 = lambda t, j: t.ap()[:, j:128:4]
            dsts = [gd.ap()[:, 0:32], gd.ap()[:, 32:64], gd.ap()[:, 64:96],
                    gd.ap()[:, 96:128], ga.ap()[:, 0:32], ga.ap()[:, 32:64]]
            for i, (a, b) in enumerate([(0, 1), (2, 3), (0, 2), (0, 3), (1, 2), (1, 3)]):
                V(lambda e, a_=a, b_=b, d=dsts[i]: e.tensor_tensor(d, g4(gc, a_), g4(gc, b_), ALU.add))
            V(lambda e: e.tensor_max(gd.ap()[:, 0:32], gd.ap()[:, 0:32], gd.ap()[:, 32:64]))
            V(lambda e: e.tensor_max(gd.ap()[:, 64:96], gd.ap()[:, 64:96], gd.ap()[:, 96:128]))
            V(lambda e: e.tensor_max(ga.ap()[:, 0:32], ga.ap()[:, 0:32], ga.ap()[:, 32:64]))
            V(lambda e: e.tensor_max(gd.ap()[:, 64:96], gd.ap()[:, 64:96], ga.ap()[:, 0:32]))
            V(lambda e: e.tensor_max(gd.ap()[:, 0:32], gd.ap()[:, 0:32], gd.ap()[:, 64:96]))
            k.snap(2, gd[:])
            t4 = lambda j: gd.ap()[:, j:32:4]
            V(lambda e: e.tensor_max(gsm.ap()[:, 0:8], t4(0), t4(1)))
            V(lambda e: e.tensor_tensor(gsm.ap()[:, 8:16], t4(0), t4(1), ALU.min))
            V(lambda e: e.tensor_max(gsm.ap()[:, 16:24], t4(2), t4(3)))
            V(lambda e: e.tensor_tensor(gsm.ap()[:, 24:32], t4(2), t4(3), ALU.min))
            V(lambda e: e.tensor_tensor(ga.ap()[:, 0:8], gsm.ap()[:, 0:8], gsm.ap()[:, 16:24], ALU.min))
            V(lambda e: e.tensor_max(ga.ap()[:, 8:16], gsm.ap()[:, 8:16], gsm.ap()[:, 24:32]))
            V(lambda e: e.tensor_max(gsm.ap()[:, 0:8], ga.ap()[:, 0:8], ga.ap()[:, 8:16]))
            k.snap(3, gsm[:], width=32)
            for g in range(4):
                V(lambda e, g_=g: e.tensor_tensor(gd.ap()[:, 32 + g_:64 + g_:4][:, 0:8],
                                                  t4(g_), gsm.ap()[:, 0:8], ALU.is_ge))
            k.snap(4, gd[:])
            for j in range(4):
                V(lambda e, j_=j: e.tensor_copy(ga.ap()[:, j_:128:4], gd.ap()[:, 32:64]))
            k.snap(5, ga[:])
            V(lambda e: e.tensor_scalar(ge[:], ga[:], -1.0, 1.0, ALU.mult, ALU.add))
            V(lambda e: e.tensor_mul(gd[:], gc[:], ga[:]))
            V(lambda e: e.tensor_mul(ge[:], ge[:], nrrep[:]))
            V(lambda e: e.tensor_add(gd[:], gd[:], ge[:]))
            k.snap(6, gd[:])
            V(lambda e: e.memset(gc[:], 0.0))
            for it in range(6):
                V(lambda e: e.tensor_max(ga.ap()[:, 0:64], gd.ap()[:, 0:128:2], gd.ap()[:, 1:128:2]))
                V(lambda e: e.tensor_max(ga.ap()[:, 0:32], ga.ap()[:, 0:64:2], ga.ap()[:, 1:64:2]))
                V(lambda e: e.tensor_max(ga.ap()[:, 0:16], ga.ap()[:, 0:32:2], ga.ap()[:, 1:32:2]))
                V(lambda e: e.tensor_max(gsm.ap()[:, 0:8], ga.ap()[:, 0:16:2], ga.ap()[:, 1:16:2]))
                for t8 in range(8):
                    V(lambda e, t_=t8: e.tensor_scalar(
                        ge.ap()[:, t_ * 16:(t_ + 1) * 16], gd.ap()[:, t_ * 16:(t_ + 1) * 16],
                        gsm.ap()[:, t_:t_ + 1], None, ALU.is_equal))
                V(lambda e: e.tensor_add(gc[:], gc[:], ge[:]))
                V(lambda e: e.tensor_scalar(ge[:], ge[:], -1e9, None, ALU.mult))
                V(lambda e: e.tensor_add(gd[:], gd[:], ge[:]))
            k.snap(7, gc[:])
            V(lambda e: e.tensor_mul(ge[:], gc[:], gb[:]))
            V(lambda e: e.tensor_add(ga.ap()[:, 0:64], ge.ap()[:, 0:128:2], ge.ap()[:, 1:128:2]))
            V(lambda e: e.tensor_add(ga.ap()[:, 0:32], ga.ap()[:, 0:64:2], ga.ap()[:, 1:64:2]))
            V(lambda e: e.tensor_add(ga.ap()[:, 0:16], ga.ap()[:, 0:32:2], ga.ap()[:, 1:32:2]))
            V(lambda e: e.tensor_add(gsm.ap()[:, 8:16], ga.ap()[:, 0:16:2], ga.ap()[:, 1:16:2]))
            V(lambda e: e.tensor_scalar(gsm.ap()[:, 8:16], gsm.ap()[:, 8:16], 1e-20, None, ALU.add))
            V(lambda e: e.reciprocal(gsm.ap()[:, 16:24], gsm.ap()[:, 8:16]))
            k.snap(8, ge[:])
            k.snap(9, gsm[:], width=32)
            comb_ops = {}
            for t8 in range(8):
                comb_ops[t8] = k.op("vector", k.sV, lambda e, t_=t8: e.tensor_scalar(
                    gc.ap()[:, t_ * 16:(t_ + 1) * 16], ge.ap()[:, t_ * 16:(t_ + 1) * 16],
                    gsm.ap()[:, 16 + t_:17 + t_], 1.25, ALU.mult, ALU.mult))
            k.snap(10, gc[:])
            tr_ev = {}
            for t8 in range(8):
                k.wait("tensor", k.sV, comb_ops[t8])
                bank = pb[4 + (t8 % 2)].ap()[0:16, 0:128]
                if t8 >= 2:
                    k.wait("tensor", k.sA, tr_ev[t8 - 2])
                pv = k.op("tensor", k.sP, lambda e, b=bank, t_=t8: e.transpose(
                    b, gc.ap()[:, t_ * 16:(t_ + 1) * 16], identf[:]))
                k.wait("scalar", k.sP, pv)
                tr_ev[t8] = evac(r16.ap()[0:16, t8 * 128:(t8 + 1) * 128], bank)
            k.wait("sync", k.sA, tr_ev[7])
            return k.dma("sync", s_cmw, bass.AP(combd.tensor, 0, [[TOK, 16], [1, TOK]]), r16[:])

        SKIP = set(os.environ.get("KSKIP", "").split(","))
        def ph_experts(comb_stored, y_done):
            k.wait("sync", s_cmw, comb_stored)
            k.wait("vector", k.sV, y_done[HCN - 1])
            wlv = {}
            blk_free = {}
            crow_free = {}
            cmb_free = {}
            sil_prev = {0: None, 1: None}
            h1_prev = {0: None, 1: None}
            iv_prev = {0: None, 1: None}
            iv_last = {0: None, 1: None}
            down_d = {}
            add_d = {}
            bcast_ev = {}
            nb = 0
            for e_ in range(n_exp):
                r = R_CR0 if e_ % 2 == 0 else R_CR1
                rbase = 0
                if e_ >= 2:
                    k.wait("sync", k.sP, crow_free[e_ - 2])
                crow_ld = k.dma("sync", s_cr[e_ % 2], r[:, 0:1024], combd[e_:e_ + 1, :])
                k.wait("tensor", s_cr[e_ % 2], crow_ld)
                if e_ >= 1:
                    k.wait("tensor", k.sA, bcast_ev[e_ - 1])
                ev = None
                for t in range(2):
                    if t == 1:
                        k.wait("tensor", k.sA, ev)
                    pv = mm(pb[6].ap(), onesr(rbase), r[:, t * 512:(t + 1) * 512],
                            True, True, inc=True)
                    k.wait("scalar", k.sP, pv)
                    if e_ >= 2:
                        k.wait("scalar", k.sV, cmb_free[e_ - 2])
                    ev = evac(cmb[e_ % 2][:, t * 512:(t + 1) * 512], pb[6].ap())
                crow_free[e_] = pv
                bcast_ev[e_] = ev
                cmb_ready = ev
                for ic in range(ICN):
                    for warr in (wg, wu):
                        if nb >= 4:
                            k.wait("gpsimd", k.sP, blk_free.get(nb - 4))
                        wlv[nb] = (k.dma("gpsimd", s_wb[nb % 4], wblk[nb % 4][:], warr[e_, ic]), s_wb[nb % 4])
                        nb += 1
                    gslot, uslot = (nb - 2) % 4, (nb - 1) % 4
                    if "gate" in SKIP:
                        blk_free[nb - 2] = None
                        blk_free[nb - 1] = None
                        continue
                    for t in range(2):
                        k.wait("tensor", wlv[nb - 2][1], wlv[nb - 2][0])
                        k.wait("tensor", k.sA, sil_prev[t])
                        for c in range(HCN):
                            gmm = mm(pb[t].ap(), wblk[gslot][:, c * 128:(c + 1) * 128],
                                     yb[c][:, t * 512:(t + 1) * 512], c == 0, c == HCN - 1,
                                     inc=(c == HCN - 1))
                        if "silu" in SKIP:
                            sil_prev[t] = gmm and k.sA.n
                            iv_prev[t] = k.sV.n
                            iv_last[t] = iv_prev[t]
                            if t == 1:
                                blk_free[nb - 2] = gmm
                                blk_free[nb - 1] = gmm
                            if ic == ICN - 1 and t == 1:
                                cmb_free[e_] = k.sV.n
                            continue
                        k.wait("scalar", k.sP, gmm)
                        k.wait("scalar", k.sV, iv_prev[t])
                        sil_prev[t] = k.op("scalar", k.sA, lambda e, t_=t: e.activation(
                            gsb[t_][:], pb[t_].ap(), AF.Silu))
                        if "up" in SKIP:
                            if t == 1:
                                blk_free[nb - 2] = gmm
                                blk_free[nb - 1] = gmm
                            iv_prev[t] = sil_prev[t] and k.sV.n
                            iv_last[t] = iv_prev[t]
                            if ic == ICN - 1 and t == 1:
                                cmb_free[e_] = k.sV.n
                            continue
                        k.wait("tensor", wlv[nb - 1][1], wlv[nb - 1][0])
                        k.wait("tensor", k.sV, h1_prev[t])
                        for c in range(HCN):
                            umm = mm(pb[2 + t].ap(), wblk[uslot][:, c * 128:(c + 1) * 128],
                                     yb[c][:, t * 512:(t + 1) * 512], c == 0, c == HCN - 1,
                                     inc=(c == HCN - 1))
                        if t == 1:
                            blk_free[nb - 2] = umm
                            blk_free[nb - 1] = umm
                        k.wait("vector", k.sP, umm)
                        k.wait("vector", k.sA, sil_prev[t])
                        k.wait("vector", k.sA, cmb_ready)
                        if e_ >= 1 and ic == 0:
                            k.wait("vector", k.sP, down_d[(e_ - 1, t)])
                        if "h1inplace" in SKIP:
                            h1_prev[t] = k.op("vector", k.sV, lambda e, t_=t: e.tensor_copy(
                                cmb[0][:, 0:512], pb[2 + t_].ap()))
                            k.wait("vector", k.sV, h1_prev[t])
                            k.op("vector", None, lambda e, t_=t: e.tensor_mul(
                                gsb[t_][:], cmb[0][:, 0:512], gsb[t_][:]))
                        else:
                            h1_prev[t] = k.op("vector", k.sV, lambda e, t_=t: e.tensor_mul(
                                gsb[t_][:], pb[2 + t_].ap(), gsb[t_][:]))
                        k.wait("vector", k.sV, h1_prev[t])
                        iv_prev[t] = k.op("vector", k.sV, lambda e, t_=t, ic_=ic, e2=e_:
                                          e.tensor_mul(ichunk(ic_, t_), gsb[t_][:],
                                                       cmb[e2 % 2][:, t_ * 512:(t_ + 1) * 512]))
                        iv_last[t] = iv_prev[t]
                        if ic == ICN - 1 and t == 1:
                            cmb_free[e_] = iv_prev[t]
                if "down" in SKIP:
                    for t in range(2):
                        down_d[(e_, t)] = iv_last[t] and k.sP.n
                    continue
                for hc in range(HCN):
                    if nb >= 4:
                        k.wait("gpsimd", k.sP, blk_free[nb - 4])
                    wlv[nb] = (k.dma("gpsimd", s_wb[nb % 4], wblk[nb % 4][:, 0:1408], wd[e_, hc]), s_wb[nb % 4])
                    nb += 1
                    slot = (nb - 1) % 4
                    k.wait("tensor", wlv[nb - 1][1], wlv[nb - 1][0])
                    for t in range(2):
                        bank = pb[4 + t]
                        k.wait("tensor", k.sV, add_d.get((hc - 1, t)))
                        if hc == 0:
                            k.wait("tensor", k.sV, iv_last[t])
                            if e_ >= 1:
                                k.wait("tensor", k.sV, add_d.get((HCN - 1, t)))
                        for ic in range(ICN):
                            pv = mm(bank.ap(), wblk[slot][:, ic * 128:(ic + 1) * 128],
                                    ichunk(ic, t), ic == 0, ic == ICN - 1, inc=(ic == ICN - 1))
                        down_d[(e_, t)] = pv
                        if t == 1:
                            blk_free[nb - 1] = pv
                        k.wait("vector", k.sP, pv)
                        add_d[(hc, t)] = k.op("vector", k.sV,
                            lambda e, hc_=hc, t_=t, b=bank: e.tensor_add(
                                xres[hc_][:, t_ * 512:(t_ + 1) * 512],
                                xres[hc_][:, t_ * 512:(t_ + 1) * 512], b.ap()))

        def ph_shared(y_done):
            k.wait("vector", k.sV, y_done[HCN - 1])
            wlv = {}
            blk_free = {}
            sil_d = {}
            h1_d = {}
            iv_d = {}
            add_d = {}
            down_half = {}
            nb = 0
            for half in range(2):
                for sc in range(11):
                    for warr in (wsg, wsu):
                        if nb >= 4:
                            k.wait("gpsimd", k.sP, blk_free[nb - 4])
                        wlv[nb] = (k.dma("gpsimd", s_wb[nb % 4], wblk[nb % 4][:], warr[half * 11 + sc]), s_wb[nb % 4])
                        nb += 1
                    gslot, uslot = (nb - 2) % 4, (nb - 1) % 4
                    for t in range(2):
                        k.wait("tensor", wlv[nb - 2][1], wlv[nb - 2][0])
                        k.wait("tensor", k.sA, sil_d.get((half, sc - 1, t)))
                        if sc == 0 and half == 1:
                            k.wait("tensor", k.sA, sil_d.get((0, 10, t)))
                        for c in range(HCN):
                            gmm = mm(pb[t].ap(), wblk[gslot][:, c * 128:(c + 1) * 128],
                                     yb[c][:, t * 512:(t + 1) * 512], c == 0, c == HCN - 1,
                                     inc=(c == HCN - 1))
                        k.wait("scalar", k.sP, gmm)
                        k.wait("scalar", k.sV, iv_d.get((half, sc - 1, t)))
                        if sc == 0 and half == 1:
                            k.wait("scalar", k.sV, iv_d.get((0, 10, t)))
                        sil_d[(half, sc, t)] = k.op("scalar", k.sA, lambda e, t_=t:
                                                    e.activation(gsb[t_][:], pb[t_].ap(), AF.Silu))
                        k.wait("tensor", wlv[nb - 1][1], wlv[nb - 1][0])
                        k.wait("tensor", k.sV, h1_d.get((half, sc - 1, t)))
                        if sc == 0 and half == 1:
                            k.wait("tensor", k.sV, h1_d.get((0, 10, t)))
                        for c in range(HCN):
                            umm = mm(pb[2 + t].ap(), wblk[uslot][:, c * 128:(c + 1) * 128],
                                     yb[c][:, t * 512:(t + 1) * 512], c == 0, c == HCN - 1,
                                     inc=(c == HCN - 1))
                        if t == 1:
                            blk_free[nb - 2] = umm
                            blk_free[nb - 1] = umm
                        k.wait("vector", k.sP, umm)
                        k.wait("vector", k.sA, sil_d[(half, sc, t)])
                        if half == 1 and sc == 0:
                            k.wait("vector", k.sP, down_half[(0, t)])
                        h1_d[(half, sc, t)] = k.op("vector", k.sV, lambda e, t_=t:
                                                   e.tensor_mul(gsb[t_][:], pb[2 + t_].ap(), gsb[t_][:]))
                        k.wait("vector", k.sV, h1_d[(half, sc, t)])
                        iv_d[(half, sc, t)] = k.op("vector", k.sV, lambda e, t_=t, sc_=sc:
                                                   e.tensor_copy(ichunk(sc_, t_), gsb[t_][:]))
                for hc in range(HCN):
                    if nb >= 4:
                        k.wait("gpsimd", k.sP, blk_free[nb - 4])
                    wlv[nb] = (k.dma("gpsimd", s_wb[nb % 4], wblk[nb % 4][:, 0:1408],
                                     bass.AP(wsd.tensor, hc * 128 * 2816 + half * 1408,
                                             [[2816, 128], [1, 1408]])), s_wb[nb % 4])
                    nb += 1
                    slot = (nb - 1) % 4
                    k.wait("tensor", wlv[nb - 1][1], wlv[nb - 1][0])
                    for t in range(2):
                        bank = pb[4 + t]
                        k.wait("tensor", k.sV, add_d.get((half, hc - 1, t)))
                        if hc == 0:
                            k.wait("tensor", k.sV, iv_d[(half, 10, t)])
                            if half == 1:
                                k.wait("tensor", k.sV, add_d.get((0, HCN - 1, t)))
                        for sc in range(11):
                            pv = mm(bank.ap(), wblk[slot][:, sc * 128:(sc + 1) * 128],
                                    ichunk(sc, t), sc == 0, sc == 10, inc=(sc == 10))
                        down_half[(half, t)] = pv
                        if t == 1:
                            blk_free[nb - 1] = pv
                        k.wait("vector", k.sP, pv)
                        add_d[(half, hc, t)] = k.op("vector", k.sV,
                            lambda e, hc_=hc, t_=t, b=bank: e.scalar_tensor_tensor(
                                xres[hc_][:, t_ * 512:(t_ + 1) * 512], b.ap(), 0.5,
                                xres[hc_][:, t_ * 512:(t_ + 1) * 512], ALU.mult, ALU.add))

        def ph_qkv(y_done):
            for dst, src in ((cqrep, cqr), (sqrep, sqr), (ckrep, ckr), (skrep, skr)):
                rp = k.dma("sync", s_rt, dst, src)
            k.wait("vector", s_rt, rp)
            k.wait("tensor", k.sV, y_done[HCN - 1])
            wlv = {}
            blk_free = {}
            mmv = {}
            bank_free = {}
            st_qk = {}
            st_v = {}
            vtr = {}
            vtr_ev = {}
            vst = {}
            rv_prev = None
            for o in range(48):
                if o >= 4:
                    k.wait("gpsimd", k.sP, blk_free[o - 4])
                wlv[o] = k.dma("gpsimd", s_wb[o % 4], wblk[o % 4][:], wqkv[o])
                k.wait("tensor", s_wb[o % 4], wlv[o])
                for t in range(2):
                    bank = pb[(2 * o + t) % 4]
                    k.wait("tensor", k.sV, bank_free.get((o - 2, t)))
                    for c in range(HCN):
                        pv = mm(bank.ap(), wblk[o % 4][:, c * 128:(c + 1) * 128],
                                yb[c][:, t * 512:(t + 1) * 512], c == 0, c == HCN - 1,
                                inc=(c == HCN - 1))
                    mmv[(o, t)] = pv
                blk_free[o] = mmv[(o, 1)]
                h = o % 16
                if o < 32:
                    isq = o < 16
                    crep, srep = (cqrep, sqrep) if isq else (ckrep, skrep)
                    dstt = qsb if isq else ksb
                    sect = (0 if isq else KOFF) + h * (HD * TOK)
                    if o >= 1 and st_qk.get(o - 1):
                        k.wait("vector", st_qk[o - 1][1], st_qk[o - 1][0])
                    rv = None
                    for t in range(2):
                        bank = pb[(2 * o + t) % 4]
                        k.wait("vector", k.sP, mmv[(o, t)])
                        k.op("vector", k.sV, lambda e, b=bank, c_=crep, t_=t: e.tensor_mul(
                            re1, b.ap(), c_[:, t_ * 512:(t_ + 1) * 512]))
                        e2v = k.op("vector", k.sV, lambda e, b=bank, s_=srep, t_=t: e.tensor_mul(
                            re2, b.ap(), s_[:, t_ * 512:(t_ + 1) * 512]))
                        bank_free[(o, t)] = e2v
                        k.wait("sync", k.sV, e2v)
                        if rv_prev:
                            k.wait("sync", k.sV, rv_prev)
                        k.dma("sync", s_sw, re2s[0:64, :], re2[64:128, :])
                        sw1 = k.dma("sync", s_sw, re2s[64:128, :], re2[0:64, :])
                        k.wait("vector", s_sw, sw1)
                        k.wait("vector", k.sV, e2v)
                        if isq:
                            od0 = dstt.ap()[0:64, t * 512:(t + 1) * 512]
                            od1 = dstt.ap()[64:128, t * 512:(t + 1) * 512]
                        else:
                            rs = slice(t * 512 + 511, t * 512 - 1 if t > 0 else None, -1)
                            od0 = dstt.ap()[0:64, rs]
                            od1 = dstt.ap()[64:128, rs]
                        k.op("vector", None, lambda e, d=od0: e.tensor_sub(d, re1[0:64, :], re2s[0:64, :]))
                        rv = k.op("vector", k.sV, lambda e, d=od1: e.tensor_add(
                            d, re1[64:128, :], re2s[64:128, :]))
                        rv_prev = rv
                    k.wait("sync", k.sV, rv)
                    ssem = s_qs if isq else s_ks
                    st_qk[o] = (k.dma("sync", ssem, bass.AP(kvl.tensor, sect, [[TOK, 128], [1, TOK]]),
                                      dstt[:]), ssem)
                else:
                    ev = None
                    for t in range(2):
                        bank = pb[(2 * o + t) % 4]
                        k.wait("vector", k.sP, mmv[(o, t)])
                        if o >= 33 and t == 0:
                            k.wait("vector", k.sP, vtr.get((o - 1, 7)))
                        rs = slice(t * 512 + 511, t * 512 - 1 if t > 0 else None, -1)
                        ev = k.op("vector", k.sV, lambda e, b=bank, r_=rs: e.tensor_copy(
                            vsb.ap()[:, r_], b.ap()))
                        bank_free[(o, t)] = ev
                    k.wait("tensor", k.sV, ev)
                    sect = VOFF + h * (TOK * HD)
                    for tc in range(8):
                        k.wait("tensor", k.sA, vtr_ev.get((o, tc - 1)) if o >= 32 else None)
                        if tc == 0 and o >= 33:
                            k.wait("tensor", k.sA, vtr_ev.get((o - 1, 7)))
                        pv = k.op("tensor", k.sP, lambda e, tc_=tc: e.transpose(
                            pt0.ap(), vsb[:, tc_ * 128:(tc_ + 1) * 128], identb[:]))
                        vtr[(o, tc)] = pv
                        k.wait("scalar", k.sP, pv)
                        pvst = vst.get((o, tc - 2)) or vst.get((o - 1, tc + 6))
                        if pvst:
                            k.wait("scalar", s_vst[tc % 2], pvst)
                        ve = evac(vt[tc % 2][:], pt0.ap())
                        vtr_ev[(o, tc)] = ve
                        k.wait("sync", k.sA, ve)
                        vst[(o, tc)] = k.dma("sync", s_vst[tc % 2], bass.AP(
                            kvl.tensor, sect + tc * 128 * 128, [[128, 128], [1, 128]]),
                            vt[tc % 2][:])
                    st_v[o] = vst[(o, 7)]

        def ph_gather_kv():
            k.barrier()
            k.op("gpsimd", k.sC, lambda e: e.collective_compute(
                "AllGather", ALU.bypass, [list(range(NC))],
                [bass.AP(kvl.tensor, 0, [[1024, SEC // 1024], [1, 1024]]).opt()],
                [bass.AP(kvgt, 0, [[1024, NC * SEC // 1024], [1, 1024]]).opt()]))
            k.barrier()
            for lh in range(LH):
                k.dma_dyn("gpsimd", s_kvh[lh],
                          bass.AP(kvh.tensor, lh * 3145728, [[1, 3145728]]),
                          kvgt, lh * HD * TOK, "kv",
                          ((SEC, 8), (QSECT, 3), (1024, 128), (1, 1024)))

        def ph_gather_ctx():
            k.barrier()
            k.op("gpsimd", k.sC, lambda e: e.collective_compute(
                "AllGather", ALU.bypass, [list(range(NC))],
                [bass.AP(ctxl.tensor, 0, [[1024, CTXSECT // 1024], [1, 1024]]).opt()],
                [bass.AP(ctxgt, 0, [[1024, NC * CTXSECT // 1024], [1, 1024]]).opt()]))
            k.barrier()
            k.dma_dyn("gpsimd", s_cxh, bass.AP(ctxh.tensor, 0, [[1024, 2048], [1, 512]]),
                      ctxgt, 0, "clo", ((S, 2048), (1, 512)))
            k.dma_dyn("gpsimd", s_cxh, bass.AP(ctxh.tensor, 512, [[1024, 2048], [1, 512]]),
                      ctxgt, 0, "chi", ((S, 2048), (1, 512)))
            k.wait("sync", s_cxh, 32)
            k.barrier()

        def ph_flash():
            GC = 0
            KB_ = 0
            qk_d = {}
            pvden = {}
            expd = {}
            muld = {}
            stored = {}
            recipd = {}
            bcastd = {}
            bevacd = {}
            pair_end = {}
            blk_qk = {}
            kblk_ld = {}
            for lh in range(LH):
                tb = k.dma("sync", s_tb[lh], cmb[lh][:],
                           bass.AP(biasd.tensor, lh * 1408 + 127, [[1, 128], [1, 1024]]))
                k.wait("vector", s_tb[lh], tb)
                k.wait("sync", s_kvh[lh], 16)
                k.wait("gpsimd", s_kvh[lh], 16)
                for qt in range(QTN):
                    p = lh * QTN + qt
                    nm = 4 * qt + 4
                    rq = qt if qt <= 7 else 15 - qt
                    halfq = 0 if qt <= 7 else 1
                    if p >= 2:
                        k.wait("sync", k.sP, pair_end[p - 2])
                    qtl_ld = k.dma("sync", s_qtl[p % 2], qtl[p % 2],
                                   bass.AP(kvh.tensor,
                                           lh * 3145728 + rq * 3 * 131072 + halfq * 512,
                                           [[1024, 128], [1, 512]]))
                    den_bank = pb[4 + (p % 2)]
                    ctx_bank = pb[2 + (p % 2)]
                    curKB = None
                    for m in range(nm):
                        J0 = 7680 - 512 * qt + 128 * m
                        b = 15 - J0 // 512
                        r = b if b <= 7 else 15 - b
                        half = 0 if b <= 7 else 1
                        s0 = J0 % 512
                        if m % 4 == 0:
                            if KB_ >= 2:
                                k.wait("sync", k.sP, blk_qk[KB_ - 2])
                            kblk_ld[KB_] = k.dma(
                                "sync", s_kb[KB_ % 2], kblk[KB_ % 2],
                                bass.AP(kvh.tensor,
                                        lh * 3145728 + (r * 3 + 1) * 131072 + half * 512,
                                        [[1024, 128], [1, 512]]))
                            curKB = KB_
                            KB_ += 1
                        if GC >= 8:
                            k.wait("gpsimd", k.sP, pvden[GC - 8])
                        vld = k.dma("gpsimd", s_v[GC % 8], vch[GC % 8],
                                    bass.AP(kvh.tensor,
                                            lh * 3145728 + (r * 3 + 2) * 131072
                                            + (half * 512 + s0) * 128,
                                            [[128, 128], [1, 128]]))
                        if m == 0:
                            k.wait("tensor", s_qtl[p % 2], qtl_ld)
                            k.wait("tensor", k.sV, muld.get(p - 2))
                            k.wait("tensor", k.sV, recipd.get(p - 2))
                        if m % 4 == 0:
                            k.wait("tensor", s_kb[curKB % 2], kblk_ld[curKB])
                        k.wait("tensor", k.sA, expd.get(GC - 2))
                        qk_d[GC] = mm(pb[m % 2].ap(),
                                      kblk[curKB % 2][:, (m % 4) * 128:(m % 4 + 1) * 128],
                                      qtl[p % 2], True, True, inc=True)
                        if m % 4 == 3 or m == nm - 1:
                            blk_qk[curKB] = qk_d[GC]
                        if m <= 4:
                            k.wait("vector", k.sP, qk_d[GC])
                            k.wait("vector", k.sA, expd.get(GC - 2))
                            av = k.op("vector", k.sV, lambda e, m_=m, lh_=lh, b=pb[m % 2]:
                                      e.tensor_add(tmpf[m_ % 2], b.ap(),
                                                   cmb[lh_][:, m_ * 128:m_ * 128 + 512]))
                            k.wait("scalar", k.sV, av)
                            k.wait("scalar", k.sP, pvden.get(GC - 2))
                            expd[GC] = k.op("scalar", k.sA, lambda e, m_=m: e.activation(
                                psb[m_ % 2], tmpf[m_ % 2], AF.Exp))
                        else:
                            k.wait("scalar", k.sP, qk_d[GC])
                            k.wait("scalar", k.sP, pvden.get(GC - 2))
                            expd[GC] = k.op("scalar", k.sA, lambda e, m_=m, lh_=lh: e.activation(
                                psb[m_ % 2], pb[m_ % 2].ap(), AF.Exp,
                                bias=farb.ap()[:, lh_:lh_ + 1]))
                        k.wait("tensor", k.sA, expd[GC])
                        k.wait("tensor", s_v[GC % 8], vld)
                        mm(ctx_bank.ap(), vch[GC % 8], psb[m % 2], m == 0, m == nm - 1)
                        pvden[GC] = mm(den_bank.ap()[0:1, :], onescolb[:], psb[m % 2],
                                       m == 0, m == nm - 1, inc=True)
                        GC += 1
                    denr = R_DR0 if p % 2 == 0 else R_DR1
                    drb = 0
                    k.wait("vector", k.sP, pvden[GC - 1])
                    k.wait("vector", k.sP, bcastd.get(p - 2))
                    recipd[p] = k.op("vector", k.sV, lambda e, d=denr, b=den_bank:
                                     e.reciprocal(d[:, 0:512], b.ap()[0:1, :]))
                    k.wait("tensor", k.sV, recipd[p])
                    k.wait("tensor", k.sA, bevacd.get(p - 1))
                    bcastd[p] = mm(pb[6].ap(), onesr(drb), denr[:, 0:512], True, True, inc=True)
                    k.wait("scalar", k.sP, bcastd[p])
                    k.wait("scalar", k.sV, muld.get(p - 2))
                    bevacd[p] = evac(bcsb[p % 2], pb[6].ap())
                    k.wait("vector", k.sA, bevacd[p])
                    if stored.get(p - 2):
                        k.wait("vector", s_cst[p % 2], stored[p - 2])
                    muld[p] = k.op("vector", k.sV, lambda e, p_=p, cb=ctx_bank: e.tensor_mul(
                        ctxsb[p_ % 2][:], cb.ap(), bcsb[p_ % 2]))
                    k.wait("sync", k.sV, muld[p])
                    stored[p] = k.dma("sync", s_cst[p % 2], bass.AP(
                        ctxl.tensor, lh * HD * S + 512 * qt, [[S, 128], [1, 512]]),
                        ctxsb[p % 2][:])
                    pair_end[p] = bcastd[p]

        def ph_dense():
            wlv = {}
            mm_done = {}
            add_d = {}
            ld = {}
            for i in range(HCN + 1):
                if i < HCN:
                    if i >= 4:
                        k.wait("gpsimd", k.sP, mm_done[i - 4])
                    wlv[i] = k.dma("gpsimd", s_wb[i % 4], wblk[i % 4][:], wdense[i])
                    ld[i] = k.dma("sync", s_dld, yb[i][:],
                                  bass.AP(ctxh.tensor, i * 131072, [[1024, 128], [1, 1024]]))
                if i >= 1:
                    hc2 = i - 1
                    k.wait("tensor", s_wb[hc2 % 4], wlv[hc2])
                    if hc2 == 0:
                        k.wait("tensor", s_dld, 16 * 16)
                    for t in range(2):
                        bank = pb[(2 * hc2 + t) % 4]
                        k.wait("tensor", k.sV, add_d.get((hc2 - 2, t)))
                        for c in range(HCN):
                            pv = mm(bank.ap(), wblk[hc2 % 4][:, c * 128:(c + 1) * 128],
                                    yb[c][:, t * 512:(t + 1) * 512], c == 0, c == HCN - 1,
                                    inc=(c == HCN - 1))
                        mm_done[hc2] = pv
                        k.wait("vector", k.sP, pv)
                        add_d[(hc2, t)] = k.op("vector", k.sV,
                            lambda e, h_=hc2, t_=t, b=bank: e.tensor_add(
                                xres[h_][:, t_ * 512:(t_ + 1) * 512],
                                xres[h_][:, t_ * 512:(t_ + 1) * 512], b.ap()))

        def ph_out():
            tr_done = {}
            st = {}
            for tc in range(8):
                for hc in range(HCN):
                    src = xres[hc].ap()[:, tc * 128:(tc + 1) * 128]
                    bank = pb[hc % 2].ap()[:, 0:128]
                    prev = tr_done.get((tc, hc - 2)) or tr_done.get((tc - 1, hc + 14))
                    k.wait("tensor", k.sA, prev)
                    if hc == 0 and tc >= 1:
                        k.wait("tensor", s_out[(tc - 1) % 2], st[(tc - 1, 1)])
                    pv = k.op("tensor", k.sP,
                              lambda e, b=bank, s=src: e.transpose(b, s, identf[:]))
                    k.wait("scalar", k.sP, pv)
                    tr_done[(tc, hc)] = evac(
                        outh[hc // 8][:, (hc % 8) * 128:(hc % 8 + 1) * 128], bank)
                k.wait("sync", k.sA, tr_done[(tc, HCN - 1)])
                st[(tc, 0)] = k.dma("sync", s_out[tc % 2], out[tc * 128:(tc + 1) * 128, 0:1024], outh[0][:])
                st[(tc, 1)] = k.dma("sync", s_out[tc % 2], out[tc * 128:(tc + 1) * 128, 1024:2048], outh[1][:])
            k.wait("sync", s_out[0], s_out[0].n)
            k.wait("sync", s_out[1], s_out[1].n)

        def dump_fm(name, tiles):
            if name not in dbg:
                return
            k.barrier()
            for hc in range(HCN):
                vv = k.op("vector", k.sV, lambda e, h_=hc: e.tensor_copy(cmb[0][:], tiles[h_][:]))
                k.wait("sync", k.sV, vv)
                sv = k.dma("sync", s_dbg, dbg[name][hc * 128:(hc + 1) * 128, :], cmb[0][:])
                k.wait("vector", s_dbg, sv)
            k.barrier()

        def dump_dram(name, tensor, nelem):
            if name not in dbg:
                return
            k.barrier()
            n = min(nelem, H * TOK)
            dv = k.dma("sync", s_dbg, bass.AP(dbg[name].tensor, 0, [[1, n]]), bass.AP(tensor, 0, [[1, n]]))
            k.wait("sync", s_dbg, dv)
            k.barrier()

        def snap(idx, tile, width=128):
            if "gdbg" not in dbg:
                return
            k.wait("sync", k.sV, k.sV.n)
            k.wait("sync", k.sA, k.sA.n)
            k.wait("sync", k.sP, k.sP.n)
            sv = k.dma("sync", s_dbg, dbg["gdbg"][idx * 128:idx * 128 + 128, 0:width], tile)
            k.wait("vector", s_dbg, sv)
            k.wait("scalar", s_dbg, sv)
            k.wait("tensor", s_dbg, sv)

        k.snap = snap

        order = ["prep", "load", "ln1", "router1", "experts1", "shared1",
                 "ln2", "qkv", "gkv", "flash", "gctx", "dense",
                 "ln3", "router2", "experts2", "shared2", "out"]
        stop = order.index(upto)

        def phase(nm, fn):
            if order.index(nm) <= stop:
                res = fn()
                k.barrier()
                return res
            return None

        phase("prep", ph_prep)
        phase("load", ph_load_x)
        if "xres0" in dbg:
            k.barrier()
            dump_fm("xres0", xres)
        y1 = phase("ln1", ph_ln)
        if y1 is not None:
            dump_fm("ln1", yb)
        if y1 is not None and "lndbg" in dbg:
            k.barrier()
            d = dbg["lndbg"]
            k.dma("sync", s_dbg, d[0:128, :], Asb[:])
            k.dma("sync", s_dbg, d[128:256, :], Bsb[:])
            k.dma("sync", s_dbg, d[256:257, :], R_M)
            k.dma("sync", s_dbg, d[257:258, :], R_V)
            k.dma("sync", s_dbg, d[258:259, :], R_A)
            k.dma("sync", s_dbg, d[259:260, :], R_B)
            sv = k.dma("sync", s_dbg, d[260:261, :], R_T)
            k.wait("sync", s_dbg, sv)
            k.barrier()
        cs1 = phase("router1", lambda: ph_router(y1))
        if cs1 is not None:
            dump_dram("comb1", combd.tensor, E * TOK)
        phase("experts1", lambda: ph_experts(cs1, y1))
        phase("shared1", lambda: ph_shared(y1))
        dump_fm("x1", xres)
        y2 = phase("ln2", ph_ln)
        if y2 is not None:
            dump_fm("ln2", yb)
        phase("qkv", lambda: ph_qkv(y2))
        if order.index("qkv") <= stop:
            dump_dram("kvl", kvl.tensor, SEC)
        phase("gkv", ph_gather_kv)
        phase("flash", ph_flash)
        if order.index("flash") <= stop:
            dump_dram("ctxl", ctxl.tensor, CTXSECT)
        phase("gctx", ph_gather_ctx)
        phase("dense", ph_dense)
        dump_fm("x2", xres)
        y3 = phase("ln3", ph_ln)
        cs2 = phase("router2", lambda: ph_router(y3))
        phase("experts2", lambda: ph_experts(cs2, y3))
        phase("shared2", lambda: ph_shared(y3))
        dump_fm("x3", xres)
        phase("out", ph_out)

        block = es.enter_context(nc.Block())
        k.run(block)
    return nc



def _retile(inputs):
    f = np.float32
    gp = np.asarray(inputs["L0_ffn_gate_proj"], f)
    up = np.asarray(inputs["L0_ffn_up_proj"], f)
    dp = np.asarray(inputs["L0_ffn_down_proj"], f)
    sg = np.asarray(inputs["L0_ffn_shared_gate_w"], f)
    su = np.asarray(inputs["L0_ffn_shared_up_w"], f)
    sd = np.asarray(inputs["L0_ffn_shared_down_w"], f)
    qkv = np.asarray(inputs["L0_attn_qkv_w"], f)
    dw = np.asarray(inputs["L0_attn_dense_w"], f)
    gwv = np.asarray(inputs["L0_ffn_gate_weight"], f)
    d = {}
    d["wg"] = np.ascontiguousarray(gp.reshape(E, ICN, 128, HCN, 128).transpose(0, 1, 4, 3, 2).reshape(E, ICN, 128, 2048))
    d["wu"] = np.ascontiguousarray(up.reshape(E, ICN, 128, HCN, 128).transpose(0, 1, 4, 3, 2).reshape(E, ICN, 128, 2048))
    d["wd"] = np.ascontiguousarray(dp.reshape(E, HCN, 128, ICN, 128).transpose(0, 1, 4, 3, 2).reshape(E, HCN, 128, 1408))
    d["wsg"] = np.ascontiguousarray(sg.reshape(SCN, 128, HCN, 128).transpose(0, 3, 2, 1).reshape(SCN, 128, 2048))
    d["wsu"] = np.ascontiguousarray(su.reshape(SCN, 128, HCN, 128).transpose(0, 3, 2, 1).reshape(SCN, 128, 2048))
    d["wsd"] = np.ascontiguousarray(sd.reshape(HCN, 128, SCN, 128).transpose(0, 3, 2, 1).reshape(HCN, 128, 2816))
    d["wqkv"] = np.ascontiguousarray(qkv.reshape(HCN, 128, 48, 128).transpose(2, 1, 0, 3).reshape(48, 128, 2048))
    d["wdense"] = np.ascontiguousarray(dw.reshape(HCN, 128, HCN, 128).transpose(2, 1, 0, 3).reshape(HCN, 128, 2048))
    d["gwt"] = np.ascontiguousarray(gwv.T.reshape(HCN, 128, 16))
    return d


def _bucket_of_delta(n):
    n = np.asarray(n)
    nf = np.maximum(n, 1).astype(np.float64)
    large = (np.log(nf / 16) / np.log(8.0) * 16).astype(np.int64) + 16
    large = np.minimum(large, 31)
    return np.where(n < 16, n, large)


def _consts():
    oh = np.zeros((32, 768), np.float32)
    b = _bucket_of_delta(np.arange(768))
    oh[b, np.arange(768)] = 1.0
    e31v = np.zeros((32, 1), np.float32)
    e31v[31, 0] = 1.0
    identf = np.eye(128, dtype=np.float32)
    nrampv = np.ascontiguousarray(
        np.tile(-(np.arange(16, dtype=np.float32) + 1.0) * 1e-30, 8)[None, :])
    return oh, e31v, identf, nrampv


def _rope_tables(tok_idx):
    inv = 1.0 / 10000.0 ** (np.arange(0, HD, 2, dtype=np.float32) / HD)
    ang = tok_idx[:, None].astype(np.float32) * inv[None, :]
    cos = np.cos(ang).astype(np.float32).T
    sin = np.sin(ang).astype(np.float32).T
    s = np.float32(1.0 / np.sqrt(HD))
    rep = lambda a: np.ascontiguousarray(np.concatenate([a, a], axis=0))
    return rep(cos * s), rep(sin * s), rep(cos), rep(sin)


def prep_in_maps(inputs):
    x = np.ascontiguousarray(np.asarray(inputs["L0_attn_x"], np.float32).reshape(S, H))
    w = _retile(inputs)
    oh, e31v, identf, nrampv = _consts()
    table = np.ascontiguousarray(np.asarray(inputs["L0_attn_rel_attn_table"], np.float32))
    escb_rep = np.ascontiguousarray(
        np.tile(np.asarray(inputs["L0_ffn_e_score_correction_bias"], np.float32), 8)[None, :])
    in_maps = []
    for c in range(NC):
        rows_idx = np.concatenate([np.arange(512 * c, 512 * (c + 1)),
                                   np.arange(512 * (15 - c), 512 * (16 - c))])
        xs = np.ascontiguousarray(x[rows_idx])
        cq, sq, ck, sk = _rope_tables(rows_idx)
        perm = [2 * c, 2 * c + 1] + [hh for hh in range(16) if hh not in (2 * c, 2 * c + 1)]
        tp = np.ascontiguousarray(table[:, perm])
        m = {"xin": xs, "cqr": cq, "sqr": sq, "ckr": ck, "skr": sk, "tablep": tp,
             "escb": escb_rep, "nramp": nrampv, "onehot": oh, "e31": e31v, "identf": identf}
        m.update(w)
        in_maps.append(m)
    return in_maps


def kernel(**inputs):
    in_maps = prep_in_maps(inputs)
    nc = build_program()
    res = run_bass_kernel_spmd(nc, in_maps, core_ids=list(range(NC)))
    full = np.empty((S, H), np.float32)
    for c in range(NC):
        o = np.asarray(res.results[c]["out"])
        full[512 * c:512 * (c + 1)] = o[:512]
        full[512 * (15 - c):512 * (16 - c)] = o[512:]
    return full.reshape(1, S, H)


# version 02
# speedup vs baseline: 1.0527x; 1.0527x over previous
import os
import numpy as np
import ml_dtypes
import concourse.bass as bass
import concourse.mybir as mybir
from concourse.bass_utils import run_bass_kernel_spmd
from concourse.alu_op_type import AluOpType as ALU
from contextlib import ExitStack

F32 = mybir.dt.float32
BF16 = mybir.dt.bfloat16
AF = mybir.ActivationFunctionType

NC = 8
H = 2048
HCN = 16
TOK = 1024
S = 8192
NH = 16
HD = 128
E = 16
ICN = 11
SCN = 22
LH = 2
QTN = 16
CSTAB = 12.0
NEGB = -30000.0

QSECT = NH * HD * TOK
SEC = 3 * QSECT
KOFF = QSECT
VOFF = 2 * QSECT
CTXSECT = LH * HD * S


class Ctr:
    __slots__ = ("h", "n")
    def __init__(self):
        self.h = None
        self.n = 0
    def inc(self, k=1):
        self.n += k
        return self.n


class KB:
    ENGS = ("tensor", "scalar", "vector", "gpsimd", "sync")

    def __init__(self, nc):
        self.nc = nc
        self.q = {e: [] for e in self.ENGS}
        self.sP = Ctr(); self.sA = Ctr(); self.sV = Ctr(); self.sC = Ctr()
        self.all_ctrs = [self.sP, self.sA, self.sV, self.sC]
        self.rt = {}

    def em(self, eng, fn):
        self.q[eng].append(fn)

    def wait(self, eng, ctr, v):
        if v:
            self.em(eng, lambda e, c=ctr, vv=v: e.wait_ge(c.h, vv))

    def dma(self, eng, ctr, out, in_, **kw):
        v = ctr.inc(16)
        self.em(eng, lambda e, o=out, i=in_, c=ctr, k=kw:
                e.dma_start(out=o, in_=i, **k).then_inc(c.h, 16))
        return v

    def dma_dyn(self, eng, ctr, out, tensor, off_static, dyn_key, apdims, **kw):
        v = ctr.inc(16)
        def fn(e, o=out, t=tensor, os_=off_static, dk=dyn_key, ad=apdims, c=ctr, en=eng, k=kw):
            src = bass.AP(t, self.rt[en][dk] + os_, [list(d) for d in ad])
            e.dma_start(out=o, in_=src, **k).then_inc(c.h, 16)
        self.em(eng, fn)
        return v

    def op(self, eng, ctr, fn):
        if eng == "vector":
            c = self.sV
            v = c.inc(1)
            self.em(eng, lambda e, f=fn, cc=c: f(e).then_inc(cc.h, 1))
            self.em(eng, lambda e, cc=c, vv=v: e.wait_ge(cc.h, vv))
            return v
        if ctr is None:
            self.em(eng, fn)
            return None
        v = ctr.inc(1)
        self.em(eng, lambda e, f=fn, c=ctr: f(e).then_inc(c.h, 1))
        return v

    def barrier(self):
        vals = [(c, c.n) for c in self.all_ctrs if c.n]
        for eng in self.ENGS:
            for c, v in vals:
                self.wait(eng, c, v)

    def init_rt(self):
        for eng in ("sync", "gpsimd"):
            def fn(e, en=eng):
                pid = e.partition_id()
                self.rt[en] = {
                    "kv": pid * 262144,
                    "clo": pid * 512,
                    "chi": pid * (-512) + 7680,
                }
            self.em(eng, fn)

    def run(self, block):
        for eng in self.ENGS:
            thunks = self.q[eng]
            def make(ts):
                def body(e):
                    for t in ts:
                        t(e)
                return body
            getattr(block, eng)(make(thunks))


def build_program(dump=(), upto="out", n_exp=E):
    nc = bass.Bass("TRN2", target_bir_lowering=False, debug=False, num_devices=NC)

    xin = nc.dram_tensor("xin", [TOK, H], F32, kind="ExternalInput").ap()
    out = nc.dram_tensor("out", [TOK, H], F32, kind="ExternalOutput").ap()
    wqkv = nc.dram_tensor("wqkv", [48, 128, 2048], F32, kind="ExternalInput").ap()
    wdense = nc.dram_tensor("wdense", [16, 128, 2048], F32, kind="ExternalInput").ap()
    wg = nc.dram_tensor("wg", [E, ICN, 128, 2048], F32, kind="ExternalInput").ap()
    wu = nc.dram_tensor("wu", [E, ICN, 128, 2048], F32, kind="ExternalInput").ap()
    wd = nc.dram_tensor("wd", [E, HCN, 128, 1408], F32, kind="ExternalInput").ap()
    wsg = nc.dram_tensor("wsg", [SCN, 128, 2048], F32, kind="ExternalInput").ap()
    wsu = nc.dram_tensor("wsu", [SCN, 128, 2048], F32, kind="ExternalInput").ap()
    wsd = nc.dram_tensor("wsd", [HCN, 128, 2816], F32, kind="ExternalInput").ap()
    gwt = nc.dram_tensor("gwt", [HCN, 128, 16], F32, kind="ExternalInput").ap()
    escb = nc.dram_tensor("escb", [1, 128], F32, kind="ExternalInput").ap()
    nramp = nc.dram_tensor("nramp", [1, 128], F32, kind="ExternalInput").ap()
    tablep = nc.dram_tensor("tablep", [32, 16], F32, kind="ExternalInput").ap()
    onehot = nc.dram_tensor("onehot", [32, 768], F32, kind="ExternalInput").ap()
    e31 = nc.dram_tensor("e31", [32, 1], F32, kind="ExternalInput").ap()
    identf_d = nc.dram_tensor("identf", [128, 128], F32, kind="ExternalInput").ap()
    cqr = nc.dram_tensor("cqr", [128, 1024], F32, kind="ExternalInput").ap()
    sqr = nc.dram_tensor("sqr", [128, 1024], F32, kind="ExternalInput").ap()
    ckr = nc.dram_tensor("ckr", [128, 1024], F32, kind="ExternalInput").ap()
    skr = nc.dram_tensor("skr", [128, 1024], F32, kind="ExternalInput").ap()

    kvl = nc.dram_tensor("kvl", [SEC], BF16, kind="Internal").ap()
    kvgt = nc.dram_tensor("kvg", [NC * SEC], BF16, kind="Internal")
    ctxl = nc.dram_tensor("ctxl", [CTXSECT], BF16, kind="Internal").ap()
    ctxgt = nc.dram_tensor("ctxg", [NC * CTXSECT], BF16, kind="Internal")
    biasd = nc.dram_tensor("biasd", [LH * 1408], F32, kind="Internal").ap()
    kvh = nc.dram_tensor("kvh", [LH * 8 * 3 * 128 * 1024], BF16, kind="Internal").ap()
    ctxh = nc.dram_tensor("ctxh", [2048 * 1024], BF16, kind="Internal").ap()
    combd = nc.dram_tensor("combd", [E, TOK], F32, kind="Internal").ap()

    dbg = {}
    for name in dump:
        dbg[name] = nc.dram_tensor("dbg_" + name, [H, TOK], F32, kind="ExternalOutput").ap()

    es = ExitStack()
    with es:
        sb = lambda name, shape, dt: es.enter_context(nc.sbuf_tensor(name, shape, dt))
        xres = [sb(f"xres{i}", [128, 1024], F32) for i in range(HCN)]
        yb = [sb(f"yb{i}", [128, 1024], BF16) for i in range(HCN)]
        inter = sb("inter", [128, 11264], BF16)
        wblk = [sb(f"wblk{i}", [128, 2048], BF16) for i in range(4)]
        cmb = [sb(f"cmb{i}", [128, 1024], F32) for i in range(2)]
        gsb = [sb(f"gsb{i}", [128, 512], F32) for i in range(2)]
        Asb = sb("Asb", [128, 1024], F32)
        Bsb = sb("Bsb", [128, 1024], F32)
        sq2 = [sb(f"sq{i}", [128, 1024], BF16) for i in range(2)]
        identf = sb("identf_sb", [128, 128], F32)
        identb = sb("identb_sb", [128, 128], BF16)
        gwb = sb("gwb", [128, 256], BF16)
        tps = sb("tps", [32, 16], F32)
        ohsb = sb("ohsb", [32, 768], F32)
        e31sb = sb("e31sb", [32, 1], F32)
        biasrow = sb("biasrow", [2, 1408], F32)
        onescolb = sb("onescolb", [128, 1], BF16)
        onesrf = sb("onesrf", [128, 128], F32)
        onesr = lambda b: onesrf.ap()[b:b + 1, 0:128]
        cstC = sb("cstC", [128, 1], F32)
        cstEps = sb("cstEps", [128, 1], F32)
        rows = [sb(f"row{i}", [1, 1024], F32) for i in range(5)]
        R_M, R_V, R_A, R_B, R_T = [r.ap() for r in rows]
        R_DR0, R_DR1 = R_M, R_V
        R_CR0, R_CR1 = R_M, R_V
        farb = sb("farb", [128, 2], F32)
        bbrep = sb("bbrep", [128, 128], F32)
        nrrep = sb("nrrep", [128, 128], F32)
        ga = sb("ga", [128, 128], F32)
        gb = sb("gb", [128, 128], F32)
        gc = sb("gc", [128, 128], F32)
        gd = sb("gd", [128, 128], F32)
        ge = sb("ge", [128, 128], F32)
        gsm = sb("gsm", [128, 32], F32)
        r16 = sb("r16", [16, 1024], F32)
        qsb = sb("qsb", [128, 1024], BF16)
        ksb = sb("ksb", [128, 1024], BF16)
        vsb = sb("vsb", [128, 1024], BF16)
        vt = [sb(f"vt{i}", [128, 128], BF16) for i in range(2)]
        ctxsb = [sb(f"ctxsb{i}", [128, 512], BF16) for i in range(2)]
        outh = [Asb, Bsb]

        w0b = wblk[0].ap()
        qtl = [w0b[:, 0:512], w0b[:, 512:1024]]
        kblk = [w0b[:, 1024:1536], w0b[:, 1536:2048]]
        w1b = wblk[1].ap()
        vch = [w1b[:, i * 128:(i + 1) * 128] for i in range(8)]
        tmpf = [wblk[2].ap().bitcast(F32), wblk[3].ap().bitcast(F32)]
        interf = inter.ap().bitcast(F32)
        cqrep = interf[:, 0:1024]
        sqrep = interf[:, 1024:2048]
        ckrep = interf[:, 2048:3072]
        skrep = interf[:, 3072:4096]
        re1 = interf[:, 4096:4608]
        re2 = interf[:, 4608:5120]
        re2s = interf[:, 5120:5632]
        interb = inter.ap()
        psb = [interb[:, 0:1024], interb[:, 1024:2048]]
        bcsb = [inter.ap().bitcast(F32)[:, 1024:1536], inter.ap().bitcast(F32)[:, 1536:2048]]
        def ichunk(kk, t):
            return interb[:, kk * 1024 + t * 512: kk * 1024 + (t + 1) * 512]

        pA = es.enter_context(nc.psum_tensor("pA", [128, 1024], F32))
        pB = es.enter_context(nc.psum_tensor("pB", [128, 1024], F32))
        pC0 = es.enter_context(nc.psum_tensor("pC0", [128, 512], F32))
        pC1 = es.enter_context(nc.psum_tensor("pC1", [128, 512], F32))
        pD = es.enter_context(nc.psum_tensor("pD", [128, 512], F32))
        pt0 = es.enter_context(nc.psum_tensor("pt0", [128, 128], BF16))

        class Bank:
            def __init__(self, a):
                self._a = a
            def ap(self):
                return self._a
        pb = [Bank(pA.ap()[:, 0:512]), Bank(pA.ap()[:, 512:1024]),
              Bank(pB.ap()[:, 0:512]), Bank(pB.ap()[:, 512:1024]),
              Bank(pC0.ap()), Bank(pC1.ap()), Bank(pD.ap())]

        k = KB(nc)
        for nm in ("sP", "sA", "sV", "sC"):
            getattr(k, nm).h = es.enter_context(nc.semaphore(nm))

        def newsem(nm):
            c = Ctr()
            c.h = es.enter_context(nc.semaphore(nm))
            k.all_ctrs.append(c)
            return c
        s_wb = [newsem(f"s_wb{i}") for i in range(4)]
        s_qtl = [newsem(f"s_qtl{i}") for i in range(2)]
        s_kb = [newsem(f"s_kb{i}") for i in range(2)]
        s_v = [newsem(f"s_v{i}") for i in range(8)]
        s_tb = [newsem(f"s_tb{i}") for i in range(2)]
        s_cr = [newsem(f"s_cr{i}") for i in range(2)]
        s_cst = [newsem(f"s_cst{i}") for i in range(2)]
        s_vst = [newsem(f"s_vst{i}") for i in range(2)]
        s_out = [newsem(f"s_out{i}") for i in range(2)]
        s_pc = newsem("s_pc")
        s_gw = newsem("s_gw")
        s_rt = newsem("s_rt")
        s_sw = newsem("s_sw")
        s_qs = newsem("s_qs")
        s_ks = newsem("s_ks")
        s_dld = newsem("s_dld")
        s_cmw = newsem("s_cmw")
        s_bw = newsem("s_bw")
        s_xl = newsem("s_xl")
        s_kvh = [newsem(f"s_kvh{i}") for i in range(2)]
        s_cxh = newsem("s_cxh")
        s_dbg = newsem("s_dbg")
        k.init_rt()

        def mm(psum_ap, lhsT, rhs, start, stop, inc=False):
            return k.op("tensor", k.sP if inc else None,
                        lambda e, p=psum_ap, l=lhsT, r=rhs, s1=start, s2=stop:
                        e.matmul(p, l, r, start=s1, stop=s2, skip_group_check=True))

        def evac(dst, src, func=AF.Copy, bias=0.0, scale=1.0):
            return k.op("scalar", k.sA,
                        lambda e, d=dst, s=src, f=func, b=bias, sc=scale:
                        e.activation(d, s, f, bias=b, scale=sc))

        def ph_load_x():
            tr_done = {}
            last_pv = {}
            for tc in range(8):
                if tc >= 1:
                    k.wait("sync", k.sP, last_pv[tc - 1])
                k.dma("sync", s_xl, outh[0][:], xin[tc * 128:(tc + 1) * 128, 0:1024])
                lv1 = k.dma("sync", s_xl, outh[1][:], xin[tc * 128:(tc + 1) * 128, 1024:2048])
                k.wait("tensor", s_xl, lv1)
                for hc in range(HCN):
                    src = outh[hc // 8][:, (hc % 8) * 128:(hc % 8 + 1) * 128]
                    bank = pb[hc % 2].ap()[:, 0:128]
                    prev = tr_done.get((tc, hc - 2)) or tr_done.get((tc - 1, hc + 14))
                    k.wait("tensor", k.sA, prev)
                    pv = k.op("tensor", k.sP,
                              lambda e, b=bank, s=src: e.transpose(b, s, identf[:]))
                    last_pv[tc] = pv
                    k.wait("scalar", k.sP, pv)
                    tr_done[(tc, hc)] = evac(xres[hc][:, tc * 128:(tc + 1) * 128], bank)

        def ph_prep():
            sv = {}
            k.dma("sync", s_pc, identf[:], identf_d)
            k.dma("sync", s_pc, tps[:], tablep)
            k.dma("sync", s_pc, ohsb[:], onehot)
            k.dma("sync", s_pc, e31sb[:], e31)
            k.dma("sync", s_pc, R_CR0[:, 0:128], escb)
            pcv = k.dma("sync", s_pc, R_CR1[:, 0:128], nramp)
            for key in ("identf", "tps", "oh", "e31", "escb", "nramp"):
                sv[key] = pcv
            gwv = None
            for c in range(HCN):
                gwv = k.dma("gpsimd", s_gw, gwb[:, c * 16:(c + 1) * 16], gwt[c])
            k.op("vector", None, lambda e: e.memset(onescolb[:], 1.0))
            k.op("vector", None, lambda e: e.memset(cstC[:], -CSTAB))
            k.op("vector", None, lambda e: e.memset(cstEps[:], 1e-5))
            brm = k.op("vector", k.sV, lambda e: e.memset(biasrow[:], NEGB))
            vmem = k.op("vector", k.sV, lambda e: e.memset(onesrf[:], 1.0))
            k.wait("vector", s_pc, sv["identf"])
            k.op("vector", k.sV, lambda e: e.tensor_copy(identb[:], identf[:]))
            k.wait("tensor", s_pc, sv["tps"])
            k.wait("tensor", s_pc, sv["oh"])
            p1 = mm(pb[0].ap()[0:2, :], tps.ap()[0:32, 0:2], ohsb.ap()[0:32, 0:512],
                    True, True, inc=True)
            p2 = mm(pb[1].ap()[0:2, 0:256], tps.ap()[0:32, 0:2], ohsb.ap()[0:32, 512:768],
                    True, True, inc=True)
            k.wait("scalar", k.sP, p1)
            k.wait("scalar", k.sV, brm)
            evac(biasrow.ap()[0:2, 638:1150], pb[0].ap()[0:2, :])
            k.wait("scalar", k.sP, p2)
            a2 = evac(biasrow.ap()[0:2, 1150:1406], pb[1].ap()[0:2, 0:256])
            k.wait("vector", k.sA, a2)
            a2v = k.op("vector", k.sV, lambda e: e.tensor_scalar(
                biasrow.ap()[0:2, 638:1406], biasrow.ap()[0:2, 638:1406], -CSTAB, None, ALU.add))
            k.wait("sync", k.sV, a2v)
            k.dma("sync", s_bw, bass.AP(biasd.tensor, 0, [[1408, 2], [1, 1408]]), biasrow[:])
            k.wait("tensor", s_pc, sv["e31"])
            p3 = mm(pb[2].ap()[0:1, 0:2], e31sb[:], tps.ap()[0:32, 0:2], True, True, inc=True)
            k.wait("scalar", k.sP, p3)
            a3 = evac(R_T[:, 0:2], pb[2].ap()[0:1, 0:2])
            k.wait("tensor", k.sA, a3)
            k.wait("tensor", k.sV, vmem)
            p4 = mm(pb[3].ap()[:, 0:2], onesr(0), R_T[:, 0:2], True, True, inc=True)
            k.wait("scalar", k.sP, p4)
            a4 = evac(farb[:], pb[3].ap()[:, 0:2])
            k.wait("vector", k.sA, a4)
            k.op("vector", k.sV, lambda e: e.tensor_scalar(
                farb[:], farb[:], -CSTAB, None, ALU.add))
            k.wait("tensor", s_pc, sv["escb"])
            p5 = mm(pb[4].ap()[:, 0:128], onesr(0), R_CR0[:, 0:128], True, True, inc=True)
            k.wait("scalar", k.sP, p5)
            evac(bbrep[:], pb[4].ap()[:, 0:128])
            k.wait("tensor", s_pc, sv["nramp"])
            p6 = mm(pb[5].ap()[:, 0:128], onesr(0), R_CR1[:, 0:128], True, True, inc=True)
            k.wait("scalar", k.sP, p6)
            evac(nrrep[:], pb[5].ap()[:, 0:128])
            k.wait("vector", s_gw, gwv)

        def ph_ln():
            cast_done = {}
            for c in range(HCN):
                cast_done[c] = k.op("vector", k.sV,
                                    lambda e, c_=c: e.tensor_copy(yb[c_][:], xres[c_][:]))
            sqmm = {}
            last = None
            for c in range(HCN):
                if c >= 2:
                    k.wait("scalar", k.sP, sqmm[c - 2])
                sq_done = k.op("scalar", k.sA,
                               lambda e, c_=c: e.activation(sq2[c_ % 2][:], xres[c_][:], AF.Square))
                k.wait("tensor", k.sV, cast_done[c])
                k.wait("tensor", k.sA, sq_done)
                for t in range(2):
                    mm(pb[t].ap()[0:1, :], onescolb[:], yb[c][:, t * 512:(t + 1) * 512],
                       c == 0, c == HCN - 1)
                for t in range(2):
                    last = mm(pb[2 + t].ap()[0:1, :], onescolb[:],
                              sq2[c % 2][:, t * 512:(t + 1) * 512],
                              c == 0, c == HCN - 1, inc=(t == 1))
                sqmm[c] = last
            k.wait("vector", k.sP, last)
            def VS(fn):
                v = k.op("vector", k.sV, fn)
                k.wait("vector", k.sV, v)
                return v
            for t in range(2):
                sl = slice(t * 512, (t + 1) * 512)
                VS(lambda e, t_=t, s_=sl: e.tensor_scalar(
                    R_M[:, s_], pb[t_].ap()[0:1, :], 1.0 / H, None, ALU.mult))
                VS(lambda e, t_=t, s_=sl: e.tensor_scalar(
                    R_V[:, s_], pb[2 + t_].ap()[0:1, :], 1.0 / H, None, ALU.mult))
            VS(lambda e: e.tensor_mul(R_T, R_M, R_M))
            vrow = VS(lambda e: e.tensor_sub(R_V, R_V, R_T))
            k.wait("scalar", k.sV, vrow)
            asq = k.op("scalar", k.sA,
                       lambda e: e.activation(R_T, R_V, AF.Sqrt, bias=cstEps.ap()[0:1, 0:1]))
            k.wait("vector", k.sA, asq)
            VS(lambda e: e.reciprocal(R_A, R_T))
            VS(lambda e: e.tensor_mul(R_B, R_M, R_A))
            brow = VS(lambda e: e.tensor_scalar(R_B, R_B, -1.0, None, ALU.mult))
            k.wait("tensor", k.sV, brow)
            for r, rb, dstt in ((R_A, 0, Asb), (R_B, 0, Bsb)):
                for t in range(2):
                    pv = mm(pb[6].ap(), onesr(rb), r[:, t * 512:(t + 1) * 512],
                            True, True, inc=True)
                    k.wait("scalar", k.sP, pv)
                    ev = evac(dstt[:, t * 512:(t + 1) * 512], pb[6].ap())
                    k.wait("tensor", k.sA, ev)
            k.wait("vector", k.sA, ev)
            y_done = {}
            for c in range(HCN):
                mv = k.op("vector", k.sV, lambda e, c_=c: e.tensor_mul(cmb[0][:], xres[c_][:], Asb[:]))
                k.wait("vector", k.sV, mv)
                y_done[c] = k.op("vector", k.sV,
                                 lambda e, c_=c: e.tensor_add(yb[c_][:], cmb[0][:], Bsb[:]))
            return y_done

        def ph_router(y_done):
            lg = None
            for c in range(HCN):
                k.wait("tensor", k.sV, y_done[c])
                for t in range(2):
                    lg = mm(pb[2 + t].ap()[0:16, :], gwb[:, c * 16:(c + 1) * 16],
                            yb[c][:, t * 512:(t + 1) * 512], c == 0, c == HCN - 1,
                            inc=(c == HCN - 1 and t == 1))
            k.wait("scalar", k.sP, lg)
            evac(r16.ap()[0:16, 0:512], pb[2].ap()[0:16, :])
            ev1 = evac(r16.ap()[0:16, 512:1024], pb[3].ap()[0:16, :])
            if "gdbg" in dbg:
                k.wait("sync", k.sA, ev1)
                s1 = k.dma("sync", s_dbg, dbg["gdbg"][11 * 128:11 * 128 + 16, :], r16[:])
                s2 = k.dma("sync", s_dbg, dbg["gdbg"][12 * 128:13 * 128, 0:128], bbrep[:])
                k.wait("tensor", s_dbg, s2)
            sig = {}
            k.wait("tensor", k.sA, ev1)
            for t8 in range(8):
                bank = pb[4 + (t8 % 2)].ap()[:, 0:16]
                if t8 >= 2:
                    k.wait("tensor", k.sA, sig[t8 - 2])
                pv = k.op("tensor", k.sP, lambda e, b=bank, t_=t8: e.transpose(
                    b, r16.ap()[0:16, t_ * 128:(t_ + 1) * 128], identf.ap()[0:16, 0:16]))
                k.wait("scalar", k.sP, pv)
                sig[t8] = k.op("scalar", k.sA, lambda e, b=bank, t_=t8: e.activation(
                    gb[:, t_ * 16:(t_ + 1) * 16], b, AF.Sigmoid))
            k.wait("vector", k.sA, sig[7])
            def V(fn):
                v = k.op("vector", k.sV, fn)
                k.wait("vector", k.sV, v)
                return v
            k.snap(0, gb[:])
            V(lambda e: e.tensor_add(gc[:], gb[:], bbrep[:]))
            k.snap(1, gc[:])
            g4 = lambda t, j: t.ap()[:, j:128:4]
            dsts = [gd.ap()[:, 0:32], gd.ap()[:, 32:64], gd.ap()[:, 64:96],
                    gd.ap()[:, 96:128], ga.ap()[:, 0:32], ga.ap()[:, 32:64]]
            for i, (a, b) in enumerate([(0, 1), (2, 3), (0, 2), (0, 3), (1, 2), (1, 3)]):
                V(lambda e, a_=a, b_=b, d=dsts[i]: e.tensor_tensor(d, g4(gc, a_), g4(gc, b_), ALU.add))
            V(lambda e: e.tensor_max(gd.ap()[:, 0:32], gd.ap()[:, 0:32], gd.ap()[:, 32:64]))
            V(lambda e: e.tensor_max(gd.ap()[:, 64:96], gd.ap()[:, 64:96], gd.ap()[:, 96:128]))
            V(lambda e: e.tensor_max(ga.ap()[:, 0:32], ga.ap()[:, 0:32], ga.ap()[:, 32:64]))
            V(lambda e: e.tensor_max(gd.ap()[:, 64:96], gd.ap()[:, 64:96], ga.ap()[:, 0:32]))
            V(lambda e: e.tensor_max(gd.ap()[:, 0:32], gd.ap()[:, 0:32], gd.ap()[:, 64:96]))
            k.snap(2, gd[:])
            t4 = lambda j: gd.ap()[:, j:32:4]
            V(lambda e: e.tensor_max(gsm.ap()[:, 0:8], t4(0), t4(1)))
            V(lambda e: e.tensor_tensor(gsm.ap()[:, 8:16], t4(0), t4(1), ALU.min))
            V(lambda e: e.tensor_max(gsm.ap()[:, 16:24], t4(2), t4(3)))
            V(lambda e: e.tensor_tensor(gsm.ap()[:, 24:32], t4(2), t4(3), ALU.min))
            V(lambda e: e.tensor_tensor(ga.ap()[:, 0:8], gsm.ap()[:, 0:8], gsm.ap()[:, 16:24], ALU.min))
            V(lambda e: e.tensor_max(ga.ap()[:, 8:16], gsm.ap()[:, 8:16], gsm.ap()[:, 24:32]))
            V(lambda e: e.tensor_max(gsm.ap()[:, 0:8], ga.ap()[:, 0:8], ga.ap()[:, 8:16]))
            k.snap(3, gsm[:], width=32)
            for g in range(4):
                V(lambda e, g_=g: e.tensor_tensor(gd.ap()[:, 32 + g_:64 + g_:4][:, 0:8],
                                                  t4(g_), gsm.ap()[:, 0:8], ALU.is_ge))
            k.snap(4, gd[:])
            for j in range(4):
                V(lambda e, j_=j: e.tensor_copy(ga.ap()[:, j_:128:4], gd.ap()[:, 32:64]))
            k.snap(5, ga[:])
            V(lambda e: e.tensor_scalar(ge[:], ga[:], -1.0, 1.0, ALU.mult, ALU.add))
            V(lambda e: e.tensor_mul(gd[:], gc[:], ga[:]))
            V(lambda e: e.tensor_mul(ge[:], ge[:], nrrep[:]))
            V(lambda e: e.tensor_add(gd[:], gd[:], ge[:]))
            k.snap(6, gd[:])
            V(lambda e: e.memset(gc[:], 0.0))
            for it in range(6):
                V(lambda e: e.tensor_max(ga.ap()[:, 0:64], gd.ap()[:, 0:128:2], gd.ap()[:, 1:128:2]))
                V(lambda e: e.tensor_max(ga.ap()[:, 0:32], ga.ap()[:, 0:64:2], ga.ap()[:, 1:64:2]))
                V(lambda e: e.tensor_max(ga.ap()[:, 0:16], ga.ap()[:, 0:32:2], ga.ap()[:, 1:32:2]))
                V(lambda e: e.tensor_max(gsm.ap()[:, 0:8], ga.ap()[:, 0:16:2], ga.ap()[:, 1:16:2]))
                for t8 in range(8):
                    V(lambda e, t_=t8: e.tensor_scalar(
                        ge.ap()[:, t_ * 16:(t_ + 1) * 16], gd.ap()[:, t_ * 16:(t_ + 1) * 16],
                        gsm.ap()[:, t_:t_ + 1], None, ALU.is_equal))
                V(lambda e: e.tensor_add(gc[:], gc[:], ge[:]))
                V(lambda e: e.tensor_scalar(ge[:], ge[:], -1e9, None, ALU.mult))
                V(lambda e: e.tensor_add(gd[:], gd[:], ge[:]))
            k.snap(7, gc[:])
            V(lambda e: e.tensor_mul(ge[:], gc[:], gb[:]))
            V(lambda e: e.tensor_add(ga.ap()[:, 0:64], ge.ap()[:, 0:128:2], ge.ap()[:, 1:128:2]))
            V(lambda e: e.tensor_add(ga.ap()[:, 0:32], ga.ap()[:, 0:64:2], ga.ap()[:, 1:64:2]))
            V(lambda e: e.tensor_add(ga.ap()[:, 0:16], ga.ap()[:, 0:32:2], ga.ap()[:, 1:32:2]))
            V(lambda e: e.tensor_add(gsm.ap()[:, 8:16], ga.ap()[:, 0:16:2], ga.ap()[:, 1:16:2]))
            V(lambda e: e.tensor_scalar(gsm.ap()[:, 8:16], gsm.ap()[:, 8:16], 1e-20, None, ALU.add))
            V(lambda e: e.reciprocal(gsm.ap()[:, 16:24], gsm.ap()[:, 8:16]))
            k.snap(8, ge[:])
            k.snap(9, gsm[:], width=32)
            comb_ops = {}
            for t8 in range(8):
                comb_ops[t8] = k.op("vector", k.sV, lambda e, t_=t8: e.tensor_scalar(
                    gc.ap()[:, t_ * 16:(t_ + 1) * 16], ge.ap()[:, t_ * 16:(t_ + 1) * 16],
                    gsm.ap()[:, 16 + t_:17 + t_], 1.25, ALU.mult, ALU.mult))
            k.snap(10, gc[:])
            tr_ev = {}
            for t8 in range(8):
                k.wait("tensor", k.sV, comb_ops[t8])
                bank = pb[4 + (t8 % 2)].ap()[0:16, 0:128]
                if t8 >= 2:
                    k.wait("tensor", k.sA, tr_ev[t8 - 2])
                pv = k.op("tensor", k.sP, lambda e, b=bank, t_=t8: e.transpose(
                    b, gc.ap()[:, t_ * 16:(t_ + 1) * 16], identf[:]))
                k.wait("scalar", k.sP, pv)
                tr_ev[t8] = evac(r16.ap()[0:16, t8 * 128:(t8 + 1) * 128], bank)
            k.wait("sync", k.sA, tr_ev[7])
            return k.dma("sync", s_cmw, bass.AP(combd.tensor, 0, [[TOK, 16], [1, TOK]]), r16[:])

        SKIP = set(os.environ.get("KSKIP", "").split(","))
        def ph_experts(comb_stored, y_done):
            k.wait("sync", s_cmw, comb_stored)
            k.wait("vector", k.sV, y_done[HCN - 1])
            wlv = {}
            blk_free = {}
            crow_free = {}
            cmb_free = {}
            sil_prev = {0: None, 1: None}
            h1_prev = {0: None, 1: None}
            iv_prev = {0: None, 1: None}
            iv_last = {0: None, 1: None}
            down_d = {}
            add_d = {}
            bcast_ev = {}
            nb = 0
            for e_ in range(n_exp):
                r = R_CR0 if e_ % 2 == 0 else R_CR1
                rbase = 0
                if e_ >= 2:
                    k.wait("sync", k.sP, crow_free[e_ - 2])
                crow_ld = k.dma("sync", s_cr[e_ % 2], r[:, 0:1024], combd[e_:e_ + 1, :])
                k.wait("tensor", s_cr[e_ % 2], crow_ld)
                if e_ >= 1:
                    k.wait("tensor", k.sA, bcast_ev[e_ - 1])
                ev = None
                for t in range(2):
                    if t == 1:
                        k.wait("tensor", k.sA, ev)
                    pv = mm(pb[6].ap(), onesr(rbase), r[:, t * 512:(t + 1) * 512],
                            True, True, inc=True)
                    k.wait("scalar", k.sP, pv)
                    if e_ >= 2:
                        k.wait("scalar", k.sV, cmb_free[e_ - 2])
                    ev = evac(cmb[e_ % 2][:, t * 512:(t + 1) * 512], pb[6].ap())
                crow_free[e_] = pv
                bcast_ev[e_] = ev
                cmb_ready = ev
                for ic in range(ICN):
                    for warr in (wg, wu):
                        if nb >= 4:
                            k.wait("gpsimd", k.sP, blk_free.get(nb - 4))
                        wlv[nb] = (k.dma("gpsimd", s_wb[nb % 4], wblk[nb % 4][:], warr[e_, ic]), s_wb[nb % 4])
                        nb += 1
                    gslot, uslot = (nb - 2) % 4, (nb - 1) % 4
                    if "gate" in SKIP:
                        blk_free[nb - 2] = None
                        blk_free[nb - 1] = None
                        continue
                    for t in range(2):
                        k.wait("tensor", wlv[nb - 2][1], wlv[nb - 2][0])
                        k.wait("tensor", k.sA, sil_prev[t])
                        for c in range(HCN):
                            gmm = mm(pb[t].ap(), wblk[gslot][:, c * 128:(c + 1) * 128],
                                     yb[c][:, t * 512:(t + 1) * 512], c == 0, c == HCN - 1,
                                     inc=(c == HCN - 1))
                        if "silu" in SKIP:
                            sil_prev[t] = gmm and k.sA.n
                            iv_prev[t] = k.sV.n
                            iv_last[t] = iv_prev[t]
                            if t == 1:
                                blk_free[nb - 2] = gmm
                                blk_free[nb - 1] = gmm
                            if ic == ICN - 1 and t == 1:
                                cmb_free[e_] = k.sV.n
                            continue
                        k.wait("scalar", k.sP, gmm)
                        k.wait("scalar", k.sV, iv_prev[t])
                        sil_prev[t] = k.op("scalar", k.sA, lambda e, t_=t: e.activation(
                            gsb[t_][:], pb[t_].ap(), AF.Silu))
                        if "up" in SKIP:
                            if t == 1:
                                blk_free[nb - 2] = gmm
                                blk_free[nb - 1] = gmm
                            iv_prev[t] = sil_prev[t] and k.sV.n
                            iv_last[t] = iv_prev[t]
                            if ic == ICN - 1 and t == 1:
                                cmb_free[e_] = k.sV.n
                            continue
                        k.wait("tensor", wlv[nb - 1][1], wlv[nb - 1][0])
                        k.wait("tensor", k.sV, h1_prev[t])
                        for c in range(HCN):
                            umm = mm(pb[2 + t].ap(), wblk[uslot][:, c * 128:(c + 1) * 128],
                                     yb[c][:, t * 512:(t + 1) * 512], c == 0, c == HCN - 1,
                                     inc=(c == HCN - 1))
                        if t == 1:
                            blk_free[nb - 2] = umm
                            blk_free[nb - 1] = umm
                        k.wait("vector", k.sP, umm)
                        k.wait("vector", k.sA, sil_prev[t])
                        k.wait("vector", k.sA, cmb_ready)
                        if e_ >= 1 and ic == 0:
                            k.wait("vector", k.sP, down_d[(e_ - 1, t)])
                        if "h1inplace" in SKIP:
                            h1_prev[t] = k.op("vector", k.sV, lambda e, t_=t: e.tensor_copy(
                                cmb[0][:, 0:512], pb[2 + t_].ap()))
                            k.wait("vector", k.sV, h1_prev[t])
                            k.op("vector", None, lambda e, t_=t: e.tensor_mul(
                                gsb[t_][:], cmb[0][:, 0:512], gsb[t_][:]))
                        else:
                            h1_prev[t] = k.op("vector", k.sV, lambda e, t_=t: e.tensor_mul(
                                gsb[t_][:], pb[2 + t_].ap(), gsb[t_][:]))
                        k.wait("vector", k.sV, h1_prev[t])
                        iv_prev[t] = k.op("vector", k.sV, lambda e, t_=t, ic_=ic, e2=e_:
                                          e.tensor_mul(ichunk(ic_, t_), gsb[t_][:],
                                                       cmb[e2 % 2][:, t_ * 512:(t_ + 1) * 512]))
                        iv_last[t] = iv_prev[t]
                        if ic == ICN - 1 and t == 1:
                            cmb_free[e_] = iv_prev[t]
                if "down" in SKIP:
                    for t in range(2):
                        down_d[(e_, t)] = iv_last[t] and k.sP.n
                    continue
                for hc in range(HCN):
                    if nb >= 4:
                        k.wait("gpsimd", k.sP, blk_free[nb - 4])
                    wlv[nb] = (k.dma("gpsimd", s_wb[nb % 4], wblk[nb % 4][:, 0:1408], wd[e_, hc]), s_wb[nb % 4])
                    nb += 1
                    slot = (nb - 1) % 4
                    k.wait("tensor", wlv[nb - 1][1], wlv[nb - 1][0])
                    for t in range(2):
                        bank = pb[4 + t]
                        k.wait("tensor", k.sV, add_d.get((hc - 1, t)))
                        if hc == 0:
                            k.wait("tensor", k.sV, iv_last[t])
                            if e_ >= 1:
                                k.wait("tensor", k.sV, add_d.get((HCN - 1, t)))
                        for ic in range(ICN):
                            pv = mm(bank.ap(), wblk[slot][:, ic * 128:(ic + 1) * 128],
                                    ichunk(ic, t), ic == 0, ic == ICN - 1, inc=(ic == ICN - 1))
                        down_d[(e_, t)] = pv
                        if t == 1:
                            blk_free[nb - 1] = pv
                        k.wait("vector", k.sP, pv)
                        add_d[(hc, t)] = k.op("vector", k.sV,
                            lambda e, hc_=hc, t_=t, b=bank: e.tensor_add(
                                xres[hc_][:, t_ * 512:(t_ + 1) * 512],
                                xres[hc_][:, t_ * 512:(t_ + 1) * 512], b.ap()))

        def ph_shared(y_done):
            k.wait("vector", k.sV, y_done[HCN - 1])
            wlv = {}
            blk_free = {}
            sil_d = {}
            h1_d = {}
            iv_d = {}
            add_d = {}
            down_half = {}
            nb = 0
            for half in range(2):
                for sc in range(11):
                    for warr in (wsg, wsu):
                        if nb >= 4:
                            k.wait("gpsimd", k.sP, blk_free[nb - 4])
                        wlv[nb] = (k.dma("gpsimd", s_wb[nb % 4], wblk[nb % 4][:], warr[half * 11 + sc]), s_wb[nb % 4])
                        nb += 1
                    gslot, uslot = (nb - 2) % 4, (nb - 1) % 4
                    for t in range(2):
                        k.wait("tensor", wlv[nb - 2][1], wlv[nb - 2][0])
                        k.wait("tensor", k.sA, sil_d.get((half, sc - 1, t)))
                        if sc == 0 and half == 1:
                            k.wait("tensor", k.sA, sil_d.get((0, 10, t)))
                        for c in range(HCN):
                            gmm = mm(pb[t].ap(), wblk[gslot][:, c * 128:(c + 1) * 128],
                                     yb[c][:, t * 512:(t + 1) * 512], c == 0, c == HCN - 1,
                                     inc=(c == HCN - 1))
                        k.wait("scalar", k.sP, gmm)
                        k.wait("scalar", k.sV, iv_d.get((half, sc - 1, t)))
                        if sc == 0 and half == 1:
                            k.wait("scalar", k.sV, iv_d.get((0, 10, t)))
                        sil_d[(half, sc, t)] = k.op("scalar", k.sA, lambda e, t_=t:
                                                    e.activation(gsb[t_][:], pb[t_].ap(), AF.Silu))
                        k.wait("tensor", wlv[nb - 1][1], wlv[nb - 1][0])
                        k.wait("tensor", k.sV, h1_d.get((half, sc - 1, t)))
                        if sc == 0 and half == 1:
                            k.wait("tensor", k.sV, h1_d.get((0, 10, t)))
                        for c in range(HCN):
                            umm = mm(pb[2 + t].ap(), wblk[uslot][:, c * 128:(c + 1) * 128],
                                     yb[c][:, t * 512:(t + 1) * 512], c == 0, c == HCN - 1,
                                     inc=(c == HCN - 1))
                        if t == 1:
                            blk_free[nb - 2] = umm
                            blk_free[nb - 1] = umm
                        k.wait("vector", k.sP, umm)
                        k.wait("vector", k.sA, sil_d[(half, sc, t)])
                        if half == 1 and sc == 0:
                            k.wait("vector", k.sP, down_half[(0, t)])
                        h1_d[(half, sc, t)] = k.op("vector", k.sV, lambda e, t_=t:
                                                   e.tensor_mul(gsb[t_][:], pb[2 + t_].ap(), gsb[t_][:]))
                        k.wait("vector", k.sV, h1_d[(half, sc, t)])
                        iv_d[(half, sc, t)] = k.op("vector", k.sV, lambda e, t_=t, sc_=sc:
                                                   e.tensor_copy(ichunk(sc_, t_), gsb[t_][:]))
                for hc in range(HCN):
                    if nb >= 4:
                        k.wait("gpsimd", k.sP, blk_free[nb - 4])
                    wlv[nb] = (k.dma("gpsimd", s_wb[nb % 4], wblk[nb % 4][:, 0:1408],
                                     bass.AP(wsd.tensor, hc * 128 * 2816 + half * 1408,
                                             [[2816, 128], [1, 1408]])), s_wb[nb % 4])
                    nb += 1
                    slot = (nb - 1) % 4
                    k.wait("tensor", wlv[nb - 1][1], wlv[nb - 1][0])
                    for t in range(2):
                        bank = pb[4 + t]
                        k.wait("tensor", k.sV, add_d.get((half, hc - 1, t)))
                        if hc == 0:
                            k.wait("tensor", k.sV, iv_d[(half, 10, t)])
                            if half == 1:
                                k.wait("tensor", k.sV, add_d.get((0, HCN - 1, t)))
                        for sc in range(11):
                            pv = mm(bank.ap(), wblk[slot][:, sc * 128:(sc + 1) * 128],
                                    ichunk(sc, t), sc == 0, sc == 10, inc=(sc == 10))
                        down_half[(half, t)] = pv
                        if t == 1:
                            blk_free[nb - 1] = pv
                        k.wait("vector", k.sP, pv)
                        add_d[(half, hc, t)] = k.op("vector", k.sV,
                            lambda e, hc_=hc, t_=t, b=bank: e.scalar_tensor_tensor(
                                xres[hc_][:, t_ * 512:(t_ + 1) * 512], b.ap(), 0.5,
                                xres[hc_][:, t_ * 512:(t_ + 1) * 512], ALU.mult, ALU.add))

        def ph_qkv(y_done):
            for dst, src in ((cqrep, cqr), (sqrep, sqr), (ckrep, ckr), (skrep, skr)):
                rp = k.dma("sync", s_rt, dst, src)
            k.wait("vector", s_rt, rp)
            k.wait("tensor", k.sV, y_done[HCN - 1])
            wlv = {}
            blk_free = {}
            mmv = {}
            bank_free = {}
            st_qk = {}
            st_v = {}
            vtr = {}
            vtr_ev = {}
            vst = {}
            rv_prev = None
            for o in range(48):
                if o >= 4:
                    k.wait("gpsimd", k.sP, blk_free[o - 4])
                wlv[o] = k.dma("gpsimd", s_wb[o % 4], wblk[o % 4][:], wqkv[o])
                k.wait("tensor", s_wb[o % 4], wlv[o])
                for t in range(2):
                    bank = pb[(2 * o + t) % 4]
                    k.wait("tensor", k.sV, bank_free.get((o - 2, t)))
                    for c in range(HCN):
                        pv = mm(bank.ap(), wblk[o % 4][:, c * 128:(c + 1) * 128],
                                yb[c][:, t * 512:(t + 1) * 512], c == 0, c == HCN - 1,
                                inc=(c == HCN - 1))
                    mmv[(o, t)] = pv
                blk_free[o] = mmv[(o, 1)]
                h = o % 16
                if o < 32:
                    isq = o < 16
                    crep, srep = (cqrep, sqrep) if isq else (ckrep, skrep)
                    dstt = qsb if isq else ksb
                    sect = (0 if isq else KOFF) + h * (HD * TOK)
                    if o >= 1 and st_qk.get(o - 1):
                        k.wait("vector", st_qk[o - 1][1], st_qk[o - 1][0])
                    rv = None
                    for t in range(2):
                        bank = pb[(2 * o + t) % 4]
                        k.wait("vector", k.sP, mmv[(o, t)])
                        k.op("vector", k.sV, lambda e, b=bank, c_=crep, t_=t: e.tensor_mul(
                            re1, b.ap(), c_[:, t_ * 512:(t_ + 1) * 512]))
                        e2v = k.op("vector", k.sV, lambda e, b=bank, s_=srep, t_=t: e.tensor_mul(
                            re2, b.ap(), s_[:, t_ * 512:(t_ + 1) * 512]))
                        bank_free[(o, t)] = e2v
                        k.wait("sync", k.sV, e2v)
                        if rv_prev:
                            k.wait("sync", k.sV, rv_prev)
                        k.dma("sync", s_sw, re2s[0:64, :], re2[64:128, :])
                        sw1 = k.dma("sync", s_sw, re2s[64:128, :], re2[0:64, :])
                        k.wait("vector", s_sw, sw1)
                        k.wait("vector", k.sV, e2v)
                        if isq:
                            od0 = dstt.ap()[0:64, t * 512:(t + 1) * 512]
                            od1 = dstt.ap()[64:128, t * 512:(t + 1) * 512]
                        else:
                            rs = slice(t * 512 + 511, t * 512 - 1 if t > 0 else None, -1)
                            od0 = dstt.ap()[0:64, rs]
                            od1 = dstt.ap()[64:128, rs]
                        k.op("vector", None, lambda e, d=od0: e.tensor_sub(d, re1[0:64, :], re2s[0:64, :]))
                        rv = k.op("vector", k.sV, lambda e, d=od1: e.tensor_add(
                            d, re1[64:128, :], re2s[64:128, :]))
                        rv_prev = rv
                    k.wait("sync", k.sV, rv)
                    ssem = s_qs if isq else s_ks
                    st_qk[o] = (k.dma("sync", ssem, bass.AP(kvl.tensor, sect, [[TOK, 128], [1, TOK]]),
                                      dstt[:]), ssem)
                else:
                    ev = None
                    for t in range(2):
                        bank = pb[(2 * o + t) % 4]
                        k.wait("vector", k.sP, mmv[(o, t)])
                        if o >= 33 and t == 0:
                            k.wait("vector", k.sP, vtr.get((o - 1, 7)))
                        rs = slice(t * 512 + 511, t * 512 - 1 if t > 0 else None, -1)
                        ev = k.op("vector", k.sV, lambda e, b=bank, r_=rs: e.tensor_copy(
                            vsb.ap()[:, r_], b.ap()))
                        bank_free[(o, t)] = ev
                    k.wait("tensor", k.sV, ev)
                    sect = VOFF + h * (TOK * HD)
                    for tc in range(8):
                        k.wait("tensor", k.sA, vtr_ev.get((o, tc - 1)) if o >= 32 else None)
                        if tc == 0 and o >= 33:
                            k.wait("tensor", k.sA, vtr_ev.get((o - 1, 7)))
                        pv = k.op("tensor", k.sP, lambda e, tc_=tc: e.transpose(
                            pt0.ap(), vsb[:, tc_ * 128:(tc_ + 1) * 128], identb[:]))
                        vtr[(o, tc)] = pv
                        k.wait("scalar", k.sP, pv)
                        pvst = vst.get((o, tc - 2)) or vst.get((o - 1, tc + 6))
                        if pvst:
                            k.wait("scalar", s_vst[tc % 2], pvst)
                        ve = evac(vt[tc % 2][:], pt0.ap())
                        vtr_ev[(o, tc)] = ve
                        k.wait("sync", k.sA, ve)
                        vst[(o, tc)] = k.dma("sync", s_vst[tc % 2], bass.AP(
                            kvl.tensor, sect + tc * 128 * 128, [[128, 128], [1, 128]]),
                            vt[tc % 2][:])
                    st_v[o] = vst[(o, 7)]

        def ph_gather_kv():
            k.barrier()
            k.op("gpsimd", k.sC, lambda e: e.collective_compute(
                "AllGather", ALU.bypass, [list(range(NC))],
                [bass.AP(kvl.tensor, 0, [[1024, SEC // 1024], [1, 1024]]).opt()],
                [bass.AP(kvgt, 0, [[1024, NC * SEC // 1024], [1, 1024]]).opt()]))
            k.barrier()
            for lh in range(LH):
                k.dma_dyn("gpsimd", s_kvh[lh],
                          bass.AP(kvh.tensor, lh * 3145728, [[1, 3145728]]),
                          kvgt, lh * HD * TOK, "kv",
                          ((SEC, 8), (QSECT, 3), (1024, 128), (1, 1024)))

        def ph_gather_ctx():
            k.barrier()
            k.op("gpsimd", k.sC, lambda e: e.collective_compute(
                "AllGather", ALU.bypass, [list(range(NC))],
                [bass.AP(ctxl.tensor, 0, [[1024, CTXSECT // 1024], [1, 1024]]).opt()],
                [bass.AP(ctxgt, 0, [[1024, NC * CTXSECT // 1024], [1, 1024]]).opt()]))
            k.barrier()
            k.dma_dyn("gpsimd", s_cxh, bass.AP(ctxh.tensor, 0, [[1024, 2048], [1, 512]]),
                      ctxgt, 0, "clo", ((S, 2048), (1, 512)))
            k.dma_dyn("gpsimd", s_cxh, bass.AP(ctxh.tensor, 512, [[1024, 2048], [1, 512]]),
                      ctxgt, 0, "chi", ((S, 2048), (1, 512)))
            k.wait("sync", s_cxh, 32)
            k.barrier()

        def ph_flash():
            GC = 0
            KB_ = 0
            qkj = {}
            pvden = {}
            expd = {}
            muld = {}
            stored = {}
            recipd = {}
            bcastd = {}
            bevacd = {}
            pair_end = {}
            blk_qk = {}
            kblk_ld = {}
            vld_d = {}
            JD = 0
            for lh in range(LH):
                tb = k.dma("sync", s_tb[lh], cmb[lh][:],
                           bass.AP(biasd.tensor, lh * 1408 + 127, [[1, 128], [1, 1024]]))
                k.wait("vector", s_tb[lh], tb)
                k.wait("sync", s_kvh[lh], 16)
                k.wait("gpsimd", s_kvh[lh], 16)
                for qt in range(QTN):
                    p = lh * QTN + qt
                    nm = 4 * qt + 4
                    nj = nm // 2
                    rq = qt if qt <= 7 else 15 - qt
                    halfq = 0 if qt <= 7 else 1
                    if p >= 2:
                        k.wait("sync", k.sP, pair_end[p - 2])
                    qtl_ld = k.dma("sync", s_qtl[p % 2], qtl[p % 2],
                                   bass.AP(kvh.tensor,
                                           lh * 3145728 + rq * 3 * 131072 + halfq * 512,
                                           [[1024, 128], [1, 512]]))
                    ctx_bank = (pC0 if p % 2 == 0 else pC1)
                    scb = [pA, pB]
                    for j in range(nj):
                        for half in range(2):
                            m = 2 * j + half
                            J0 = 7680 - 512 * qt + 128 * m
                            b = 15 - J0 // 512
                            r = b if b <= 7 else 15 - b
                            hv = 0 if b <= 7 else 1
                            s0 = J0 % 512
                            if m % 4 == 0:
                                if KB_ >= 2:
                                    k.wait("sync", k.sP, blk_qk[KB_ - 2])
                                kblk_ld[KB_] = k.dma(
                                    "sync", s_kb[KB_ % 2], kblk[KB_ % 2],
                                    bass.AP(kvh.tensor,
                                            lh * 3145728 + (r * 3 + 1) * 131072 + hv * 512,
                                            [[1024, 128], [1, 512]]))
                                curKB = KB_
                                KB_ += 1
                            if GC >= 8:
                                k.wait("gpsimd", k.sP, pvden[(GC - 8) // 2])
                            vld_d[GC] = k.dma("gpsimd", s_v[GC % 8], vch[GC % 8],
                                        bass.AP(kvh.tensor,
                                                lh * 3145728 + (r * 3 + 2) * 131072
                                                + (hv * 512 + s0) * 128,
                                                [[128, 128], [1, 128]]))
                            if m == 0:
                                k.wait("tensor", s_qtl[p % 2], qtl_ld)
                                k.wait("tensor", k.sV, muld.get(p - 2))
                            if m % 4 == 0:
                                k.wait("tensor", s_kb[curKB % 2], kblk_ld[curKB])
                            if half == 0:
                                k.wait("tensor", k.sA, expd.get(JD - 2))
                            sc = scb[JD % 2].ap()[:, half * 512:(half + 1) * 512]
                            v_ = mm(sc, kblk[curKB % 2][:, (m % 4) * 128:(m % 4 + 1) * 128],
                                    qtl[p % 2], True, True, inc=(half == 1))
                            if m % 4 == 3 or m == nm - 1:
                                blk_qk[curKB] = k.sP.n
                            GC += 1
                        qkj[JD] = v_
                        near0, near1 = (2 * j) <= 4, (2 * j + 1) <= 4
                        if near0 or near1:
                            k.wait("vector", k.sP, qkj[JD])
                            k.wait("vector", k.sA, expd.get(JD - 2))
                            for half in range(2):
                                m = 2 * j + half
                                if m <= 4:
                                    k.op("vector", k.sV, lambda e, m_=m, lh_=lh, jj=JD, h_=half:
                                         e.tensor_add(tmpf[jj % 2][:, h_ * 512:(h_ + 1) * 512],
                                                      scb[jj % 2].ap()[:, h_ * 512:(h_ + 1) * 512],
                                                      cmb[lh_][:, m_ * 128:m_ * 128 + 512]))
                            av = k.sV.n
                            k.wait("scalar", k.sV, av)
                            k.wait("scalar", k.sP, pvden.get(JD - 2))
                            if near0 and near1:
                                expd[JD] = k.op("scalar", k.sA, lambda e, jj=JD: e.activation(
                                    psb[jj % 2], tmpf[jj % 2][:, 0:1024], AF.Exp))
                            else:
                                k.op("scalar", k.sA, lambda e, jj=JD: e.activation(
                                    psb[jj % 2][:, 0:512], tmpf[jj % 2][:, 0:512], AF.Exp))
                                expd[JD] = k.op("scalar", k.sA, lambda e, jj=JD, lh_=lh: e.activation(
                                    psb[jj % 2][:, 512:1024],
                                    scb[jj % 2].ap()[:, 512:1024], AF.Exp,
                                    bias=farb.ap()[:, lh_:lh_ + 1]))
                        else:
                            k.wait("scalar", k.sP, qkj[JD])
                            k.wait("scalar", k.sP, pvden.get(JD - 2))
                            expd[JD] = k.op("scalar", k.sA, lambda e, jj=JD, lh_=lh: e.activation(
                                psb[jj % 2], scb[jj % 2].ap()[:, 0:1024], AF.Exp,
                                bias=farb.ap()[:, lh_:lh_ + 1]))
                        k.wait("tensor", k.sA, expd[JD])
                        if j == 0:
                            k.wait("tensor", k.sA, bevacd.get(p - 1))
                            k.wait("tensor", k.sV, recipd.get(p - 1))
                        for half in range(2):
                            m = 2 * j + half
                            gci = GC - 2 + half
                            k.wait("tensor", s_v[gci % 8], vld_d[gci])
                            mm(ctx_bank.ap(), vch[gci % 8], psb[JD % 2][:, half * 512:(half + 1) * 512],
                               m == 0, m == nm - 1)
                            v_ = mm(pD.ap()[0:1, :], onescolb[:],
                                    psb[JD % 2][:, half * 512:(half + 1) * 512],
                                    m == 0, m == nm - 1, inc=(half == 1))
                        pvden[JD] = v_
                        JD += 1
                    denr = R_DR0 if p % 2 == 0 else R_DR1
                    drb = 0
                    k.wait("vector", k.sP, pvden[JD - 1])
                    k.wait("vector", k.sP, bcastd.get(p - 1))
                    recipd[p] = k.op("vector", k.sV, lambda e, d=denr:
                                     e.reciprocal(d[:, 0:512], pD.ap()[0:1, :]))
                    k.wait("tensor", k.sV, recipd[p])
                    bcastd[p] = mm(pD.ap(), onesr(drb), denr[:, 0:512], True, True, inc=True)
                    k.wait("scalar", k.sP, bcastd[p])
                    k.wait("scalar", k.sV, muld.get(p - 2))
                    bevacd[p] = evac(bcsb[p % 2], pD.ap())
                    k.wait("vector", k.sA, bevacd[p])
                    if stored.get(p - 2):
                        k.wait("vector", s_cst[p % 2], stored[p - 2])
                    muld[p] = k.op("vector", k.sV, lambda e, p_=p, cb=ctx_bank: e.tensor_mul(
                        ctxsb[p_ % 2][:], cb.ap(), bcsb[p_ % 2]))
                    k.wait("sync", k.sV, muld[p])
                    stored[p] = k.dma("sync", s_cst[p % 2], bass.AP(
                        ctxl.tensor, lh * HD * S + 512 * qt, [[S, 128], [1, 512]]),
                        ctxsb[p % 2][:])
                    pair_end[p] = bcastd[p]

        def ph_dense():
            wlv = {}
            mm_done = {}
            add_d = {}
            ld = {}
            for i in range(HCN + 1):
                if i < HCN:
                    if i >= 4:
                        k.wait("gpsimd", k.sP, mm_done[i - 4])
                    wlv[i] = k.dma("gpsimd", s_wb[i % 4], wblk[i % 4][:], wdense[i])
                    ld[i] = k.dma("sync", s_dld, yb[i][:],
                                  bass.AP(ctxh.tensor, i * 131072, [[1024, 128], [1, 1024]]))
                if i >= 1:
                    hc2 = i - 1
                    k.wait("tensor", s_wb[hc2 % 4], wlv[hc2])
                    if hc2 == 0:
                        k.wait("tensor", s_dld, 16 * 16)
                    for t in range(2):
                        bank = pb[(2 * hc2 + t) % 4]
                        k.wait("tensor", k.sV, add_d.get((hc2 - 2, t)))
                        for c in range(HCN):
                            pv = mm(bank.ap(), wblk[hc2 % 4][:, c * 128:(c + 1) * 128],
                                    yb[c][:, t * 512:(t + 1) * 512], c == 0, c == HCN - 1,
                                    inc=(c == HCN - 1))
                        mm_done[hc2] = pv
                        k.wait("vector", k.sP, pv)
                        add_d[(hc2, t)] = k.op("vector", k.sV,
                            lambda e, h_=hc2, t_=t, b=bank: e.tensor_add(
                                xres[h_][:, t_ * 512:(t_ + 1) * 512],
                                xres[h_][:, t_ * 512:(t_ + 1) * 512], b.ap()))

        def ph_out():
            tr_done = {}
            st = {}
            for tc in range(8):
                for hc in range(HCN):
                    src = xres[hc].ap()[:, tc * 128:(tc + 1) * 128]
                    bank = pb[hc % 2].ap()[:, 0:128]
                    prev = tr_done.get((tc, hc - 2)) or tr_done.get((tc - 1, hc + 14))
                    k.wait("tensor", k.sA, prev)
                    if hc == 0 and tc >= 1:
                        k.wait("tensor", s_out[(tc - 1) % 2], st[(tc - 1, 1)])
                    pv = k.op("tensor", k.sP,
                              lambda e, b=bank, s=src: e.transpose(b, s, identf[:]))
                    k.wait("scalar", k.sP, pv)
                    tr_done[(tc, hc)] = evac(
                        outh[hc // 8][:, (hc % 8) * 128:(hc % 8 + 1) * 128], bank)
                k.wait("sync", k.sA, tr_done[(tc, HCN - 1)])
                st[(tc, 0)] = k.dma("sync", s_out[tc % 2], out[tc * 128:(tc + 1) * 128, 0:1024], outh[0][:])
                st[(tc, 1)] = k.dma("sync", s_out[tc % 2], out[tc * 128:(tc + 1) * 128, 1024:2048], outh[1][:])
            k.wait("sync", s_out[0], s_out[0].n)
            k.wait("sync", s_out[1], s_out[1].n)

        def dump_fm(name, tiles):
            if name not in dbg:
                return
            k.barrier()
            for hc in range(HCN):
                vv = k.op("vector", k.sV, lambda e, h_=hc: e.tensor_copy(cmb[0][:], tiles[h_][:]))
                k.wait("sync", k.sV, vv)
                sv = k.dma("sync", s_dbg, dbg[name][hc * 128:(hc + 1) * 128, :], cmb[0][:])
                k.wait("vector", s_dbg, sv)
            k.barrier()

        def dump_dram(name, tensor, nelem):
            if name not in dbg:
                return
            k.barrier()
            n = min(nelem, H * TOK)
            dv = k.dma("sync", s_dbg, bass.AP(dbg[name].tensor, 0, [[1, n]]), bass.AP(tensor, 0, [[1, n]]))
            k.wait("sync", s_dbg, dv)
            k.barrier()

        def snap(idx, tile, width=128):
            if "gdbg" not in dbg:
                return
            k.wait("sync", k.sV, k.sV.n)
            k.wait("sync", k.sA, k.sA.n)
            k.wait("sync", k.sP, k.sP.n)
            sv = k.dma("sync", s_dbg, dbg["gdbg"][idx * 128:idx * 128 + 128, 0:width], tile)
            k.wait("vector", s_dbg, sv)
            k.wait("scalar", s_dbg, sv)
            k.wait("tensor", s_dbg, sv)

        k.snap = snap

        order = ["prep", "load", "ln1", "router1", "experts1", "shared1",
                 "ln2", "qkv", "gkv", "flash", "gctx", "dense",
                 "ln3", "router2", "experts2", "shared2", "out"]
        stop = order.index(upto)

        def phase(nm, fn):
            if order.index(nm) <= stop:
                res = fn()
                k.barrier()
                return res
            return None

        phase("prep", ph_prep)
        phase("load", ph_load_x)
        if "xres0" in dbg:
            k.barrier()
            dump_fm("xres0", xres)
        y1 = phase("ln1", ph_ln)
        if y1 is not None:
            dump_fm("ln1", yb)
        if y1 is not None and "lndbg" in dbg:
            k.barrier()
            d = dbg["lndbg"]
            k.dma("sync", s_dbg, d[0:128, :], Asb[:])
            k.dma("sync", s_dbg, d[128:256, :], Bsb[:])
            k.dma("sync", s_dbg, d[256:257, :], R_M)
            k.dma("sync", s_dbg, d[257:258, :], R_V)
            k.dma("sync", s_dbg, d[258:259, :], R_A)
            k.dma("sync", s_dbg, d[259:260, :], R_B)
            sv = k.dma("sync", s_dbg, d[260:261, :], R_T)
            k.wait("sync", s_dbg, sv)
            k.barrier()
        cs1 = phase("router1", lambda: ph_router(y1))
        if cs1 is not None:
            dump_dram("comb1", combd.tensor, E * TOK)
        phase("experts1", lambda: ph_experts(cs1, y1))
        phase("shared1", lambda: ph_shared(y1))
        dump_fm("x1", xres)
        y2 = phase("ln2", ph_ln)
        if y2 is not None:
            dump_fm("ln2", yb)
        phase("qkv", lambda: ph_qkv(y2))
        if order.index("qkv") <= stop:
            dump_dram("kvl", kvl.tensor, SEC)
        phase("gkv", ph_gather_kv)
        phase("flash", ph_flash)
        if order.index("flash") <= stop:
            dump_dram("ctxl", ctxl.tensor, CTXSECT)
        phase("gctx", ph_gather_ctx)
        phase("dense", ph_dense)
        dump_fm("x2", xres)
        y3 = phase("ln3", ph_ln)
        cs2 = phase("router2", lambda: ph_router(y3))
        phase("experts2", lambda: ph_experts(cs2, y3))
        phase("shared2", lambda: ph_shared(y3))
        dump_fm("x3", xres)
        phase("out", ph_out)

        block = es.enter_context(nc.Block())
        k.run(block)
    return nc



def _retile(inputs):
    f = np.float32
    gp = np.asarray(inputs["L0_ffn_gate_proj"], f)
    up = np.asarray(inputs["L0_ffn_up_proj"], f)
    dp = np.asarray(inputs["L0_ffn_down_proj"], f)
    sg = np.asarray(inputs["L0_ffn_shared_gate_w"], f)
    su = np.asarray(inputs["L0_ffn_shared_up_w"], f)
    sd = np.asarray(inputs["L0_ffn_shared_down_w"], f)
    qkv = np.asarray(inputs["L0_attn_qkv_w"], f)
    dw = np.asarray(inputs["L0_attn_dense_w"], f)
    gwv = np.asarray(inputs["L0_ffn_gate_weight"], f)
    d = {}
    d["wg"] = np.ascontiguousarray(gp.reshape(E, ICN, 128, HCN, 128).transpose(0, 1, 4, 3, 2).reshape(E, ICN, 128, 2048))
    d["wu"] = np.ascontiguousarray(up.reshape(E, ICN, 128, HCN, 128).transpose(0, 1, 4, 3, 2).reshape(E, ICN, 128, 2048))
    d["wd"] = np.ascontiguousarray(dp.reshape(E, HCN, 128, ICN, 128).transpose(0, 1, 4, 3, 2).reshape(E, HCN, 128, 1408))
    d["wsg"] = np.ascontiguousarray(sg.reshape(SCN, 128, HCN, 128).transpose(0, 3, 2, 1).reshape(SCN, 128, 2048))
    d["wsu"] = np.ascontiguousarray(su.reshape(SCN, 128, HCN, 128).transpose(0, 3, 2, 1).reshape(SCN, 128, 2048))
    d["wsd"] = np.ascontiguousarray(sd.reshape(HCN, 128, SCN, 128).transpose(0, 3, 2, 1).reshape(HCN, 128, 2816))
    d["wqkv"] = np.ascontiguousarray(qkv.reshape(HCN, 128, 48, 128).transpose(2, 1, 0, 3).reshape(48, 128, 2048))
    d["wdense"] = np.ascontiguousarray(dw.reshape(HCN, 128, HCN, 128).transpose(2, 1, 0, 3).reshape(HCN, 128, 2048))
    d["gwt"] = np.ascontiguousarray(gwv.T.reshape(HCN, 128, 16))
    return d


def _bucket_of_delta(n):
    n = np.asarray(n)
    nf = np.maximum(n, 1).astype(np.float64)
    large = (np.log(nf / 16) / np.log(8.0) * 16).astype(np.int64) + 16
    large = np.minimum(large, 31)
    return np.where(n < 16, n, large)


def _consts():
    oh = np.zeros((32, 768), np.float32)
    b = _bucket_of_delta(np.arange(768))
    oh[b, np.arange(768)] = 1.0
    e31v = np.zeros((32, 1), np.float32)
    e31v[31, 0] = 1.0
    identf = np.eye(128, dtype=np.float32)
    nrampv = np.ascontiguousarray(
        np.tile(-(np.arange(16, dtype=np.float32) + 1.0) * 1e-30, 8)[None, :])
    return oh, e31v, identf, nrampv


def _rope_tables(tok_idx):
    inv = 1.0 / 10000.0 ** (np.arange(0, HD, 2, dtype=np.float32) / HD)
    ang = tok_idx[:, None].astype(np.float32) * inv[None, :]
    cos = np.cos(ang).astype(np.float32).T
    sin = np.sin(ang).astype(np.float32).T
    s = np.float32(1.0 / np.sqrt(HD))
    rep = lambda a: np.ascontiguousarray(np.concatenate([a, a], axis=0))
    return rep(cos * s), rep(sin * s), rep(cos), rep(sin)


def prep_in_maps(inputs):
    x = np.ascontiguousarray(np.asarray(inputs["L0_attn_x"], np.float32).reshape(S, H))
    w = _retile(inputs)
    oh, e31v, identf, nrampv = _consts()
    table = np.ascontiguousarray(np.asarray(inputs["L0_attn_rel_attn_table"], np.float32))
    escb_rep = np.ascontiguousarray(
        np.tile(np.asarray(inputs["L0_ffn_e_score_correction_bias"], np.float32), 8)[None, :])
    in_maps = []
    for c in range(NC):
        rows_idx = np.concatenate([np.arange(512 * c, 512 * (c + 1)),
                                   np.arange(512 * (15 - c), 512 * (16 - c))])
        xs = np.ascontiguousarray(x[rows_idx])
        cq, sq, ck, sk = _rope_tables(rows_idx)
        perm = [2 * c, 2 * c + 1] + [hh for hh in range(16) if hh not in (2 * c, 2 * c + 1)]
        tp = np.ascontiguousarray(table[:, perm])
        m = {"xin": xs, "cqr": cq, "sqr": sq, "ckr": ck, "skr": sk, "tablep": tp,
             "escb": escb_rep, "nramp": nrampv, "onehot": oh, "e31": e31v, "identf": identf}
        m.update(w)
        in_maps.append(m)
    return in_maps


def kernel(**inputs):
    in_maps = prep_in_maps(inputs)
    nc = build_program()
    res = run_bass_kernel_spmd(nc, in_maps, core_ids=list(range(NC)))
    full = np.empty((S, H), np.float32)
    for c in range(NC):
        o = np.asarray(res.results[c]["out"])
        full[512 * c:512 * (c + 1)] = o[:512]
        full[512 * (15 - c):512 * (16 - c)] = o[512:]
    return full.reshape(1, S, H)


# version 03
# speedup vs baseline: 1.0775x; 1.0236x over previous
import os
import numpy as np
import ml_dtypes
import concourse.bass as bass
import concourse.mybir as mybir
from concourse.bass_utils import run_bass_kernel_spmd
from concourse.alu_op_type import AluOpType as ALU
from contextlib import ExitStack

F32 = mybir.dt.float32
BF16 = mybir.dt.bfloat16
AF = mybir.ActivationFunctionType

NC = 8
H = 2048
HCN = 16
TOK = 1024
S = 8192
NH = 16
HD = 128
E = 16
ICN = 11
SCN = 22
LH = 2
QTN = 16
CSTAB = 12.0
NEGB = -30000.0

QSECT = NH * HD * TOK
SEC = 3 * QSECT
KOFF = QSECT
VOFF = 2 * QSECT
CTXSECT = LH * HD * S


class Ctr:
    __slots__ = ("h", "n")
    def __init__(self):
        self.h = None
        self.n = 0
    def inc(self, k=1):
        self.n += k
        return self.n


class KB:
    ENGS = ("tensor", "scalar", "vector", "gpsimd", "sync")

    def __init__(self, nc):
        self.nc = nc
        self.q = {e: [] for e in self.ENGS}
        self.sP = Ctr(); self.sA = Ctr(); self.sV = Ctr(); self.sC = Ctr()
        self.all_ctrs = [self.sP, self.sA, self.sV, self.sC]
        self.rt = {}

    def em(self, eng, fn):
        self.q[eng].append(fn)

    def wait(self, eng, ctr, v):
        if v:
            self.em(eng, lambda e, c=ctr, vv=v: e.wait_ge(c.h, vv))

    def dma(self, eng, ctr, out, in_, **kw):
        v = ctr.inc(16)
        self.em(eng, lambda e, o=out, i=in_, c=ctr, k=kw:
                e.dma_start(out=o, in_=i, **k).then_inc(c.h, 16))
        return v

    def dma_dyn(self, eng, ctr, out, tensor, off_static, dyn_key, apdims, **kw):
        v = ctr.inc(16)
        def fn(e, o=out, t=tensor, os_=off_static, dk=dyn_key, ad=apdims, c=ctr, en=eng, k=kw):
            src = bass.AP(t, self.rt[en][dk] + os_, [list(d) for d in ad])
            e.dma_start(out=o, in_=src, **k).then_inc(c.h, 16)
        self.em(eng, fn)
        return v

    def op(self, eng, ctr, fn):
        if eng == "vector":
            c = self.sV
            v = c.inc(1)
            self.em(eng, lambda e, f=fn, cc=c: f(e).then_inc(cc.h, 1))
            self.em(eng, lambda e, cc=c, vv=v: e.wait_ge(cc.h, vv))
            return v
        if ctr is None:
            self.em(eng, fn)
            return None
        v = ctr.inc(1)
        self.em(eng, lambda e, f=fn, c=ctr: f(e).then_inc(c.h, 1))
        return v

    def barrier(self):
        vals = [(c, c.n) for c in self.all_ctrs if c.n]
        for eng in self.ENGS:
            for c, v in vals:
                self.wait(eng, c, v)

    def init_rt(self):
        for eng in ("sync", "gpsimd"):
            def fn(e, en=eng):
                pid = e.partition_id()
                self.rt[en] = {
                    "kv": pid * 262144,
                    "kvh": pid * 131072,
                    "clo": pid * 512,
                    "chi": pid * (-512) + 7680,
                }
            self.em(eng, fn)

    def run(self, block):
        for eng in self.ENGS:
            thunks = self.q[eng]
            def make(ts):
                def body(e):
                    for t in ts:
                        t(e)
                return body
            getattr(block, eng)(make(thunks))


def build_program(dump=(), upto="out", n_exp=E):
    nc = bass.Bass("TRN2", target_bir_lowering=False, debug=False, num_devices=NC)

    xin = nc.dram_tensor("xin", [TOK, H], F32, kind="ExternalInput").ap()
    out = nc.dram_tensor("out", [TOK, H], F32, kind="ExternalOutput").ap()
    wqkv = nc.dram_tensor("wqkv", [48, 128, 2048], F32, kind="ExternalInput").ap()
    wdense = nc.dram_tensor("wdense", [16, 128, 2048], F32, kind="ExternalInput").ap()
    wg = nc.dram_tensor("wg", [E, ICN, 128, 2048], F32, kind="ExternalInput").ap()
    wu = nc.dram_tensor("wu", [E, ICN, 128, 2048], F32, kind="ExternalInput").ap()
    wd = nc.dram_tensor("wd", [E, HCN, 128, 1408], F32, kind="ExternalInput").ap()
    wsg = nc.dram_tensor("wsg", [SCN, 128, 2048], F32, kind="ExternalInput").ap()
    wsu = nc.dram_tensor("wsu", [SCN, 128, 2048], F32, kind="ExternalInput").ap()
    wsd = nc.dram_tensor("wsd", [HCN, 128, 2816], F32, kind="ExternalInput").ap()
    gwt = nc.dram_tensor("gwt", [HCN, 128, 16], F32, kind="ExternalInput").ap()
    escb = nc.dram_tensor("escb", [1, 128], F32, kind="ExternalInput").ap()
    nramp = nc.dram_tensor("nramp", [1, 128], F32, kind="ExternalInput").ap()
    tablep = nc.dram_tensor("tablep", [32, 16], F32, kind="ExternalInput").ap()
    onehot = nc.dram_tensor("onehot", [32, 768], F32, kind="ExternalInput").ap()
    e31 = nc.dram_tensor("e31", [32, 1], F32, kind="ExternalInput").ap()
    identf_d = nc.dram_tensor("identf", [128, 128], F32, kind="ExternalInput").ap()
    cqr = nc.dram_tensor("cqr", [128, 1024], F32, kind="ExternalInput").ap()
    sqr = nc.dram_tensor("sqr", [128, 1024], F32, kind="ExternalInput").ap()
    ckr = nc.dram_tensor("ckr", [128, 1024], F32, kind="ExternalInput").ap()
    skr = nc.dram_tensor("skr", [128, 1024], F32, kind="ExternalInput").ap()

    kvl = nc.dram_tensor("kvl", [SEC], BF16, kind="Internal").ap()
    kvgt = [nc.dram_tensor(f"kvg{i}", [NC * SEC // 2], BF16, kind="Internal") for i in range(2)]
    ctxl = nc.dram_tensor("ctxl", [CTXSECT], BF16, kind="Internal").ap()
    ctxgt = [nc.dram_tensor(f"ctxg{i}", [NC * CTXSECT // 2], BF16, kind="Internal") for i in range(2)]
    biasd = nc.dram_tensor("biasd", [LH * 1408], F32, kind="Internal").ap()
    kvh = nc.dram_tensor("kvh", [LH * 8 * 3 * 128 * 1024], BF16, kind="Internal").ap()
    ctxh = nc.dram_tensor("ctxh", [2048 * 1024], BF16, kind="Internal").ap()
    combd = nc.dram_tensor("combd", [E, TOK], F32, kind="Internal").ap()

    dbg = {}
    for name in dump:
        dbg[name] = nc.dram_tensor("dbg_" + name, [H, TOK], F32, kind="ExternalOutput").ap()

    es = ExitStack()
    with es:
        sb = lambda name, shape, dt: es.enter_context(nc.sbuf_tensor(name, shape, dt))
        xres = [sb(f"xres{i}", [128, 1024], F32) for i in range(HCN)]
        yb = [sb(f"yb{i}", [128, 1024], BF16) for i in range(HCN)]
        inter = sb("inter", [128, 11264], BF16)
        wblk = [sb(f"wblk{i}", [128, 2048], BF16) for i in range(4)]
        cmb = [sb(f"cmb{i}", [128, 1024], F32) for i in range(2)]
        gsb = [sb(f"gsb{i}", [128, 512], F32) for i in range(2)]
        Asb = sb("Asb", [128, 1024], F32)
        Bsb = sb("Bsb", [128, 1024], F32)
        sq2 = [sb(f"sq{i}", [128, 1024], BF16) for i in range(2)]
        identf = sb("identf_sb", [128, 128], F32)
        identb = sb("identb_sb", [128, 128], BF16)
        gwb = sb("gwb", [128, 256], BF16)
        tps = sb("tps", [32, 16], F32)
        ohsb = sb("ohsb", [32, 768], F32)
        e31sb = sb("e31sb", [32, 1], F32)
        biasrow = sb("biasrow", [2, 1408], F32)
        onescolb = sb("onescolb", [128, 1], BF16)
        onesrf = sb("onesrf", [128, 128], F32)
        onesr = lambda b: onesrf.ap()[b:b + 1, 0:128]
        cstC = sb("cstC", [128, 1], F32)
        cstEps = sb("cstEps", [128, 1], F32)
        rows = [sb(f"row{i}", [1, 1024], F32) for i in range(5)]
        R_M, R_V, R_A, R_B, R_T = [r.ap() for r in rows]
        R_DR0, R_DR1 = R_M, R_V
        R_CR0, R_CR1 = R_M, R_V
        farb = sb("farb", [128, 2], F32)
        bbrep = sb("bbrep", [128, 128], F32)
        nrrep = sb("nrrep", [128, 128], F32)
        ga = sb("ga", [128, 128], F32)
        gb = sb("gb", [128, 128], F32)
        gc = sb("gc", [128, 128], F32)
        gd = sb("gd", [128, 128], F32)
        ge = sb("ge", [128, 128], F32)
        gsm = sb("gsm", [128, 32], F32)
        r16 = sb("r16", [16, 1024], F32)
        qsb = sb("qsb", [128, 1024], BF16)
        ksb = sb("ksb", [128, 1024], BF16)
        vsb = sb("vsb", [128, 1024], BF16)
        vt = [sb(f"vt{i}", [128, 128], BF16) for i in range(2)]
        ctxsb = [sb(f"ctxsb{i}", [128, 512], BF16) for i in range(2)]
        outh = [Asb, Bsb]

        w0b = wblk[0].ap()
        qtl = [w0b[:, 0:512], w0b[:, 512:1024]]
        kblk = [w0b[:, 1024:1536], w0b[:, 1536:2048]]
        w1b = wblk[1].ap()
        vch = [w1b[:, i * 128:(i + 1) * 128] for i in range(8)]
        tmpf = [wblk[2].ap().bitcast(F32), wblk[3].ap().bitcast(F32)]
        interf = inter.ap().bitcast(F32)
        cqrep = interf[:, 0:1024]
        sqrep = interf[:, 1024:2048]
        ckrep = interf[:, 2048:3072]
        skrep = interf[:, 3072:4096]
        re1 = interf[:, 4096:4608]
        re2 = interf[:, 4608:5120]
        re2s = interf[:, 5120:5632]
        interb = inter.ap()
        psb = [interb[:, 0:1024], interb[:, 1024:2048]]
        bcsb = [inter.ap().bitcast(F32)[:, 1024:1536], inter.ap().bitcast(F32)[:, 1536:2048]]
        def ichunk(kk, t):
            return interb[:, kk * 1024 + t * 512: kk * 1024 + (t + 1) * 512]

        pA = es.enter_context(nc.psum_tensor("pA", [128, 1024], F32))
        pB = es.enter_context(nc.psum_tensor("pB", [128, 1024], F32))
        pC0 = es.enter_context(nc.psum_tensor("pC0", [128, 512], F32))
        pC1 = es.enter_context(nc.psum_tensor("pC1", [128, 512], F32))
        pD = es.enter_context(nc.psum_tensor("pD", [128, 512], F32))
        pt0 = es.enter_context(nc.psum_tensor("pt0", [128, 128], BF16))

        class Bank:
            def __init__(self, a):
                self._a = a
            def ap(self):
                return self._a
        pb = [Bank(pA.ap()[:, 0:512]), Bank(pA.ap()[:, 512:1024]),
              Bank(pB.ap()[:, 0:512]), Bank(pB.ap()[:, 512:1024]),
              Bank(pC0.ap()), Bank(pC1.ap()), Bank(pD.ap())]

        k = KB(nc)
        for nm in ("sP", "sA", "sV", "sC"):
            getattr(k, nm).h = es.enter_context(nc.semaphore(nm))

        def newsem(nm):
            c = Ctr()
            c.h = es.enter_context(nc.semaphore(nm))
            k.all_ctrs.append(c)
            return c
        s_wb = [newsem(f"s_wb{i}") for i in range(4)]
        s_qtl = [newsem(f"s_qtl{i}") for i in range(2)]
        s_kb = [newsem(f"s_kb{i}") for i in range(2)]
        s_v = [newsem(f"s_v{i}") for i in range(8)]
        s_vb = [newsem(f"s_vb{i}") for i in range(2)]
        s_tb = [newsem(f"s_tb{i}") for i in range(2)]
        s_cr = [newsem(f"s_cr{i}") for i in range(2)]
        s_cst = [[newsem(f"s_cst{j}_{i}") for i in range(2)] for j in range(2)]
        s_vst = [[newsem(f"s_vst{j}_{i}") for i in range(2)] for j in range(2)]
        s_out = [newsem(f"s_out{i}") for i in range(2)]
        s_pc = newsem("s_pc")
        s_gw = newsem("s_gw")
        s_rt = newsem("s_rt")
        s_sw = newsem("s_sw")
        s_qs = [newsem(f"s_qs{i}") for i in range(2)]
        s_ks = [newsem(f"s_ks{i}") for i in range(2)]
        s_dld = newsem("s_dld")
        s_cmw = newsem("s_cmw")
        s_bw = newsem("s_bw")
        s_xl = newsem("s_xl")
        s_kvh = [newsem(f"s_kvh{i}") for i in range(2)]
        s_cxh = newsem("s_cxh")
        s_dbg = newsem("s_dbg")
        k.init_rt()

        def mm(psum_ap, lhsT, rhs, start, stop, inc=False):
            return k.op("tensor", k.sP if inc else None,
                        lambda e, p=psum_ap, l=lhsT, r=rhs, s1=start, s2=stop:
                        e.matmul(p, l, r, start=s1, stop=s2, skip_group_check=True))

        def evac(dst, src, func=AF.Copy, bias=0.0, scale=1.0):
            return k.op("scalar", k.sA,
                        lambda e, d=dst, s=src, f=func, b=bias, sc=scale:
                        e.activation(d, s, f, bias=b, scale=sc))

        def ph_load_x():
            tr_done = {}
            last_pv = {}
            for tc in range(8):
                if tc >= 1:
                    k.wait("sync", k.sP, last_pv[tc - 1])
                k.dma("sync", s_xl, outh[0][:], xin[tc * 128:(tc + 1) * 128, 0:1024])
                lv1 = k.dma("sync", s_xl, outh[1][:], xin[tc * 128:(tc + 1) * 128, 1024:2048])
                k.wait("tensor", s_xl, lv1)
                for hc in range(HCN):
                    src = outh[hc // 8][:, (hc % 8) * 128:(hc % 8 + 1) * 128]
                    bank = pb[hc % 2].ap()[:, 0:128]
                    prev = tr_done.get((tc, hc - 2)) or tr_done.get((tc - 1, hc + 14))
                    k.wait("tensor", k.sA, prev)
                    pv = k.op("tensor", k.sP,
                              lambda e, b=bank, s=src: e.transpose(b, s, identf[:]))
                    last_pv[tc] = pv
                    k.wait("scalar", k.sP, pv)
                    tr_done[(tc, hc)] = evac(xres[hc][:, tc * 128:(tc + 1) * 128], bank)

        def ph_prep():
            sv = {}
            k.dma("sync", s_pc, identf[:], identf_d)
            k.dma("sync", s_pc, tps[:], tablep)
            k.dma("sync", s_pc, ohsb[:], onehot)
            k.dma("sync", s_pc, e31sb[:], e31)
            k.dma("sync", s_pc, R_CR0[:, 0:128], escb)
            pcv = k.dma("sync", s_pc, R_CR1[:, 0:128], nramp)
            for key in ("identf", "tps", "oh", "e31", "escb", "nramp"):
                sv[key] = pcv
            gwv = None
            for c in range(HCN):
                gwv = k.dma("gpsimd", s_gw, gwb[:, c * 16:(c + 1) * 16], gwt[c])
            k.op("vector", None, lambda e: e.memset(onescolb[:], 1.0))
            k.op("vector", None, lambda e: e.memset(cstC[:], -CSTAB))
            k.op("vector", None, lambda e: e.memset(cstEps[:], 1e-5))
            brm = k.op("vector", k.sV, lambda e: e.memset(biasrow[:], NEGB))
            vmem = k.op("vector", k.sV, lambda e: e.memset(onesrf[:], 1.0))
            k.wait("vector", s_pc, sv["identf"])
            k.op("vector", k.sV, lambda e: e.tensor_copy(identb[:], identf[:]))
            k.wait("tensor", s_pc, sv["tps"])
            k.wait("tensor", s_pc, sv["oh"])
            p1 = mm(pb[0].ap()[0:2, :], tps.ap()[0:32, 0:2], ohsb.ap()[0:32, 0:512],
                    True, True, inc=True)
            p2 = mm(pb[1].ap()[0:2, 0:256], tps.ap()[0:32, 0:2], ohsb.ap()[0:32, 512:768],
                    True, True, inc=True)
            k.wait("scalar", k.sP, p1)
            k.wait("scalar", k.sV, brm)
            evac(biasrow.ap()[0:2, 638:1150], pb[0].ap()[0:2, :])
            k.wait("scalar", k.sP, p2)
            a2 = evac(biasrow.ap()[0:2, 1150:1406], pb[1].ap()[0:2, 0:256])
            k.wait("vector", k.sA, a2)
            a2v = k.op("vector", k.sV, lambda e: e.tensor_scalar(
                biasrow.ap()[0:2, 638:1406], biasrow.ap()[0:2, 638:1406], -CSTAB, None, ALU.add))
            k.wait("sync", k.sV, a2v)
            k.dma("sync", s_bw, bass.AP(biasd.tensor, 0, [[1408, 2], [1, 1408]]), biasrow[:])
            k.wait("tensor", s_pc, sv["e31"])
            p3 = mm(pb[2].ap()[0:1, 0:2], e31sb[:], tps.ap()[0:32, 0:2], True, True, inc=True)
            k.wait("scalar", k.sP, p3)
            a3 = evac(R_T[:, 0:2], pb[2].ap()[0:1, 0:2])
            k.wait("tensor", k.sA, a3)
            k.wait("tensor", k.sV, vmem)
            p4 = mm(pb[3].ap()[:, 0:2], onesr(0), R_T[:, 0:2], True, True, inc=True)
            k.wait("scalar", k.sP, p4)
            a4 = evac(farb[:], pb[3].ap()[:, 0:2])
            k.wait("vector", k.sA, a4)
            k.op("vector", k.sV, lambda e: e.tensor_scalar(
                farb[:], farb[:], -CSTAB, None, ALU.add))
            k.wait("tensor", s_pc, sv["escb"])
            p5 = mm(pb[4].ap()[:, 0:128], onesr(0), R_CR0[:, 0:128], True, True, inc=True)
            k.wait("scalar", k.sP, p5)
            evac(bbrep[:], pb[4].ap()[:, 0:128])
            k.wait("tensor", s_pc, sv["nramp"])
            p6 = mm(pb[5].ap()[:, 0:128], onesr(0), R_CR1[:, 0:128], True, True, inc=True)
            k.wait("scalar", k.sP, p6)
            evac(nrrep[:], pb[5].ap()[:, 0:128])
            k.wait("vector", s_gw, gwv)

        def ph_ln():
            cast_done = {}
            for c in range(HCN):
                cast_done[c] = k.op("vector", k.sV,
                                    lambda e, c_=c: e.tensor_copy(yb[c_][:], xres[c_][:]))
            sqmm = {}
            last = None
            for c in range(HCN):
                if c >= 2:
                    k.wait("scalar", k.sP, sqmm[c - 2])
                sq_done = k.op("scalar", k.sA,
                               lambda e, c_=c: e.activation(sq2[c_ % 2][:], xres[c_][:], AF.Square))
                k.wait("tensor", k.sV, cast_done[c])
                k.wait("tensor", k.sA, sq_done)
                for t in range(2):
                    mm(pb[t].ap()[0:1, :], onescolb[:], yb[c][:, t * 512:(t + 1) * 512],
                       c == 0, c == HCN - 1)
                for t in range(2):
                    last = mm(pb[2 + t].ap()[0:1, :], onescolb[:],
                              sq2[c % 2][:, t * 512:(t + 1) * 512],
                              c == 0, c == HCN - 1, inc=(t == 1))
                sqmm[c] = last
            k.wait("vector", k.sP, last)
            def VS(fn):
                v = k.op("vector", k.sV, fn)
                k.wait("vector", k.sV, v)
                return v
            for t in range(2):
                sl = slice(t * 512, (t + 1) * 512)
                VS(lambda e, t_=t, s_=sl: e.tensor_scalar(
                    R_M[:, s_], pb[t_].ap()[0:1, :], 1.0 / H, None, ALU.mult))
                VS(lambda e, t_=t, s_=sl: e.tensor_scalar(
                    R_V[:, s_], pb[2 + t_].ap()[0:1, :], 1.0 / H, None, ALU.mult))
            VS(lambda e: e.tensor_mul(R_T, R_M, R_M))
            vrow = VS(lambda e: e.tensor_sub(R_V, R_V, R_T))
            k.wait("scalar", k.sV, vrow)
            asq = k.op("scalar", k.sA,
                       lambda e: e.activation(R_T, R_V, AF.Sqrt, bias=cstEps.ap()[0:1, 0:1]))
            k.wait("vector", k.sA, asq)
            VS(lambda e: e.reciprocal(R_A, R_T))
            VS(lambda e: e.tensor_mul(R_B, R_M, R_A))
            brow = VS(lambda e: e.tensor_scalar(R_B, R_B, -1.0, None, ALU.mult))
            k.wait("tensor", k.sV, brow)
            for r, rb, dstt in ((R_A, 0, Asb), (R_B, 0, Bsb)):
                for t in range(2):
                    pv = mm(pb[6].ap(), onesr(rb), r[:, t * 512:(t + 1) * 512],
                            True, True, inc=True)
                    k.wait("scalar", k.sP, pv)
                    ev = evac(dstt[:, t * 512:(t + 1) * 512], pb[6].ap())
                    k.wait("tensor", k.sA, ev)
            k.wait("vector", k.sA, ev)
            y_done = {}
            for c in range(HCN):
                mv = k.op("vector", k.sV, lambda e, c_=c: e.tensor_mul(cmb[0][:], xres[c_][:], Asb[:]))
                k.wait("vector", k.sV, mv)
                y_done[c] = k.op("vector", k.sV,
                                 lambda e, c_=c: e.tensor_add(yb[c_][:], cmb[0][:], Bsb[:]))
            return y_done

        def ph_router(y_done):
            lg = None
            for c in range(HCN):
                k.wait("tensor", k.sV, y_done[c])
                for t in range(2):
                    lg = mm(pb[2 + t].ap()[0:16, :], gwb[:, c * 16:(c + 1) * 16],
                            yb[c][:, t * 512:(t + 1) * 512], c == 0, c == HCN - 1,
                            inc=(c == HCN - 1 and t == 1))
            k.wait("scalar", k.sP, lg)
            evac(r16.ap()[0:16, 0:512], pb[2].ap()[0:16, :])
            ev1 = evac(r16.ap()[0:16, 512:1024], pb[3].ap()[0:16, :])
            if "gdbg" in dbg:
                k.wait("sync", k.sA, ev1)
                s1 = k.dma("sync", s_dbg, dbg["gdbg"][11 * 128:11 * 128 + 16, :], r16[:])
                s2 = k.dma("sync", s_dbg, dbg["gdbg"][12 * 128:13 * 128, 0:128], bbrep[:])
                k.wait("tensor", s_dbg, s2)
            sig = {}
            k.wait("tensor", k.sA, ev1)
            for t8 in range(8):
                bank = pb[4 + (t8 % 2)].ap()[:, 0:16]
                if t8 >= 2:
                    k.wait("tensor", k.sA, sig[t8 - 2])
                pv = k.op("tensor", k.sP, lambda e, b=bank, t_=t8: e.transpose(
                    b, r16.ap()[0:16, t_ * 128:(t_ + 1) * 128], identf.ap()[0:16, 0:16]))
                k.wait("scalar", k.sP, pv)
                sig[t8] = k.op("scalar", k.sA, lambda e, b=bank, t_=t8: e.activation(
                    gb[:, t_ * 16:(t_ + 1) * 16], b, AF.Sigmoid))
            k.wait("vector", k.sA, sig[7])
            def V(fn):
                v = k.op("vector", k.sV, fn)
                k.wait("vector", k.sV, v)
                return v
            k.snap(0, gb[:])
            V(lambda e: e.tensor_add(gc[:], gb[:], bbrep[:]))
            k.snap(1, gc[:])
            g4 = lambda t, j: t.ap()[:, j:128:4]
            dsts = [gd.ap()[:, 0:32], gd.ap()[:, 32:64], gd.ap()[:, 64:96],
                    gd.ap()[:, 96:128], ga.ap()[:, 0:32], ga.ap()[:, 32:64]]
            for i, (a, b) in enumerate([(0, 1), (2, 3), (0, 2), (0, 3), (1, 2), (1, 3)]):
                V(lambda e, a_=a, b_=b, d=dsts[i]: e.tensor_tensor(d, g4(gc, a_), g4(gc, b_), ALU.add))
            V(lambda e: e.tensor_max(gd.ap()[:, 0:32], gd.ap()[:, 0:32], gd.ap()[:, 32:64]))
            V(lambda e: e.tensor_max(gd.ap()[:, 64:96], gd.ap()[:, 64:96], gd.ap()[:, 96:128]))
            V(lambda e: e.tensor_max(ga.ap()[:, 0:32], ga.ap()[:, 0:32], ga.ap()[:, 32:64]))
            V(lambda e: e.tensor_max(gd.ap()[:, 64:96], gd.ap()[:, 64:96], ga.ap()[:, 0:32]))
            V(lambda e: e.tensor_max(gd.ap()[:, 0:32], gd.ap()[:, 0:32], gd.ap()[:, 64:96]))
            k.snap(2, gd[:])
            t4 = lambda j: gd.ap()[:, j:32:4]
            V(lambda e: e.tensor_max(gsm.ap()[:, 0:8], t4(0), t4(1)))
            V(lambda e: e.tensor_tensor(gsm.ap()[:, 8:16], t4(0), t4(1), ALU.min))
            V(lambda e: e.tensor_max(gsm.ap()[:, 16:24], t4(2), t4(3)))
            V(lambda e: e.tensor_tensor(gsm.ap()[:, 24:32], t4(2), t4(3), ALU.min))
            V(lambda e: e.tensor_tensor(ga.ap()[:, 0:8], gsm.ap()[:, 0:8], gsm.ap()[:, 16:24], ALU.min))
            V(lambda e: e.tensor_max(ga.ap()[:, 8:16], gsm.ap()[:, 8:16], gsm.ap()[:, 24:32]))
            V(lambda e: e.tensor_max(gsm.ap()[:, 0:8], ga.ap()[:, 0:8], ga.ap()[:, 8:16]))
            k.snap(3, gsm[:], width=32)
            for g in range(4):
                V(lambda e, g_=g: e.tensor_tensor(gd.ap()[:, 32 + g_:64 + g_:4][:, 0:8],
                                                  t4(g_), gsm.ap()[:, 0:8], ALU.is_ge))
            k.snap(4, gd[:])
            for j in range(4):
                V(lambda e, j_=j: e.tensor_copy(ga.ap()[:, j_:128:4], gd.ap()[:, 32:64]))
            k.snap(5, ga[:])
            V(lambda e: e.tensor_scalar(ge[:], ga[:], -1.0, 1.0, ALU.mult, ALU.add))
            V(lambda e: e.tensor_mul(gd[:], gc[:], ga[:]))
            V(lambda e: e.tensor_mul(ge[:], ge[:], nrrep[:]))
            V(lambda e: e.tensor_add(gd[:], gd[:], ge[:]))
            k.snap(6, gd[:])
            V(lambda e: e.memset(gc[:], 0.0))
            for it in range(6):
                V(lambda e: e.tensor_max(ga.ap()[:, 0:64], gd.ap()[:, 0:128:2], gd.ap()[:, 1:128:2]))
                V(lambda e: e.tensor_max(ga.ap()[:, 0:32], ga.ap()[:, 0:64:2], ga.ap()[:, 1:64:2]))
                V(lambda e: e.tensor_max(ga.ap()[:, 0:16], ga.ap()[:, 0:32:2], ga.ap()[:, 1:32:2]))
                V(lambda e: e.tensor_max(gsm.ap()[:, 0:8], ga.ap()[:, 0:16:2], ga.ap()[:, 1:16:2]))
                for t8 in range(8):
                    V(lambda e, t_=t8: e.tensor_scalar(
                        ge.ap()[:, t_ * 16:(t_ + 1) * 16], gd.ap()[:, t_ * 16:(t_ + 1) * 16],
                        gsm.ap()[:, t_:t_ + 1], None, ALU.is_equal))
                V(lambda e: e.tensor_add(gc[:], gc[:], ge[:]))
                V(lambda e: e.tensor_scalar(ge[:], ge[:], -1e9, None, ALU.mult))
                V(lambda e: e.tensor_add(gd[:], gd[:], ge[:]))
            k.snap(7, gc[:])
            V(lambda e: e.tensor_mul(ge[:], gc[:], gb[:]))
            V(lambda e: e.tensor_add(ga.ap()[:, 0:64], ge.ap()[:, 0:128:2], ge.ap()[:, 1:128:2]))
            V(lambda e: e.tensor_add(ga.ap()[:, 0:32], ga.ap()[:, 0:64:2], ga.ap()[:, 1:64:2]))
            V(lambda e: e.tensor_add(ga.ap()[:, 0:16], ga.ap()[:, 0:32:2], ga.ap()[:, 1:32:2]))
            V(lambda e: e.tensor_add(gsm.ap()[:, 8:16], ga.ap()[:, 0:16:2], ga.ap()[:, 1:16:2]))
            V(lambda e: e.tensor_scalar(gsm.ap()[:, 8:16], gsm.ap()[:, 8:16], 1e-20, None, ALU.add))
            V(lambda e: e.reciprocal(gsm.ap()[:, 16:24], gsm.ap()[:, 8:16]))
            k.snap(8, ge[:])
            k.snap(9, gsm[:], width=32)
            comb_ops = {}
            for t8 in range(8):
                comb_ops[t8] = k.op("vector", k.sV, lambda e, t_=t8: e.tensor_scalar(
                    gc.ap()[:, t_ * 16:(t_ + 1) * 16], ge.ap()[:, t_ * 16:(t_ + 1) * 16],
                    gsm.ap()[:, 16 + t_:17 + t_], 1.25, ALU.mult, ALU.mult))
            k.snap(10, gc[:])
            tr_ev = {}
            for t8 in range(8):
                k.wait("tensor", k.sV, comb_ops[t8])
                bank = pb[4 + (t8 % 2)].ap()[0:16, 0:128]
                if t8 >= 2:
                    k.wait("tensor", k.sA, tr_ev[t8 - 2])
                pv = k.op("tensor", k.sP, lambda e, b=bank, t_=t8: e.transpose(
                    b, gc.ap()[:, t_ * 16:(t_ + 1) * 16], identf[:]))
                k.wait("scalar", k.sP, pv)
                tr_ev[t8] = evac(r16.ap()[0:16, t8 * 128:(t8 + 1) * 128], bank)
            k.wait("sync", k.sA, tr_ev[7])
            return k.dma("sync", s_cmw, bass.AP(combd.tensor, 0, [[TOK, 16], [1, TOK]]), r16[:])

        SKIP = set(os.environ.get("KSKIP", "").split(","))
        def ph_experts(comb_stored, y_done):
            k.wait("sync", s_cmw, comb_stored)
            k.wait("vector", k.sV, y_done[HCN - 1])
            wlv = {}
            blk_free = {}
            crow_free = {}
            cmb_free = {}
            sil_prev = {0: None, 1: None}
            h1_prev = {0: None, 1: None}
            iv_prev = {0: None, 1: None}
            iv_last = {0: None, 1: None}
            down_d = {}
            add_d = {}
            bcast_ev = {}
            nb = 0
            for e_ in range(n_exp):
                r = R_CR0 if e_ % 2 == 0 else R_CR1
                rbase = 0
                if e_ >= 2:
                    k.wait("sync", k.sP, crow_free[e_ - 2])
                crow_ld = k.dma("sync", s_cr[e_ % 2], r[:, 0:1024], combd[e_:e_ + 1, :])
                k.wait("tensor", s_cr[e_ % 2], crow_ld)
                if e_ >= 1:
                    k.wait("tensor", k.sA, bcast_ev[e_ - 1])
                ev = None
                for t in range(2):
                    if t == 1:
                        k.wait("tensor", k.sA, ev)
                    pv = mm(pb[6].ap(), onesr(rbase), r[:, t * 512:(t + 1) * 512],
                            True, True, inc=True)
                    k.wait("scalar", k.sP, pv)
                    if e_ >= 2:
                        k.wait("scalar", k.sV, cmb_free[e_ - 2])
                    ev = evac(cmb[e_ % 2][:, t * 512:(t + 1) * 512], pb[6].ap())
                crow_free[e_] = pv
                bcast_ev[e_] = ev
                cmb_ready = ev
                for ic in range(ICN):
                    for warr in (wg, wu):
                        if nb >= 4:
                            k.wait("gpsimd", k.sP, blk_free.get(nb - 4))
                        wlv[nb] = (k.dma("gpsimd", s_wb[nb % 4], wblk[nb % 4][:], warr[e_, ic]), s_wb[nb % 4])
                        nb += 1
                    gslot, uslot = (nb - 2) % 4, (nb - 1) % 4
                    if "gate" in SKIP:
                        blk_free[nb - 2] = None
                        blk_free[nb - 1] = None
                        continue
                    for t in range(2):
                        k.wait("tensor", wlv[nb - 2][1], wlv[nb - 2][0])
                        k.wait("tensor", k.sA, sil_prev[t])
                        for c in range(HCN):
                            gmm = mm(pb[t].ap(), wblk[gslot][:, c * 128:(c + 1) * 128],
                                     yb[c][:, t * 512:(t + 1) * 512], c == 0, c == HCN - 1,
                                     inc=(c == HCN - 1))
                        if "silu" in SKIP:
                            sil_prev[t] = gmm and k.sA.n
                            iv_prev[t] = k.sV.n
                            iv_last[t] = iv_prev[t]
                            if t == 1:
                                blk_free[nb - 2] = gmm
                                blk_free[nb - 1] = gmm
                            if ic == ICN - 1 and t == 1:
                                cmb_free[e_] = k.sV.n
                            continue
                        k.wait("scalar", k.sP, gmm)
                        k.wait("scalar", k.sV, iv_prev[t])
                        sil_prev[t] = k.op("scalar", k.sA, lambda e, t_=t: e.activation(
                            gsb[t_][:], pb[t_].ap(), AF.Silu))
                        if "up" in SKIP:
                            if t == 1:
                                blk_free[nb - 2] = gmm
                                blk_free[nb - 1] = gmm
                            iv_prev[t] = sil_prev[t] and k.sV.n
                            iv_last[t] = iv_prev[t]
                            if ic == ICN - 1 and t == 1:
                                cmb_free[e_] = k.sV.n
                            continue
                        k.wait("tensor", wlv[nb - 1][1], wlv[nb - 1][0])
                        k.wait("tensor", k.sV, h1_prev[t])
                        for c in range(HCN):
                            umm = mm(pb[2 + t].ap(), wblk[uslot][:, c * 128:(c + 1) * 128],
                                     yb[c][:, t * 512:(t + 1) * 512], c == 0, c == HCN - 1,
                                     inc=(c == HCN - 1))
                        if t == 1:
                            blk_free[nb - 2] = umm
                            blk_free[nb - 1] = umm
                        k.wait("vector", k.sP, umm)
                        k.wait("vector", k.sA, sil_prev[t])
                        k.wait("vector", k.sA, cmb_ready)
                        if e_ >= 1 and ic == 0:
                            k.wait("vector", k.sP, down_d[(e_ - 1, t)])
                        if "h1inplace" in SKIP:
                            h1_prev[t] = k.op("vector", k.sV, lambda e, t_=t: e.tensor_copy(
                                cmb[0][:, 0:512], pb[2 + t_].ap()))
                            k.wait("vector", k.sV, h1_prev[t])
                            k.op("vector", None, lambda e, t_=t: e.tensor_mul(
                                gsb[t_][:], cmb[0][:, 0:512], gsb[t_][:]))
                        else:
                            h1_prev[t] = k.op("vector", k.sV, lambda e, t_=t: e.tensor_mul(
                                gsb[t_][:], pb[2 + t_].ap(), gsb[t_][:]))
                        k.wait("vector", k.sV, h1_prev[t])
                        iv_prev[t] = k.op("vector", k.sV, lambda e, t_=t, ic_=ic, e2=e_:
                                          e.tensor_mul(ichunk(ic_, t_), gsb[t_][:],
                                                       cmb[e2 % 2][:, t_ * 512:(t_ + 1) * 512]))
                        iv_last[t] = iv_prev[t]
                        if ic == ICN - 1 and t == 1:
                            cmb_free[e_] = iv_prev[t]
                if "down" in SKIP:
                    for t in range(2):
                        down_d[(e_, t)] = iv_last[t] and k.sP.n
                    continue
                for hc in range(HCN):
                    if nb >= 4:
                        k.wait("gpsimd", k.sP, blk_free[nb - 4])
                    wlv[nb] = (k.dma("gpsimd", s_wb[nb % 4], wblk[nb % 4][:, 0:1408], wd[e_, hc]), s_wb[nb % 4])
                    nb += 1
                    slot = (nb - 1) % 4
                    k.wait("tensor", wlv[nb - 1][1], wlv[nb - 1][0])
                    for t in range(2):
                        bank = pb[4 + t]
                        k.wait("tensor", k.sV, add_d.get((hc - 1, t)))
                        if hc == 0:
                            k.wait("tensor", k.sV, iv_last[t])
                            if e_ >= 1:
                                k.wait("tensor", k.sV, add_d.get((HCN - 1, t)))
                        for ic in range(ICN):
                            pv = mm(bank.ap(), wblk[slot][:, ic * 128:(ic + 1) * 128],
                                    ichunk(ic, t), ic == 0, ic == ICN - 1, inc=(ic == ICN - 1))
                        down_d[(e_, t)] = pv
                        if t == 1:
                            blk_free[nb - 1] = pv
                        k.wait("vector", k.sP, pv)
                        add_d[(hc, t)] = k.op("vector", k.sV,
                            lambda e, hc_=hc, t_=t, b=bank: e.tensor_add(
                                xres[hc_][:, t_ * 512:(t_ + 1) * 512],
                                xres[hc_][:, t_ * 512:(t_ + 1) * 512], b.ap()))

        def ph_shared(y_done):
            k.wait("vector", k.sV, y_done[HCN - 1])
            wlv = {}
            blk_free = {}
            sil_d = {}
            h1_d = {}
            iv_d = {}
            add_d = {}
            down_half = {}
            nb = 0
            for half in range(2):
                for sc in range(11):
                    for warr in (wsg, wsu):
                        if nb >= 4:
                            k.wait("gpsimd", k.sP, blk_free[nb - 4])
                        wlv[nb] = (k.dma("gpsimd", s_wb[nb % 4], wblk[nb % 4][:], warr[half * 11 + sc]), s_wb[nb % 4])
                        nb += 1
                    gslot, uslot = (nb - 2) % 4, (nb - 1) % 4
                    for t in range(2):
                        k.wait("tensor", wlv[nb - 2][1], wlv[nb - 2][0])
                        k.wait("tensor", k.sA, sil_d.get((half, sc - 1, t)))
                        if sc == 0 and half == 1:
                            k.wait("tensor", k.sA, sil_d.get((0, 10, t)))
                        for c in range(HCN):
                            gmm = mm(pb[t].ap(), wblk[gslot][:, c * 128:(c + 1) * 128],
                                     yb[c][:, t * 512:(t + 1) * 512], c == 0, c == HCN - 1,
                                     inc=(c == HCN - 1))
                        k.wait("scalar", k.sP, gmm)
                        k.wait("scalar", k.sV, iv_d.get((half, sc - 1, t)))
                        if sc == 0 and half == 1:
                            k.wait("scalar", k.sV, iv_d.get((0, 10, t)))
                        sil_d[(half, sc, t)] = k.op("scalar", k.sA, lambda e, t_=t:
                                                    e.activation(gsb[t_][:], pb[t_].ap(), AF.Silu))
                        k.wait("tensor", wlv[nb - 1][1], wlv[nb - 1][0])
                        k.wait("tensor", k.sV, h1_d.get((half, sc - 1, t)))
                        if sc == 0 and half == 1:
                            k.wait("tensor", k.sV, h1_d.get((0, 10, t)))
                        for c in range(HCN):
                            umm = mm(pb[2 + t].ap(), wblk[uslot][:, c * 128:(c + 1) * 128],
                                     yb[c][:, t * 512:(t + 1) * 512], c == 0, c == HCN - 1,
                                     inc=(c == HCN - 1))
                        if t == 1:
                            blk_free[nb - 2] = umm
                            blk_free[nb - 1] = umm
                        k.wait("vector", k.sP, umm)
                        k.wait("vector", k.sA, sil_d[(half, sc, t)])
                        if half == 1 and sc == 0:
                            k.wait("vector", k.sP, down_half[(0, t)])
                        h1_d[(half, sc, t)] = k.op("vector", k.sV, lambda e, t_=t:
                                                   e.tensor_mul(gsb[t_][:], pb[2 + t_].ap(), gsb[t_][:]))
                        k.wait("vector", k.sV, h1_d[(half, sc, t)])
                        iv_d[(half, sc, t)] = k.op("vector", k.sV, lambda e, t_=t, sc_=sc:
                                                   e.tensor_copy(ichunk(sc_, t_), gsb[t_][:]))
                for hc in range(HCN):
                    if nb >= 4:
                        k.wait("gpsimd", k.sP, blk_free[nb - 4])
                    wlv[nb] = (k.dma("gpsimd", s_wb[nb % 4], wblk[nb % 4][:, 0:1408],
                                     bass.AP(wsd.tensor, hc * 128 * 2816 + half * 1408,
                                             [[2816, 128], [1, 1408]])), s_wb[nb % 4])
                    nb += 1
                    slot = (nb - 1) % 4
                    k.wait("tensor", wlv[nb - 1][1], wlv[nb - 1][0])
                    for t in range(2):
                        bank = pb[4 + t]
                        k.wait("tensor", k.sV, add_d.get((half, hc - 1, t)))
                        if hc == 0:
                            k.wait("tensor", k.sV, iv_d[(half, 10, t)])
                            if half == 1:
                                k.wait("tensor", k.sV, add_d.get((0, HCN - 1, t)))
                        for sc in range(11):
                            pv = mm(bank.ap(), wblk[slot][:, sc * 128:(sc + 1) * 128],
                                    ichunk(sc, t), sc == 0, sc == 10, inc=(sc == 10))
                        down_half[(half, t)] = pv
                        if t == 1:
                            blk_free[nb - 1] = pv
                        k.wait("vector", k.sP, pv)
                        add_d[(half, hc, t)] = k.op("vector", k.sV,
                            lambda e, hc_=hc, t_=t, b=bank: e.scalar_tensor_tensor(
                                xres[hc_][:, t_ * 512:(t_ + 1) * 512], b.ap(), 0.5,
                                xres[hc_][:, t_ * 512:(t_ + 1) * 512], ALU.mult, ALU.add))

        def ph_qkv(y_done):
            for dst, src in ((cqrep, cqr), (sqrep, sqr), (ckrep, ckr), (skrep, skr)):
                rp = k.dma("sync", s_rt, dst, src)
            k.wait("vector", s_rt, rp)
            k.wait("tensor", k.sV, y_done[HCN - 1])
            wlv = {}
            blk_free = {}
            mmv = {}
            bank_free = {}
            st_qk = {}
            st_v = {}
            vtr = {}
            vtr_ev = {}
            vst = {}
            rv_prev = None
            prev_qo = {}
            prev_vo = [None]
            OSEQ = []
            for par in range(2):
                for hi in range(8):
                    hh = 2 * hi + par
                    OSEQ += [hh, 16 + hh, 32 + hh]
            for o in range(48):
                if o >= 4:
                    k.wait("gpsimd", k.sP, blk_free[o - 4])
                wlv[o] = k.dma("gpsimd", s_wb[o % 4], wblk[o % 4][:], wqkv[OSEQ[o]])
                k.wait("tensor", s_wb[o % 4], wlv[o])
                if o == 24:
                    emit_gather(0)
                for t in range(2):
                    bank = pb[(2 * o + t) % 4]
                    k.wait("tensor", k.sV, bank_free.get((o - 2, t)))
                    for c in range(HCN):
                        pv = mm(bank.ap(), wblk[o % 4][:, c * 128:(c + 1) * 128],
                                yb[c][:, t * 512:(t + 1) * 512], c == 0, c == HCN - 1,
                                inc=(c == HCN - 1))
                    mmv[(o, t)] = pv
                blk_free[o] = mmv[(o, 1)]
                h = OSEQ[o] % 16
                if OSEQ[o] < 32:
                    isq = OSEQ[o] < 16
                    crep, srep = (cqrep, sqrep) if isq else (ckrep, skrep)
                    dstt = qsb if isq else ksb
                    sect = (h % 2) * 3145728 + (0 if isq else 1) * 1048576 + (h // 2) * 131072
                    po = prev_qo.get(isq)
                    if po is not None:
                        k.wait("vector", st_qk[po][1], st_qk[po][0])
                    rv = None
                    for t in range(2):
                        bank = pb[(2 * o + t) % 4]
                        k.wait("vector", k.sP, mmv[(o, t)])
                        k.op("vector", k.sV, lambda e, b=bank, c_=crep, t_=t: e.tensor_mul(
                            re1, b.ap(), c_[:, t_ * 512:(t_ + 1) * 512]))
                        e2v = k.op("vector", k.sV, lambda e, b=bank, s_=srep, t_=t: e.tensor_mul(
                            re2, b.ap(), s_[:, t_ * 512:(t_ + 1) * 512]))
                        bank_free[(o, t)] = e2v
                        k.wait("sync", k.sV, e2v)
                        if rv_prev:
                            k.wait("sync", k.sV, rv_prev)
                        k.dma("sync", s_sw, re2s[0:64, :], re2[64:128, :])
                        sw1 = k.dma("sync", s_sw, re2s[64:128, :], re2[0:64, :])
                        k.wait("vector", s_sw, sw1)
                        k.wait("vector", k.sV, e2v)
                        if isq:
                            od0 = dstt.ap()[0:64, t * 512:(t + 1) * 512]
                            od1 = dstt.ap()[64:128, t * 512:(t + 1) * 512]
                        else:
                            rs = slice(t * 512 + 511, t * 512 - 1 if t > 0 else None, -1)
                            od0 = dstt.ap()[0:64, rs]
                            od1 = dstt.ap()[64:128, rs]
                        k.op("vector", None, lambda e, d=od0: e.tensor_sub(d, re1[0:64, :], re2s[0:64, :]))
                        rv = k.op("vector", k.sV, lambda e, d=od1: e.tensor_add(
                            d, re1[64:128, :], re2s[64:128, :]))
                        rv_prev = rv
                    k.wait("sync", k.sV, rv)
                    ssem = s_qs[h % 2] if isq else s_ks[h % 2]
                    st_qk[o] = (k.dma("sync", ssem, bass.AP(kvl.tensor, sect, [[TOK, 128], [1, TOK]]),
                                      dstt[:]), ssem)
                    prev_qo[isq] = o
                else:
                    ev = None
                    for t in range(2):
                        bank = pb[(2 * o + t) % 4]
                        k.wait("vector", k.sP, mmv[(o, t)])
                        if prev_vo[0] is not None and t == 0:
                            k.wait("vector", k.sP, vtr.get((prev_vo[0], 7)))
                        rs = slice(t * 512 + 511, t * 512 - 1 if t > 0 else None, -1)
                        ev = k.op("vector", k.sV, lambda e, b=bank, r_=rs: e.tensor_copy(
                            vsb.ap()[:, r_], b.ap()))
                        bank_free[(o, t)] = ev
                    k.wait("tensor", k.sV, ev)
                    sect = (h % 2) * 3145728 + 2 * 1048576 + (h // 2) * 131072
                    for tc in range(8):
                        k.wait("tensor", k.sA, vtr_ev.get((o, tc - 1)))
                        if tc == 0 and prev_vo[0] is not None:
                            k.wait("tensor", k.sA, vtr_ev.get((prev_vo[0], 7)))
                        pv = k.op("tensor", k.sP, lambda e, tc_=tc: e.transpose(
                            pt0.ap(), vsb[:, tc_ * 128:(tc_ + 1) * 128], identb[:]))
                        vtr[(o, tc)] = pv
                        k.wait("scalar", k.sP, pv)
                        pvst = vst.get((o, tc - 2)) or (vst.get((prev_vo[0], tc + 6)) if prev_vo[0] is not None else None)
                        if pvst:
                            k.wait("scalar", pvst[1], pvst[0])
                        ve = evac(vt[tc % 2][:], pt0.ap())
                        vtr_ev[(o, tc)] = ve
                        k.wait("sync", k.sA, ve)
                        vst[(o, tc)] = (k.dma("sync", s_vst[h % 2][tc % 2], bass.AP(
                            kvl.tensor, sect + tc * 128 * 128, [[128, 128], [1, 128]]),
                            vt[tc % 2][:]), s_vst[h % 2][tc % 2])
                    st_v[o] = vst[(o, 7)][0]
                    prev_vo[0] = o

        SECH = 3145728
        def emit_gather(par):
            for c_ in (s_qs[par], s_ks[par], s_vst[par][0], s_vst[par][1]):
                k.wait("gpsimd", c_, c_.n)
            cv = k.op("gpsimd", k.sC, lambda e, p_=par: e.collective_compute(
                "AllGather", ALU.bypass, [list(range(NC))],
                [bass.AP(kvl.tensor, p_ * SECH, [[1024, SECH // 1024], [1, 1024]]).opt()],
                [bass.AP(kvgt[p_], 0, [[1024, NC * SECH // 1024], [1, 1024]]).opt()]))
            k.wait("gpsimd", k.sC, cv)
            k.dma_dyn("gpsimd", s_kvh[par],
                      bass.AP(kvh.tensor, par * 3145728, [[1, 3145728]]),
                      kvgt[par], 0, "kvh",
                      ((SECH, 8), (1048576, 3), (1024, 128), (1, 1024)))

        def ph_gather_kv():
            emit_gather(1)
        

        CHS = CTXSECT // 2
        def emit_gather_ctx(par):
            for c_ in s_cst[par]:
                k.wait("gpsimd", c_, c_.n)
            cv = k.op("gpsimd", k.sC, lambda e, p_=par: e.collective_compute(
                "AllGather", ALU.bypass, [list(range(NC))],
                [bass.AP(ctxl.tensor, p_ * CHS, [[1024, CHS // 1024], [1, 1024]]).opt()],
                [bass.AP(ctxgt[p_], 0, [[1024, NC * CHS // 1024], [1, 1024]]).opt()]))
            k.wait("gpsimd", k.sC, cv)
            k.dma_dyn("gpsimd", s_cxh,
                      bass.AP(ctxh.tensor, par * 131072, [[262144, 8], [1024, 128], [1, 512]]),
                      ctxgt[par], 0, "clo", ((CHS, 8), (S, 128), (1, 512)))
            k.dma_dyn("gpsimd", s_cxh,
                      bass.AP(ctxh.tensor, par * 131072 + 512, [[262144, 8], [1024, 128], [1, 512]]),
                      ctxgt[par], 0, "chi", ((CHS, 8), (S, 128), (1, 512)))

        def ph_gather_ctx():
            emit_gather_ctx(1)
            k.wait("sync", s_cxh, 64)
        

        def ph_flash():
            GC = 0
            KB_ = 0
            qkj = {}
            pvden = {}
            expd = {}
            muld = {}
            stored = {}
            recipd = {}
            bcastd = {}
            bevacd = {}
            pair_end = {}
            blk_qk = {}
            kblk_ld = {}
            vbld = {}
            blk_pv = {}
            JD = 0
            for lh in range(LH):
                tb = k.dma("sync", s_tb[lh], cmb[lh][:],
                           bass.AP(biasd.tensor, lh * 1408 + 127, [[1, 128], [1, 1024]]))
                k.wait("vector", s_tb[lh], tb)
                k.wait("sync", s_kvh[lh], 16)
                k.wait("gpsimd", s_kvh[lh], 16)
                for qt in range(QTN):
                    p = lh * QTN + qt
                    nm = 4 * qt + 4
                    nj = nm // 2
                    rq = qt if qt <= 7 else 15 - qt
                    halfq = 0 if qt <= 7 else 1
                    if p >= 2:
                        k.wait("sync", k.sP, pair_end[p - 2])
                    qtl_ld = k.dma("sync", s_qtl[p % 2], qtl[p % 2],
                                   bass.AP(kvh.tensor,
                                           lh * 3145728 + rq * 3 * 131072 + halfq * 512,
                                           [[1024, 128], [1, 512]]))
                    ctx_bank = (pC0 if p % 2 == 0 else pC1)
                    scb = [pA, pB]
                    for j in range(nj):
                        for half in range(2):
                            m = 2 * j + half
                            J0 = 7680 - 512 * qt + 128 * m
                            b = 15 - J0 // 512
                            r = b if b <= 7 else 15 - b
                            hv = 0 if b <= 7 else 1
                            s0 = J0 % 512
                            if m % 4 == 0:
                                if KB_ >= 2:
                                    k.wait("sync", k.sP, blk_qk[KB_ - 2])
                                kblk_ld[KB_] = k.dma(
                                    "sync", s_kb[KB_ % 2], kblk[KB_ % 2],
                                    bass.AP(kvh.tensor,
                                            lh * 3145728 + (r * 3 + 1) * 131072 + hv * 512,
                                            [[1024, 128], [1, 512]]))
                                curKB = KB_
                                KB_ += 1
                            if m % 4 == 0:
                                VB = KB_ - 1
                                if VB >= 2:
                                    k.wait("sync", k.sP, blk_pv.get(VB - 2, 0))
                                vbld[VB] = k.dma("sync", s_vb[VB % 2],
                                                 w1b[:, (VB % 2) * 512:(VB % 2 + 1) * 512],
                                                 bass.AP(kvh.tensor,
                                                         lh * 3145728 + (r * 3 + 2) * 131072 + hv * 65536,
                                                         [[128, 128], [16384, 4], [1, 128]]))
                            if m == 0:
                                k.wait("tensor", s_qtl[p % 2], qtl_ld)
                                k.wait("tensor", k.sV, muld.get(p - 2))
                            if m % 4 == 0:
                                k.wait("tensor", s_kb[curKB % 2], kblk_ld[curKB])
                            if half == 0:
                                k.wait("tensor", k.sA, expd.get(JD - 2))
                            sc = scb[JD % 2].ap()[:, half * 512:(half + 1) * 512]
                            v_ = mm(sc, kblk[curKB % 2][:, (m % 4) * 128:(m % 4 + 1) * 128],
                                    qtl[p % 2], True, True, inc=(half == 1))
                            if m % 4 == 3 or m == nm - 1:
                                blk_qk[curKB] = k.sP.n
                            GC += 1
                        qkj[JD] = v_
                        near0, near1 = (2 * j) <= 4, (2 * j + 1) <= 4
                        if near0 or near1:
                            k.wait("vector", k.sP, qkj[JD])
                            k.wait("vector", k.sA, expd.get(JD - 2))
                            for half in range(2):
                                m = 2 * j + half
                                if m <= 4:
                                    k.op("vector", k.sV, lambda e, m_=m, lh_=lh, jj=JD, h_=half:
                                         e.tensor_add(tmpf[jj % 2][:, h_ * 512:(h_ + 1) * 512],
                                                      scb[jj % 2].ap()[:, h_ * 512:(h_ + 1) * 512],
                                                      cmb[lh_][:, m_ * 128:m_ * 128 + 512]))
                            av = k.sV.n
                            k.wait("scalar", k.sV, av)
                            k.wait("scalar", k.sP, pvden.get(JD - 2))
                            if near0 and near1:
                                expd[JD] = k.op("scalar", k.sA, lambda e, jj=JD: e.activation(
                                    psb[jj % 2], tmpf[jj % 2][:, 0:1024], AF.Exp))
                            else:
                                k.op("scalar", k.sA, lambda e, jj=JD: e.activation(
                                    psb[jj % 2][:, 0:512], tmpf[jj % 2][:, 0:512], AF.Exp))
                                expd[JD] = k.op("scalar", k.sA, lambda e, jj=JD, lh_=lh: e.activation(
                                    psb[jj % 2][:, 512:1024],
                                    scb[jj % 2].ap()[:, 512:1024], AF.Exp,
                                    bias=farb.ap()[:, lh_:lh_ + 1]))
                        else:
                            k.wait("scalar", k.sP, qkj[JD])
                            k.wait("scalar", k.sP, pvden.get(JD - 2))
                            expd[JD] = k.op("scalar", k.sA, lambda e, jj=JD, lh_=lh: e.activation(
                                psb[jj % 2], scb[jj % 2].ap()[:, 0:1024], AF.Exp,
                                bias=farb.ap()[:, lh_:lh_ + 1]))
                        k.wait("tensor", k.sA, expd[JD])
                        if j == 0:
                            k.wait("tensor", k.sA, bevacd.get(p - 1))
                            k.wait("tensor", k.sV, recipd.get(p - 1))
                        for half in range(2):
                            m = 2 * j + half
                            gci = GC - 2 + half
                            vb = gci // 4
                            k.wait("tensor", s_vb[vb % 2], vbld[vb])
                            vtile = w1b[:, (vb % 2) * 512 + (gci % 4) * 128:
                                        (vb % 2) * 512 + (gci % 4) * 128 + 128]
                            mm(ctx_bank.ap(), vtile, psb[JD % 2][:, half * 512:(half + 1) * 512],
                               m == 0, m == nm - 1)
                            v_ = mm(pD.ap()[0:1, :], onescolb[:],
                                    psb[JD % 2][:, half * 512:(half + 1) * 512],
                                    m == 0, m == nm - 1, inc=(half == 1))
                        pvden[JD] = v_
                        blk_pv[JD * 0 + (GC - 1) // 4] = v_
                        JD += 1
                    denr = R_DR0 if p % 2 == 0 else R_DR1
                    drb = 0
                    k.wait("vector", k.sP, pvden[JD - 1])
                    k.wait("vector", k.sP, bcastd.get(p - 1))
                    recipd[p] = k.op("vector", k.sV, lambda e, d=denr:
                                     e.reciprocal(d[:, 0:512], pD.ap()[0:1, :]))
                    k.wait("tensor", k.sV, recipd[p])
                    bcastd[p] = mm(pD.ap(), onesr(drb), denr[:, 0:512], True, True, inc=True)
                    k.wait("scalar", k.sP, bcastd[p])
                    k.wait("scalar", k.sV, muld.get(p - 2))
                    bevacd[p] = evac(bcsb[p % 2], pD.ap())
                    k.wait("vector", k.sA, bevacd[p])
                    if stored.get(p - 2):
                        k.wait("vector", stored[p - 2][1], stored[p - 2][0])
                    muld[p] = k.op("vector", k.sV, lambda e, p_=p, cb=ctx_bank: e.tensor_mul(
                        ctxsb[p_ % 2][:], cb.ap(), bcsb[p_ % 2]))
                    k.wait("sync", k.sV, muld[p])
                    stored[p] = (k.dma("sync", s_cst[lh][p % 2], bass.AP(
                        ctxl.tensor, lh * HD * S + 512 * qt, [[S, 128], [1, 512]]),
                        ctxsb[p % 2][:]), s_cst[lh][p % 2])
                    pair_end[p] = bcastd[p]
                if lh == 0:
                    emit_gather_ctx(0)

        def ph_dense():
            wlv = {}
            mm_done = {}
            add_d = {}
            ld = {}
            for i in range(HCN + 1):
                if i < HCN:
                    if i >= 4:
                        k.wait("gpsimd", k.sP, mm_done[i - 4])
                    wlv[i] = k.dma("gpsimd", s_wb[i % 4], wblk[i % 4][:], wdense[i])
                    if i == 0:
                        k.wait("sync", s_cxh, 64)
                    ld[i] = k.dma("sync", s_dld, yb[i][:],
                                  bass.AP(ctxh.tensor, i * 131072, [[1024, 128], [1, 1024]]))
                if i >= 1:
                    hc2 = i - 1
                    k.wait("tensor", s_wb[hc2 % 4], wlv[hc2])
                    if hc2 == 0:
                        k.wait("tensor", s_dld, 16 * 16)
                    for t in range(2):
                        bank = pb[(2 * hc2 + t) % 4]
                        k.wait("tensor", k.sV, add_d.get((hc2 - 2, t)))
                        for c in range(HCN):
                            pv = mm(bank.ap(), wblk[hc2 % 4][:, c * 128:(c + 1) * 128],
                                    yb[c][:, t * 512:(t + 1) * 512], c == 0, c == HCN - 1,
                                    inc=(c == HCN - 1))
                        mm_done[hc2] = pv
                        k.wait("vector", k.sP, pv)
                        add_d[(hc2, t)] = k.op("vector", k.sV,
                            lambda e, h_=hc2, t_=t, b=bank: e.tensor_add(
                                xres[h_][:, t_ * 512:(t_ + 1) * 512],
                                xres[h_][:, t_ * 512:(t_ + 1) * 512], b.ap()))

        def ph_out():
            tr_done = {}
            st = {}
            for tc in range(8):
                for hc in range(HCN):
                    src = xres[hc].ap()[:, tc * 128:(tc + 1) * 128]
                    bank = pb[hc % 2].ap()[:, 0:128]
                    prev = tr_done.get((tc, hc - 2)) or tr_done.get((tc - 1, hc + 14))
                    k.wait("tensor", k.sA, prev)
                    if hc == 0 and tc >= 1:
                        k.wait("tensor", s_out[(tc - 1) % 2], st[(tc - 1, 1)])
                    pv = k.op("tensor", k.sP,
                              lambda e, b=bank, s=src: e.transpose(b, s, identf[:]))
                    k.wait("scalar", k.sP, pv)
                    tr_done[(tc, hc)] = evac(
                        outh[hc // 8][:, (hc % 8) * 128:(hc % 8 + 1) * 128], bank)
                k.wait("sync", k.sA, tr_done[(tc, HCN - 1)])
                st[(tc, 0)] = k.dma("sync", s_out[tc % 2], out[tc * 128:(tc + 1) * 128, 0:1024], outh[0][:])
                st[(tc, 1)] = k.dma("sync", s_out[tc % 2], out[tc * 128:(tc + 1) * 128, 1024:2048], outh[1][:])
            k.wait("sync", s_out[0], s_out[0].n)
            k.wait("sync", s_out[1], s_out[1].n)

        def dump_fm(name, tiles):
            if name not in dbg:
                return
            k.barrier()
            for hc in range(HCN):
                vv = k.op("vector", k.sV, lambda e, h_=hc: e.tensor_copy(cmb[0][:], tiles[h_][:]))
                k.wait("sync", k.sV, vv)
                sv = k.dma("sync", s_dbg, dbg[name][hc * 128:(hc + 1) * 128, :], cmb[0][:])
                k.wait("vector", s_dbg, sv)
            k.barrier()

        def dump_dram(name, tensor, nelem):
            if name not in dbg:
                return
            k.barrier()
            n = min(nelem, H * TOK)
            dv = k.dma("sync", s_dbg, bass.AP(dbg[name].tensor, 0, [[1, n]]), bass.AP(tensor, 0, [[1, n]]))
            k.wait("sync", s_dbg, dv)
            k.barrier()

        def snap(idx, tile, width=128):
            if "gdbg" not in dbg:
                return
            k.wait("sync", k.sV, k.sV.n)
            k.wait("sync", k.sA, k.sA.n)
            k.wait("sync", k.sP, k.sP.n)
            sv = k.dma("sync", s_dbg, dbg["gdbg"][idx * 128:idx * 128 + 128, 0:width], tile)
            k.wait("vector", s_dbg, sv)
            k.wait("scalar", s_dbg, sv)
            k.wait("tensor", s_dbg, sv)

        k.snap = snap

        order = ["prep", "load", "ln1", "router1", "experts1", "shared1",
                 "ln2", "qkv", "gkv", "flash", "gctx", "dense",
                 "ln3", "router2", "experts2", "shared2", "out"]
        stop = order.index(upto)

        def minibar():
            vals = [(k.sP, k.sP.n), (k.sA, k.sA.n), (k.sV, k.sV.n)]
            for eng in KB.ENGS:
                for c, v in vals:
                    k.wait(eng, c, v)

        def phase(nm, fn, bar=True):
            if order.index(nm) <= stop:
                res = fn()
                if bar:
                    k.barrier()
                else:
                    minibar()
                return res
            return None

        phase("prep", ph_prep)
        phase("load", ph_load_x)
        if "xres0" in dbg:
            k.barrier()
            dump_fm("xres0", xres)
        y1 = phase("ln1", ph_ln)
        if y1 is not None:
            dump_fm("ln1", yb)
        if y1 is not None and "lndbg" in dbg:
            k.barrier()
            d = dbg["lndbg"]
            k.dma("sync", s_dbg, d[0:128, :], Asb[:])
            k.dma("sync", s_dbg, d[128:256, :], Bsb[:])
            k.dma("sync", s_dbg, d[256:257, :], R_M)
            k.dma("sync", s_dbg, d[257:258, :], R_V)
            k.dma("sync", s_dbg, d[258:259, :], R_A)
            k.dma("sync", s_dbg, d[259:260, :], R_B)
            sv = k.dma("sync", s_dbg, d[260:261, :], R_T)
            k.wait("sync", s_dbg, sv)
            k.barrier()
        cs1 = phase("router1", lambda: ph_router(y1))
        if cs1 is not None:
            dump_dram("comb1", combd.tensor, E * TOK)
        phase("experts1", lambda: ph_experts(cs1, y1))
        phase("shared1", lambda: ph_shared(y1))
        dump_fm("x1", xres)
        y2 = phase("ln2", ph_ln)
        if y2 is not None:
            dump_fm("ln2", yb)
        phase("qkv", lambda: ph_qkv(y2), bar=False)
        if order.index("qkv") <= stop:
            dump_dram("kvl", kvl.tensor, SEC)
        phase("gkv", ph_gather_kv, bar=False)
        phase("flash", ph_flash, bar=False)
        if order.index("flash") <= stop:
            dump_dram("ctxl", ctxl.tensor, CTXSECT)
        phase("gctx", ph_gather_ctx, bar=False)
        phase("dense", ph_dense)
        dump_fm("x2", xres)
        y3 = phase("ln3", ph_ln)
        cs2 = phase("router2", lambda: ph_router(y3))
        phase("experts2", lambda: ph_experts(cs2, y3))
        phase("shared2", lambda: ph_shared(y3))
        dump_fm("x3", xres)
        phase("out", ph_out)

        block = es.enter_context(nc.Block())
        k.run(block)
    return nc



def _retile(inputs):
    f = np.float32
    gp = np.asarray(inputs["L0_ffn_gate_proj"], f)
    up = np.asarray(inputs["L0_ffn_up_proj"], f)
    dp = np.asarray(inputs["L0_ffn_down_proj"], f)
    sg = np.asarray(inputs["L0_ffn_shared_gate_w"], f)
    su = np.asarray(inputs["L0_ffn_shared_up_w"], f)
    sd = np.asarray(inputs["L0_ffn_shared_down_w"], f)
    qkv = np.asarray(inputs["L0_attn_qkv_w"], f)
    dw = np.asarray(inputs["L0_attn_dense_w"], f)
    gwv = np.asarray(inputs["L0_ffn_gate_weight"], f)
    d = {}
    d["wg"] = np.ascontiguousarray(gp.reshape(E, ICN, 128, HCN, 128).transpose(0, 1, 4, 3, 2).reshape(E, ICN, 128, 2048))
    d["wu"] = np.ascontiguousarray(up.reshape(E, ICN, 128, HCN, 128).transpose(0, 1, 4, 3, 2).reshape(E, ICN, 128, 2048))
    d["wd"] = np.ascontiguousarray(dp.reshape(E, HCN, 128, ICN, 128).transpose(0, 1, 4, 3, 2).reshape(E, HCN, 128, 1408))
    d["wsg"] = np.ascontiguousarray(sg.reshape(SCN, 128, HCN, 128).transpose(0, 3, 2, 1).reshape(SCN, 128, 2048))
    d["wsu"] = np.ascontiguousarray(su.reshape(SCN, 128, HCN, 128).transpose(0, 3, 2, 1).reshape(SCN, 128, 2048))
    d["wsd"] = np.ascontiguousarray(sd.reshape(HCN, 128, SCN, 128).transpose(0, 3, 2, 1).reshape(HCN, 128, 2816))
    d["wqkv"] = np.ascontiguousarray(qkv.reshape(HCN, 128, 48, 128).transpose(2, 1, 0, 3).reshape(48, 128, 2048))
    d["wdense"] = np.ascontiguousarray(dw.reshape(HCN, 128, HCN, 128).transpose(2, 1, 0, 3).reshape(HCN, 128, 2048))
    d["gwt"] = np.ascontiguousarray(gwv.T.reshape(HCN, 128, 16))
    return d


def _bucket_of_delta(n):
    n = np.asarray(n)
    nf = np.maximum(n, 1).astype(np.float64)
    large = (np.log(nf / 16) / np.log(8.0) * 16).astype(np.int64) + 16
    large = np.minimum(large, 31)
    return np.where(n < 16, n, large)


def _consts():
    oh = np.zeros((32, 768), np.float32)
    b = _bucket_of_delta(np.arange(768))
    oh[b, np.arange(768)] = 1.0
    e31v = np.zeros((32, 1), np.float32)
    e31v[31, 0] = 1.0
    identf = np.eye(128, dtype=np.float32)
    nrampv = np.ascontiguousarray(
        np.tile(-(np.arange(16, dtype=np.float32) + 1.0) * 1e-30, 8)[None, :])
    return oh, e31v, identf, nrampv


def _rope_tables(tok_idx):
    inv = 1.0 / 10000.0 ** (np.arange(0, HD, 2, dtype=np.float32) / HD)
    ang = tok_idx[:, None].astype(np.float32) * inv[None, :]
    cos = np.cos(ang).astype(np.float32).T
    sin = np.sin(ang).astype(np.float32).T
    s = np.float32(1.0 / np.sqrt(HD))
    rep = lambda a: np.ascontiguousarray(np.concatenate([a, a], axis=0))
    return rep(cos * s), rep(sin * s), rep(cos), rep(sin)


def prep_in_maps(inputs):
    x = np.ascontiguousarray(np.asarray(inputs["L0_attn_x"], np.float32).reshape(S, H))
    w = _retile(inputs)
    oh, e31v, identf, nrampv = _consts()
    table = np.ascontiguousarray(np.asarray(inputs["L0_attn_rel_attn_table"], np.float32))
    escb_rep = np.ascontiguousarray(
        np.tile(np.asarray(inputs["L0_ffn_e_score_correction_bias"], np.float32), 8)[None, :])
    in_maps = []
    for c in range(NC):
        rows_idx = np.concatenate([np.arange(512 * c, 512 * (c + 1)),
                                   np.arange(512 * (15 - c), 512 * (16 - c))])
        xs = np.ascontiguousarray(x[rows_idx])
        cq, sq, ck, sk = _rope_tables(rows_idx)
        perm = [2 * c, 2 * c + 1] + [hh for hh in range(16) if hh not in (2 * c, 2 * c + 1)]
        tp = np.ascontiguousarray(table[:, perm])
        m = {"xin": xs, "cqr": cq, "sqr": sq, "ckr": ck, "skr": sk, "tablep": tp,
             "escb": escb_rep, "nramp": nrampv, "onehot": oh, "e31": e31v, "identf": identf}
        m.update(w)
        in_maps.append(m)
    return in_maps


def kernel(**inputs):
    in_maps = prep_in_maps(inputs)
    nc = build_program()
    res = run_bass_kernel_spmd(nc, in_maps, core_ids=list(range(NC)))
    full = np.empty((S, H), np.float32)
    for c in range(NC):
        o = np.asarray(res.results[c]["out"])
        full[512 * c:512 * (c + 1)] = o[:512]
        full[512 * (15 - c):512 * (16 - c)] = o[512:]
    return full.reshape(1, S, H)


# version 04 (final)
# speedup vs baseline: 1.1649x; 1.0811x over previous
import os
import numpy as np
import ml_dtypes
import concourse.bass as bass
import concourse.mybir as mybir
from concourse.bass_utils import run_bass_kernel_spmd
from concourse.alu_op_type import AluOpType as ALU
from contextlib import ExitStack

F32 = mybir.dt.float32
BF16 = mybir.dt.bfloat16
AF = mybir.ActivationFunctionType

NC = 8
H = 2048
HCN = 16
TOK = 1024
S = 8192
NH = 16
HD = 128
E = 16
ICN = 11
SCN = 22
LH = 2
QTN = 16
CSTAB = 12.0
NEGB = -30000.0

QSECT = NH * HD * TOK
SEC = 3 * QSECT
KOFF = QSECT
VOFF = 2 * QSECT
CTXSECT = LH * HD * S


class Ctr:
    __slots__ = ("h", "n")
    def __init__(self):
        self.h = None
        self.n = 0
    def inc(self, k=1):
        self.n += k
        return self.n


class KB:
    ENGS = ("tensor", "scalar", "vector", "gpsimd", "sync")

    def __init__(self, nc):
        self.nc = nc
        self.q = {e: [] for e in self.ENGS}
        self.sP = Ctr(); self.sA = Ctr(); self.sV = Ctr(); self.sC = Ctr()
        self.all_ctrs = [self.sP, self.sA, self.sV, self.sC]
        self.rt = {}

    def em(self, eng, fn):
        self.q[eng].append(fn)

    def wait(self, eng, ctr, v):
        if v:
            self.em(eng, lambda e, c=ctr, vv=v: e.wait_ge(c.h, vv))

    def dma(self, eng, ctr, out, in_, **kw):
        v = ctr.inc(16)
        self.em(eng, lambda e, o=out, i=in_, c=ctr, k=kw:
                e.dma_start(out=o, in_=i, **k).then_inc(c.h, 16))
        return v

    def dma_dyn(self, eng, ctr, out, tensor, off_static, dyn_key, apdims, **kw):
        v = ctr.inc(16)
        def fn(e, o=out, t=tensor, os_=off_static, dk=dyn_key, ad=apdims, c=ctr, en=eng, k=kw):
            src = bass.AP(t, self.rt[en][dk] + os_, [list(d) for d in ad])
            e.dma_start(out=o, in_=src, **k).then_inc(c.h, 16)
        self.em(eng, fn)
        return v

    def op(self, eng, ctr, fn):
        if eng == "vector":
            c = self.sV
            v = c.inc(1)
            self.em(eng, lambda e, f=fn, cc=c: f(e).then_inc(cc.h, 1))
            self.em(eng, lambda e, cc=c, vv=v: e.wait_ge(cc.h, vv))
            return v
        if ctr is None:
            self.em(eng, fn)
            return None
        v = ctr.inc(1)
        self.em(eng, lambda e, f=fn, c=ctr: f(e).then_inc(c.h, 1))
        return v

    def barrier(self):
        vals = [(c, c.n) for c in self.all_ctrs if c.n]
        for eng in self.ENGS:
            for c, v in vals:
                self.wait(eng, c, v)

    def init_rt(self):
        for eng in ("sync", "gpsimd"):
            def fn(e, en=eng):
                pid = e.partition_id()
                self.rt[en] = {
                    "kv": pid * 262144,
                    "kvh": pid * 131072,
                    "clo": pid * 512,
                    "chi": pid * (-512) + 7680,
                }
            self.em(eng, fn)

    def run(self, block):
        for eng in self.ENGS:
            thunks = self.q[eng]
            def make(ts):
                def body(e):
                    for t in ts:
                        t(e)
                return body
            getattr(block, eng)(make(thunks))


def build_program(dump=(), upto="out", n_exp=E):
    nc = bass.Bass("TRN2", target_bir_lowering=False, debug=False, num_devices=NC)

    xin = nc.dram_tensor("xin", [TOK, H], F32, kind="ExternalInput").ap()
    out = nc.dram_tensor("out", [TOK, H], F32, kind="ExternalOutput").ap()
    wqkv = nc.dram_tensor("wqkv", [48, 128, 2048], F32, kind="ExternalInput").ap()
    wdense = nc.dram_tensor("wdense", [16, 128, 2048], F32, kind="ExternalInput").ap()
    wg = nc.dram_tensor("wg", [E, ICN, 128, 2048], F32, kind="ExternalInput").ap()
    wu = nc.dram_tensor("wu", [E, ICN, 128, 2048], F32, kind="ExternalInput").ap()
    wd = nc.dram_tensor("wd", [E, HCN, 128, 1408], F32, kind="ExternalInput").ap()
    wsg = nc.dram_tensor("wsg", [SCN, 128, 2048], F32, kind="ExternalInput").ap()
    wsu = nc.dram_tensor("wsu", [SCN, 128, 2048], F32, kind="ExternalInput").ap()
    wsd = nc.dram_tensor("wsd", [HCN, 128, 2816], F32, kind="ExternalInput").ap()
    gwt = nc.dram_tensor("gwt", [HCN, 128, 16], F32, kind="ExternalInput").ap()
    escb = nc.dram_tensor("escb", [1, 128], F32, kind="ExternalInput").ap()
    nramp = nc.dram_tensor("nramp", [1, 128], F32, kind="ExternalInput").ap()
    tablep = nc.dram_tensor("tablep", [32, 16], F32, kind="ExternalInput").ap()
    onehot = nc.dram_tensor("onehot", [32, 768], F32, kind="ExternalInput").ap()
    e31 = nc.dram_tensor("e31", [32, 1], F32, kind="ExternalInput").ap()
    identf_d = nc.dram_tensor("identf", [128, 128], F32, kind="ExternalInput").ap()
    cqr = nc.dram_tensor("cqr", [128, 1024], F32, kind="ExternalInput").ap()
    sqr = nc.dram_tensor("sqr", [128, 1024], F32, kind="ExternalInput").ap()
    ckr = nc.dram_tensor("ckr", [128, 1024], F32, kind="ExternalInput").ap()
    skr = nc.dram_tensor("skr", [128, 1024], F32, kind="ExternalInput").ap()

    kvl = nc.dram_tensor("kvl", [SEC], BF16, kind="Internal").ap()
    kvgt = [nc.dram_tensor(f"kvg{i}", [NC * SEC // 2], BF16, kind="Internal") for i in range(2)]
    ctxl = nc.dram_tensor("ctxl", [CTXSECT], BF16, kind="Internal").ap()
    ctxgt = [nc.dram_tensor(f"ctxg{i}", [NC * CTXSECT // 2], BF16, kind="Internal") for i in range(2)]
    biasd = nc.dram_tensor("biasd", [LH * 1408], F32, kind="Internal").ap()
    kvh = nc.dram_tensor("kvh", [LH * 8 * 3 * 128 * 1024], BF16, kind="Internal").ap()
    ctxh = nc.dram_tensor("ctxh", [2048 * 1024], BF16, kind="Internal").ap()
    combd = nc.dram_tensor("combd", [E, TOK], F32, kind="Internal").ap()

    dbg = {}
    for name in dump:
        dbg[name] = nc.dram_tensor("dbg_" + name, [H, TOK], F32, kind="ExternalOutput").ap()

    es = ExitStack()
    with es:
        sb = lambda name, shape, dt: es.enter_context(nc.sbuf_tensor(name, shape, dt))
        xres = [sb(f"xres{i}", [128, 1024], F32) for i in range(HCN)]
        yb = [sb(f"yb{i}", [128, 1024], BF16) for i in range(HCN)]
        inter = sb("inter", [128, 11264], BF16)
        wblk = [sb(f"wblk{i}", [128, 2048], BF16) for i in range(4)]
        cmb = [sb(f"cmb{i}", [128, 1024], F32) for i in range(2)]
        gsb = [sb(f"gsb{i}", [128, 512], F32) for i in range(2)]
        Asb = sb("Asb", [128, 1024], F32)
        Bsb = sb("Bsb", [128, 1024], F32)
        sq2 = [sb(f"sq{i}", [128, 1024], BF16) for i in range(2)]
        identf = sb("identf_sb", [128, 128], F32)
        identb = sb("identb_sb", [128, 128], BF16)
        gwb = sb("gwb", [128, 256], BF16)
        tps = sb("tps", [32, 16], F32)
        ohsb = sb("ohsb", [32, 768], F32)
        e31sb = sb("e31sb", [32, 1], F32)
        biasrow = sb("biasrow", [2, 1408], F32)
        onescolb = sb("onescolb", [128, 1], BF16)
        onesrf = sb("onesrf", [128, 128], F32)
        onesr = lambda b: onesrf.ap()[b:b + 1, 0:128]
        cstC = sb("cstC", [128, 1], F32)
        cstEps = sb("cstEps", [128, 1], F32)
        rows = [sb(f"row{i}", [1, 1024], F32) for i in range(5)]
        R_M, R_V, R_A, R_B, R_T = [r.ap() for r in rows]
        R_DR0, R_DR1 = R_M, R_V
        R_CR0, R_CR1 = R_M, R_V
        farb = sb("farb", [128, 2], F32)
        bbrep = sb("bbrep", [128, 128], F32)
        nrrep = sb("nrrep", [128, 128], F32)
        ga = sb("ga", [128, 128], F32)
        gb = sb("gb", [128, 128], F32)
        gc = sb("gc", [128, 128], F32)
        gd = sb("gd", [128, 128], F32)
        ge = sb("ge", [128, 128], F32)
        gsm = sb("gsm", [128, 32], F32)
        r16 = sb("r16", [16, 1024], F32)
        qsb = sb("qsb", [128, 1024], BF16)
        ksb = sb("ksb", [128, 1024], BF16)
        vsb = sb("vsb", [128, 1024], BF16)
        vt = [sb(f"vt{i}", [128, 128], BF16) for i in range(2)]
        ctxsb = [sb(f"ctxsb{i}", [128, 512], BF16) for i in range(2)]
        outh = [Asb, Bsb]

        w0b = wblk[0].ap()
        qtl = [w0b[:, 0:512], w0b[:, 512:1024]]
        kblk = [w0b[:, 1024:1536], w0b[:, 1536:2048]]
        w1b = wblk[1].ap()
        vch = [w1b[:, i * 128:(i + 1) * 128] for i in range(8)]
        tmpf = [wblk[2].ap().bitcast(F32), wblk[3].ap().bitcast(F32)]
        interf = inter.ap().bitcast(F32)
        cqrep = interf[:, 0:1024]
        sqrep = interf[:, 1024:2048]
        ckrep = interf[:, 2048:3072]
        skrep = interf[:, 3072:4096]
        re1 = interf[:, 4096:4608]
        re2 = interf[:, 4608:5120]
        re2s = interf[:, 5120:5632]
        interb = inter.ap()
        psb = [interb[:, 0:1024], interb[:, 1024:2048]]
        bcsb = [inter.ap().bitcast(F32)[:, 1024:1536], inter.ap().bitcast(F32)[:, 1536:2048]]
        def ichunk(kk, t):
            return interb[:, kk * 1024 + t * 512: kk * 1024 + (t + 1) * 512]

        pA = es.enter_context(nc.psum_tensor("pA", [128, 1024], F32))
        pB = es.enter_context(nc.psum_tensor("pB", [128, 1024], F32))
        pC0 = es.enter_context(nc.psum_tensor("pC0", [128, 512], F32))
        pC1 = es.enter_context(nc.psum_tensor("pC1", [128, 512], F32))
        pD = es.enter_context(nc.psum_tensor("pD", [128, 512], F32))
        pt0 = es.enter_context(nc.psum_tensor("pt0", [128, 128], BF16))

        class Bank:
            def __init__(self, a):
                self._a = a
            def ap(self):
                return self._a
        pb = [Bank(pA.ap()[:, 0:512]), Bank(pA.ap()[:, 512:1024]),
              Bank(pB.ap()[:, 0:512]), Bank(pB.ap()[:, 512:1024]),
              Bank(pC0.ap()), Bank(pC1.ap()), Bank(pD.ap())]

        k = KB(nc)
        for nm in ("sP", "sA", "sV", "sC"):
            getattr(k, nm).h = es.enter_context(nc.semaphore(nm))

        def newsem(nm):
            c = Ctr()
            c.h = es.enter_context(nc.semaphore(nm))
            k.all_ctrs.append(c)
            return c
        s_wb = [newsem(f"s_wb{i}") for i in range(4)]
        s_qtl = [newsem(f"s_qtl{i}") for i in range(2)]
        s_kb = [newsem(f"s_kb{i}") for i in range(2)]
        s_v = [newsem(f"s_v{i}") for i in range(8)]
        s_vb = [newsem(f"s_vb{i}") for i in range(2)]
        s_tb = [newsem(f"s_tb{i}") for i in range(2)]
        s_cr = [newsem(f"s_cr{i}") for i in range(2)]
        s_cst = [[newsem(f"s_cst{j}_{i}") for i in range(2)] for j in range(2)]
        s_vst = [[newsem(f"s_vst{j}_{i}") for i in range(2)] for j in range(2)]
        s_out = [newsem(f"s_out{i}") for i in range(2)]
        s_pc = newsem("s_pc")
        s_gw = newsem("s_gw")
        s_rt = newsem("s_rt")
        s_sw = newsem("s_sw")
        s_qs = [newsem(f"s_qs{i}") for i in range(2)]
        s_ks = [newsem(f"s_ks{i}") for i in range(2)]
        s_dld = newsem("s_dld")
        s_cmw = newsem("s_cmw")
        s_bw = newsem("s_bw")
        s_xl = newsem("s_xl")
        s_kvh = [newsem(f"s_kvh{i}") for i in range(2)]
        s_cxh = newsem("s_cxh")
        s_dbg = newsem("s_dbg")
        k.init_rt()

        def mm(psum_ap, lhsT, rhs, start, stop, inc=False):
            return k.op("tensor", k.sP if inc else None,
                        lambda e, p=psum_ap, l=lhsT, r=rhs, s1=start, s2=stop:
                        e.matmul(p, l, r, start=s1, stop=s2, skip_group_check=True))

        def evac(dst, src, func=AF.Copy, bias=0.0, scale=1.0):
            return k.op("scalar", k.sA,
                        lambda e, d=dst, s=src, f=func, b=bias, sc=scale:
                        e.activation(d, s, f, bias=b, scale=sc))

        def ph_load_x():
            tr_done = {}
            last_pv = {}
            for tc in range(8):
                if tc >= 1:
                    k.wait("sync", k.sP, last_pv[tc - 1])
                k.dma("sync", s_xl, outh[0][:], xin[tc * 128:(tc + 1) * 128, 0:1024])
                lv1 = k.dma("sync", s_xl, outh[1][:], xin[tc * 128:(tc + 1) * 128, 1024:2048])
                k.wait("tensor", s_xl, lv1)
                for hc in range(HCN):
                    src = outh[hc // 8][:, (hc % 8) * 128:(hc % 8 + 1) * 128]
                    bank = pb[hc % 2].ap()[:, 0:128]
                    prev = tr_done.get((tc, hc - 2)) or tr_done.get((tc - 1, hc + 14))
                    k.wait("tensor", k.sA, prev)
                    pv = k.op("tensor", k.sP,
                              lambda e, b=bank, s=src: e.transpose(b, s, identf[:]))
                    last_pv[tc] = pv
                    k.wait("scalar", k.sP, pv)
                    tr_done[(tc, hc)] = evac(xres[hc][:, tc * 128:(tc + 1) * 128], bank)

        def ph_prep():
            sv = {}
            k.dma("sync", s_pc, identf[:], identf_d)
            k.dma("sync", s_pc, tps[:], tablep)
            k.dma("sync", s_pc, ohsb[:], onehot)
            k.dma("sync", s_pc, e31sb[:], e31)
            k.dma("sync", s_pc, R_CR0[:, 0:128], escb)
            pcv = k.dma("sync", s_pc, R_CR1[:, 0:128], nramp)
            for key in ("identf", "tps", "oh", "e31", "escb", "nramp"):
                sv[key] = pcv
            gwv = None
            for c in range(HCN):
                gwv = k.dma("gpsimd", s_gw, gwb[:, c * 16:(c + 1) * 16], gwt[c])
            k.op("vector", None, lambda e: e.memset(onescolb[:], 1.0))
            k.op("vector", None, lambda e: e.memset(cstC[:], -CSTAB))
            k.op("vector", None, lambda e: e.memset(cstEps[:], 1e-5))
            brm = k.op("vector", k.sV, lambda e: e.memset(biasrow[:], NEGB))
            vmem = k.op("vector", k.sV, lambda e: e.memset(onesrf[:], 1.0))
            k.wait("vector", s_pc, sv["identf"])
            k.op("vector", k.sV, lambda e: e.tensor_copy(identb[:], identf[:]))
            k.wait("tensor", s_pc, sv["tps"])
            k.wait("tensor", s_pc, sv["oh"])
            p1 = mm(pb[0].ap()[0:2, :], tps.ap()[0:32, 0:2], ohsb.ap()[0:32, 0:512],
                    True, True, inc=True)
            p2 = mm(pb[1].ap()[0:2, 0:256], tps.ap()[0:32, 0:2], ohsb.ap()[0:32, 512:768],
                    True, True, inc=True)
            k.wait("scalar", k.sP, p1)
            k.wait("scalar", k.sV, brm)
            evac(biasrow.ap()[0:2, 638:1150], pb[0].ap()[0:2, :])
            k.wait("scalar", k.sP, p2)
            a2 = evac(biasrow.ap()[0:2, 1150:1406], pb[1].ap()[0:2, 0:256])
            k.wait("vector", k.sA, a2)
            a2v = k.op("vector", k.sV, lambda e: e.tensor_scalar(
                biasrow.ap()[0:2, 638:1406], biasrow.ap()[0:2, 638:1406], -CSTAB, None, ALU.add))
            k.wait("sync", k.sV, a2v)
            k.dma("sync", s_bw, bass.AP(biasd.tensor, 0, [[1408, 2], [1, 1408]]), biasrow[:])
            k.wait("tensor", s_pc, sv["e31"])
            p3 = mm(pb[2].ap()[0:1, 0:2], e31sb[:], tps.ap()[0:32, 0:2], True, True, inc=True)
            k.wait("scalar", k.sP, p3)
            a3 = evac(R_T[:, 0:2], pb[2].ap()[0:1, 0:2])
            k.wait("tensor", k.sA, a3)
            k.wait("tensor", k.sV, vmem)
            p4 = mm(pb[3].ap()[:, 0:2], onesr(0), R_T[:, 0:2], True, True, inc=True)
            k.wait("scalar", k.sP, p4)
            a4 = evac(farb[:], pb[3].ap()[:, 0:2])
            k.wait("vector", k.sA, a4)
            k.op("vector", k.sV, lambda e: e.tensor_scalar(
                farb[:], farb[:], -CSTAB, None, ALU.add))
            k.wait("tensor", s_pc, sv["escb"])
            p5 = mm(pb[4].ap()[:, 0:128], onesr(0), R_CR0[:, 0:128], True, True, inc=True)
            k.wait("scalar", k.sP, p5)
            evac(bbrep[:], pb[4].ap()[:, 0:128])
            k.wait("tensor", s_pc, sv["nramp"])
            p6 = mm(pb[5].ap()[:, 0:128], onesr(0), R_CR1[:, 0:128], True, True, inc=True)
            k.wait("scalar", k.sP, p6)
            evac(nrrep[:], pb[5].ap()[:, 0:128])
            k.wait("vector", s_gw, gwv)

        def ph_ln():
            cast_done = {}
            for c in range(HCN):
                cast_done[c] = k.op("vector", k.sV,
                                    lambda e, c_=c: e.tensor_copy(yb[c_][:], xres[c_][:]))
            sqmm = {}
            last = None
            for c in range(HCN):
                if c >= 2:
                    k.wait("scalar", k.sP, sqmm[c - 2])
                sq_done = k.op("scalar", k.sA,
                               lambda e, c_=c: e.activation(sq2[c_ % 2][:], xres[c_][:], AF.Square))
                k.wait("tensor", k.sV, cast_done[c])
                k.wait("tensor", k.sA, sq_done)
                for t in range(2):
                    mm(pb[t].ap()[0:1, :], onescolb[:], yb[c][:, t * 512:(t + 1) * 512],
                       c == 0, c == HCN - 1)
                for t in range(2):
                    last = mm(pb[2 + t].ap()[0:1, :], onescolb[:],
                              sq2[c % 2][:, t * 512:(t + 1) * 512],
                              c == 0, c == HCN - 1, inc=(t == 1))
                sqmm[c] = last
            k.wait("vector", k.sP, last)
            def VS(fn):
                v = k.op("vector", k.sV, fn)
                k.wait("vector", k.sV, v)
                return v
            for t in range(2):
                sl = slice(t * 512, (t + 1) * 512)
                VS(lambda e, t_=t, s_=sl: e.tensor_scalar(
                    R_M[:, s_], pb[t_].ap()[0:1, :], 1.0 / H, None, ALU.mult))
                VS(lambda e, t_=t, s_=sl: e.tensor_scalar(
                    R_V[:, s_], pb[2 + t_].ap()[0:1, :], 1.0 / H, None, ALU.mult))
            VS(lambda e: e.tensor_mul(R_T, R_M, R_M))
            vrow = VS(lambda e: e.tensor_sub(R_V, R_V, R_T))
            k.wait("scalar", k.sV, vrow)
            asq = k.op("scalar", k.sA,
                       lambda e: e.activation(R_T, R_V, AF.Sqrt, bias=cstEps.ap()[0:1, 0:1]))
            k.wait("vector", k.sA, asq)
            VS(lambda e: e.reciprocal(R_A, R_T))
            VS(lambda e: e.tensor_mul(R_B, R_M, R_A))
            brow = VS(lambda e: e.tensor_scalar(R_B, R_B, -1.0, None, ALU.mult))
            k.wait("tensor", k.sV, brow)
            for r, rb, dstt in ((R_A, 0, Asb), (R_B, 0, Bsb)):
                for t in range(2):
                    pv = mm(pb[6].ap(), onesr(rb), r[:, t * 512:(t + 1) * 512],
                            True, True, inc=True)
                    k.wait("scalar", k.sP, pv)
                    ev = evac(dstt[:, t * 512:(t + 1) * 512], pb[6].ap())
                    k.wait("tensor", k.sA, ev)
            k.wait("vector", k.sA, ev)
            y_done = {}
            for c in range(HCN):
                mv = k.op("vector", k.sV, lambda e, c_=c: e.tensor_mul(cmb[0][:], xres[c_][:], Asb[:]))
                k.wait("vector", k.sV, mv)
                y_done[c] = k.op("vector", k.sV,
                                 lambda e, c_=c: e.tensor_add(yb[c_][:], cmb[0][:], Bsb[:]))
            return y_done

        def ph_router(y_done):
            lg = None
            for c in range(HCN):
                k.wait("tensor", k.sV, y_done[c])
                for t in range(2):
                    lg = mm(pb[2 + t].ap()[0:16, :], gwb[:, c * 16:(c + 1) * 16],
                            yb[c][:, t * 512:(t + 1) * 512], c == 0, c == HCN - 1,
                            inc=(c == HCN - 1 and t == 1))
            k.wait("scalar", k.sP, lg)
            evac(r16.ap()[0:16, 0:512], pb[2].ap()[0:16, :])
            ev1 = evac(r16.ap()[0:16, 512:1024], pb[3].ap()[0:16, :])
            if "gdbg" in dbg:
                k.wait("sync", k.sA, ev1)
                s1 = k.dma("sync", s_dbg, dbg["gdbg"][11 * 128:11 * 128 + 16, :], r16[:])
                s2 = k.dma("sync", s_dbg, dbg["gdbg"][12 * 128:13 * 128, 0:128], bbrep[:])
                k.wait("tensor", s_dbg, s2)
            sig = {}
            k.wait("tensor", k.sA, ev1)
            for t8 in range(8):
                bank = pb[4 + (t8 % 2)].ap()[:, 0:16]
                if t8 >= 2:
                    k.wait("tensor", k.sA, sig[t8 - 2])
                pv = k.op("tensor", k.sP, lambda e, b=bank, t_=t8: e.transpose(
                    b, r16.ap()[0:16, t_ * 128:(t_ + 1) * 128], identf.ap()[0:16, 0:16]))
                k.wait("scalar", k.sP, pv)
                sig[t8] = k.op("scalar", k.sA, lambda e, b=bank, t_=t8: e.activation(
                    gb[:, t_ * 16:(t_ + 1) * 16], b, AF.Sigmoid))
            k.wait("vector", k.sA, sig[7])
            def V(fn):
                v = k.op("vector", k.sV, fn)
                k.wait("vector", k.sV, v)
                return v
            k.snap(0, gb[:])
            V(lambda e: e.tensor_add(gc[:], gb[:], bbrep[:]))
            k.snap(1, gc[:])
            g4 = lambda t, j: t.ap()[:, j:128:4]
            dsts = [gd.ap()[:, 0:32], gd.ap()[:, 32:64], gd.ap()[:, 64:96],
                    gd.ap()[:, 96:128], ga.ap()[:, 0:32], ga.ap()[:, 32:64]]
            for i, (a, b) in enumerate([(0, 1), (2, 3), (0, 2), (0, 3), (1, 2), (1, 3)]):
                V(lambda e, a_=a, b_=b, d=dsts[i]: e.tensor_tensor(d, g4(gc, a_), g4(gc, b_), ALU.add))
            V(lambda e: e.tensor_max(gd.ap()[:, 0:32], gd.ap()[:, 0:32], gd.ap()[:, 32:64]))
            V(lambda e: e.tensor_max(gd.ap()[:, 64:96], gd.ap()[:, 64:96], gd.ap()[:, 96:128]))
            V(lambda e: e.tensor_max(ga.ap()[:, 0:32], ga.ap()[:, 0:32], ga.ap()[:, 32:64]))
            V(lambda e: e.tensor_max(gd.ap()[:, 64:96], gd.ap()[:, 64:96], ga.ap()[:, 0:32]))
            V(lambda e: e.tensor_max(gd.ap()[:, 0:32], gd.ap()[:, 0:32], gd.ap()[:, 64:96]))
            k.snap(2, gd[:])
            t4 = lambda j: gd.ap()[:, j:32:4]
            V(lambda e: e.tensor_max(gsm.ap()[:, 0:8], t4(0), t4(1)))
            V(lambda e: e.tensor_tensor(gsm.ap()[:, 8:16], t4(0), t4(1), ALU.min))
            V(lambda e: e.tensor_max(gsm.ap()[:, 16:24], t4(2), t4(3)))
            V(lambda e: e.tensor_tensor(gsm.ap()[:, 24:32], t4(2), t4(3), ALU.min))
            V(lambda e: e.tensor_tensor(ga.ap()[:, 0:8], gsm.ap()[:, 0:8], gsm.ap()[:, 16:24], ALU.min))
            V(lambda e: e.tensor_max(ga.ap()[:, 8:16], gsm.ap()[:, 8:16], gsm.ap()[:, 24:32]))
            V(lambda e: e.tensor_max(gsm.ap()[:, 0:8], ga.ap()[:, 0:8], ga.ap()[:, 8:16]))
            k.snap(3, gsm[:], width=32)
            for g in range(4):
                V(lambda e, g_=g: e.tensor_tensor(gd.ap()[:, 32 + g_:64 + g_:4][:, 0:8],
                                                  t4(g_), gsm.ap()[:, 0:8], ALU.is_ge))
            k.snap(4, gd[:])
            for j in range(4):
                V(lambda e, j_=j: e.tensor_copy(ga.ap()[:, j_:128:4], gd.ap()[:, 32:64]))
            k.snap(5, ga[:])
            V(lambda e: e.tensor_scalar(ge[:], ga[:], -1.0, 1.0, ALU.mult, ALU.add))
            V(lambda e: e.tensor_mul(gd[:], gc[:], ga[:]))
            V(lambda e: e.tensor_mul(ge[:], ge[:], nrrep[:]))
            V(lambda e: e.tensor_add(gd[:], gd[:], ge[:]))
            k.snap(6, gd[:])
            V(lambda e: e.memset(gc[:], 0.0))
            for it in range(6):
                V(lambda e: e.tensor_max(ga.ap()[:, 0:64], gd.ap()[:, 0:128:2], gd.ap()[:, 1:128:2]))
                V(lambda e: e.tensor_max(ga.ap()[:, 0:32], ga.ap()[:, 0:64:2], ga.ap()[:, 1:64:2]))
                V(lambda e: e.tensor_max(ga.ap()[:, 0:16], ga.ap()[:, 0:32:2], ga.ap()[:, 1:32:2]))
                V(lambda e: e.tensor_max(gsm.ap()[:, 0:8], ga.ap()[:, 0:16:2], ga.ap()[:, 1:16:2]))
                for t8 in range(8):
                    V(lambda e, t_=t8: e.tensor_scalar(
                        ge.ap()[:, t_ * 16:(t_ + 1) * 16], gd.ap()[:, t_ * 16:(t_ + 1) * 16],
                        gsm.ap()[:, t_:t_ + 1], None, ALU.is_equal))
                V(lambda e: e.tensor_add(gc[:], gc[:], ge[:]))
                V(lambda e: e.tensor_scalar(ge[:], ge[:], -1e9, None, ALU.mult))
                V(lambda e: e.tensor_add(gd[:], gd[:], ge[:]))
            k.snap(7, gc[:])
            V(lambda e: e.tensor_mul(ge[:], gc[:], gb[:]))
            V(lambda e: e.tensor_add(ga.ap()[:, 0:64], ge.ap()[:, 0:128:2], ge.ap()[:, 1:128:2]))
            V(lambda e: e.tensor_add(ga.ap()[:, 0:32], ga.ap()[:, 0:64:2], ga.ap()[:, 1:64:2]))
            V(lambda e: e.tensor_add(ga.ap()[:, 0:16], ga.ap()[:, 0:32:2], ga.ap()[:, 1:32:2]))
            V(lambda e: e.tensor_add(gsm.ap()[:, 8:16], ga.ap()[:, 0:16:2], ga.ap()[:, 1:16:2]))
            V(lambda e: e.tensor_scalar(gsm.ap()[:, 8:16], gsm.ap()[:, 8:16], 1e-20, None, ALU.add))
            V(lambda e: e.reciprocal(gsm.ap()[:, 16:24], gsm.ap()[:, 8:16]))
            k.snap(8, ge[:])
            k.snap(9, gsm[:], width=32)
            comb_ops = {}
            for t8 in range(8):
                comb_ops[t8] = k.op("vector", k.sV, lambda e, t_=t8: e.tensor_scalar(
                    gc.ap()[:, t_ * 16:(t_ + 1) * 16], ge.ap()[:, t_ * 16:(t_ + 1) * 16],
                    gsm.ap()[:, 16 + t_:17 + t_], 1.25, ALU.mult, ALU.mult))
            k.snap(10, gc[:])
            tr_ev = {}
            for t8 in range(8):
                k.wait("tensor", k.sV, comb_ops[t8])
                bank = pb[4 + (t8 % 2)].ap()[0:16, 0:128]
                if t8 >= 2:
                    k.wait("tensor", k.sA, tr_ev[t8 - 2])
                pv = k.op("tensor", k.sP, lambda e, b=bank, t_=t8: e.transpose(
                    b, gc.ap()[:, t_ * 16:(t_ + 1) * 16], identf[:]))
                k.wait("scalar", k.sP, pv)
                tr_ev[t8] = evac(r16.ap()[0:16, t8 * 128:(t8 + 1) * 128], bank)
            k.wait("sync", k.sA, tr_ev[7])
            return k.dma("sync", s_cmw, bass.AP(combd.tensor, 0, [[TOK, 16], [1, TOK]]), r16[:])

        SKIP = set(os.environ.get("KSKIP", "").split(","))
        def ph_experts(comb_stored, y_done):
            k.wait("sync", s_cmw, comb_stored)
            k.wait("vector", k.sV, y_done[HCN - 1])
            wlv = {}
            blk_free = {}
            crow_free = {}
            cmb_free = {}
            sil_prev = {0: None, 1: None}
            h1_prev = {0: None, 1: None}
            iv_prev = {0: None, 1: None}
            iv_last = {0: None, 1: None}
            down_d = {}
            add_d = {}
            bcast_ev = {}
            nb = 0
            for e_ in range(n_exp):
                r = R_CR0 if e_ % 2 == 0 else R_CR1
                rbase = 0
                if e_ >= 2:
                    k.wait("sync", k.sP, crow_free[e_ - 2])
                crow_ld = k.dma("sync", s_cr[e_ % 2], r[:, 0:1024], combd[e_:e_ + 1, :])
                k.wait("tensor", s_cr[e_ % 2], crow_ld)
                if e_ >= 1:
                    k.wait("tensor", k.sA, bcast_ev[e_ - 1])
                ev = None
                for t in range(2):
                    if t == 1:
                        k.wait("tensor", k.sA, ev)
                    pv = mm(pb[6].ap(), onesr(rbase), r[:, t * 512:(t + 1) * 512],
                            True, True, inc=True)
                    k.wait("scalar", k.sP, pv)
                    if e_ >= 2:
                        k.wait("scalar", k.sV, cmb_free[e_ - 2])
                    ev = evac(cmb[e_ % 2][:, t * 512:(t + 1) * 512], pb[6].ap())
                crow_free[e_] = pv
                bcast_ev[e_] = ev
                cmb_ready = ev
                for ic in range(ICN):
                    for warr in (wg, wu):
                        if nb >= 4:
                            k.wait("gpsimd", k.sP, blk_free.get(nb - 4))
                        wlv[nb] = (k.dma("gpsimd", s_wb[nb % 4], wblk[nb % 4][:], warr[e_, ic]), s_wb[nb % 4])
                        nb += 1
                    gslot, uslot = (nb - 2) % 4, (nb - 1) % 4
                    if "gate" in SKIP:
                        blk_free[nb - 2] = None
                        blk_free[nb - 1] = None
                        continue
                    for t in range(2):
                        k.wait("tensor", wlv[nb - 2][1], wlv[nb - 2][0])
                        k.wait("tensor", k.sA, sil_prev[t])
                        for c in range(HCN):
                            gmm = mm(pb[t].ap(), wblk[gslot][:, c * 128:(c + 1) * 128],
                                     yb[c][:, t * 512:(t + 1) * 512], c == 0, c == HCN - 1,
                                     inc=(c == HCN - 1))
                        if "silu" in SKIP:
                            sil_prev[t] = gmm and k.sA.n
                            iv_prev[t] = k.sV.n
                            iv_last[t] = iv_prev[t]
                            if t == 1:
                                blk_free[nb - 2] = gmm
                                blk_free[nb - 1] = gmm
                            if ic == ICN - 1 and t == 1:
                                cmb_free[e_] = k.sV.n
                            continue
                        k.wait("scalar", k.sP, gmm)
                        k.wait("scalar", k.sV, iv_prev[t])
                        sil_prev[t] = k.op("scalar", k.sA, lambda e, t_=t: e.activation(
                            gsb[t_][:], pb[t_].ap(), AF.Silu))
                        if "up" in SKIP:
                            if t == 1:
                                blk_free[nb - 2] = gmm
                                blk_free[nb - 1] = gmm
                            iv_prev[t] = sil_prev[t] and k.sV.n
                            iv_last[t] = iv_prev[t]
                            if ic == ICN - 1 and t == 1:
                                cmb_free[e_] = k.sV.n
                            continue
                        k.wait("tensor", wlv[nb - 1][1], wlv[nb - 1][0])
                        k.wait("tensor", k.sV, h1_prev[t])
                        for c in range(HCN):
                            umm = mm(pb[2 + t].ap(), wblk[uslot][:, c * 128:(c + 1) * 128],
                                     yb[c][:, t * 512:(t + 1) * 512], c == 0, c == HCN - 1,
                                     inc=(c == HCN - 1))
                        if t == 1:
                            blk_free[nb - 2] = umm
                            blk_free[nb - 1] = umm
                        k.wait("vector", k.sP, umm)
                        k.wait("vector", k.sA, sil_prev[t])
                        k.wait("vector", k.sA, cmb_ready)
                        if e_ >= 1 and ic == 0:
                            k.wait("vector", k.sP, down_d[(e_ - 1, t)])
                        if "h1inplace" in SKIP:
                            h1_prev[t] = k.op("vector", k.sV, lambda e, t_=t: e.tensor_copy(
                                cmb[0][:, 0:512], pb[2 + t_].ap()))
                            k.wait("vector", k.sV, h1_prev[t])
                            k.op("vector", None, lambda e, t_=t: e.tensor_mul(
                                gsb[t_][:], cmb[0][:, 0:512], gsb[t_][:]))
                        else:
                            h1_prev[t] = k.op("vector", k.sV, lambda e, t_=t: e.tensor_mul(
                                gsb[t_][:], pb[2 + t_].ap(), gsb[t_][:]))
                        k.wait("vector", k.sV, h1_prev[t])
                        iv_prev[t] = k.op("vector", k.sV, lambda e, t_=t, ic_=ic, e2=e_:
                                          e.tensor_mul(ichunk(ic_, t_), gsb[t_][:],
                                                       cmb[e2 % 2][:, t_ * 512:(t_ + 1) * 512]))
                        iv_last[t] = iv_prev[t]
                        if ic == ICN - 1 and t == 1:
                            cmb_free[e_] = iv_prev[t]
                if "down" in SKIP:
                    for t in range(2):
                        down_d[(e_, t)] = iv_last[t] and k.sP.n
                    continue
                for hc in range(HCN):
                    if nb >= 4:
                        k.wait("gpsimd", k.sP, blk_free[nb - 4])
                    wlv[nb] = (k.dma("gpsimd", s_wb[nb % 4], wblk[nb % 4][:, 0:1408], wd[e_, hc]), s_wb[nb % 4])
                    nb += 1
                    slot = (nb - 1) % 4
                    k.wait("tensor", wlv[nb - 1][1], wlv[nb - 1][0])
                    for t in range(2):
                        bank = pb[4 + t]
                        k.wait("tensor", k.sV, add_d.get((hc - 1, t)))
                        if hc == 0:
                            k.wait("tensor", k.sV, iv_last[t])
                            if e_ >= 1:
                                k.wait("tensor", k.sV, add_d.get((HCN - 1, t)))
                        for ic in range(ICN):
                            pv = mm(bank.ap(), wblk[slot][:, ic * 128:(ic + 1) * 128],
                                    ichunk(ic, t), ic == 0, ic == ICN - 1, inc=(ic == ICN - 1))
                        down_d[(e_, t)] = pv
                        if t == 1:
                            blk_free[nb - 1] = pv
                        k.wait("vector", k.sP, pv)
                        add_d[(hc, t)] = k.op("vector", k.sV,
                            lambda e, hc_=hc, t_=t, b=bank: e.tensor_add(
                                xres[hc_][:, t_ * 512:(t_ + 1) * 512],
                                xres[hc_][:, t_ * 512:(t_ + 1) * 512], b.ap()))

        def ph_shared(y_done):
            k.wait("vector", k.sV, y_done[HCN - 1])
            wlv = {}
            blk_free = {}
            sil_d = {}
            h1_d = {}
            iv_d = {}
            add_d = {}
            down_half = {}
            nb = 0
            for half in range(2):
                for sc in range(11):
                    for warr in (wsg, wsu):
                        if nb >= 4:
                            k.wait("gpsimd", k.sP, blk_free[nb - 4])
                        wlv[nb] = (k.dma("gpsimd", s_wb[nb % 4], wblk[nb % 4][:], warr[half * 11 + sc]), s_wb[nb % 4])
                        nb += 1
                    gslot, uslot = (nb - 2) % 4, (nb - 1) % 4
                    for t in range(2):
                        k.wait("tensor", wlv[nb - 2][1], wlv[nb - 2][0])
                        k.wait("tensor", k.sA, sil_d.get((half, sc - 1, t)))
                        if sc == 0 and half == 1:
                            k.wait("tensor", k.sA, sil_d.get((0, 10, t)))
                        for c in range(HCN):
                            gmm = mm(pb[t].ap(), wblk[gslot][:, c * 128:(c + 1) * 128],
                                     yb[c][:, t * 512:(t + 1) * 512], c == 0, c == HCN - 1,
                                     inc=(c == HCN - 1))
                        k.wait("scalar", k.sP, gmm)
                        k.wait("scalar", k.sV, iv_d.get((half, sc - 1, t)))
                        if sc == 0 and half == 1:
                            k.wait("scalar", k.sV, iv_d.get((0, 10, t)))
                        sil_d[(half, sc, t)] = k.op("scalar", k.sA, lambda e, t_=t:
                                                    e.activation(gsb[t_][:], pb[t_].ap(), AF.Silu))
                        k.wait("tensor", wlv[nb - 1][1], wlv[nb - 1][0])
                        k.wait("tensor", k.sV, h1_d.get((half, sc - 1, t)))
                        if sc == 0 and half == 1:
                            k.wait("tensor", k.sV, h1_d.get((0, 10, t)))
                        for c in range(HCN):
                            umm = mm(pb[2 + t].ap(), wblk[uslot][:, c * 128:(c + 1) * 128],
                                     yb[c][:, t * 512:(t + 1) * 512], c == 0, c == HCN - 1,
                                     inc=(c == HCN - 1))
                        if t == 1:
                            blk_free[nb - 2] = umm
                            blk_free[nb - 1] = umm
                        k.wait("vector", k.sP, umm)
                        k.wait("vector", k.sA, sil_d[(half, sc, t)])
                        if half == 1 and sc == 0:
                            k.wait("vector", k.sP, down_half[(0, t)])
                        h1_d[(half, sc, t)] = k.op("vector", k.sV, lambda e, t_=t:
                                                   e.tensor_mul(gsb[t_][:], pb[2 + t_].ap(), gsb[t_][:]))
                        k.wait("vector", k.sV, h1_d[(half, sc, t)])
                        iv_d[(half, sc, t)] = k.op("vector", k.sV, lambda e, t_=t, sc_=sc:
                                                   e.tensor_copy(ichunk(sc_, t_), gsb[t_][:]))
                for hc in range(HCN):
                    if nb >= 4:
                        k.wait("gpsimd", k.sP, blk_free[nb - 4])
                    wlv[nb] = (k.dma("gpsimd", s_wb[nb % 4], wblk[nb % 4][:, 0:1408],
                                     bass.AP(wsd.tensor, hc * 128 * 2816 + half * 1408,
                                             [[2816, 128], [1, 1408]])), s_wb[nb % 4])
                    nb += 1
                    slot = (nb - 1) % 4
                    k.wait("tensor", wlv[nb - 1][1], wlv[nb - 1][0])
                    for t in range(2):
                        bank = pb[4 + t]
                        k.wait("tensor", k.sV, add_d.get((half, hc - 1, t)))
                        if hc == 0:
                            k.wait("tensor", k.sV, iv_d[(half, 10, t)])
                            if half == 1:
                                k.wait("tensor", k.sV, add_d.get((0, HCN - 1, t)))
                        for sc in range(11):
                            pv = mm(bank.ap(), wblk[slot][:, sc * 128:(sc + 1) * 128],
                                    ichunk(sc, t), sc == 0, sc == 10, inc=(sc == 10))
                        down_half[(half, t)] = pv
                        if t == 1:
                            blk_free[nb - 1] = pv
                        k.wait("vector", k.sP, pv)
                        add_d[(half, hc, t)] = k.op("vector", k.sV,
                            lambda e, hc_=hc, t_=t, b=bank: e.scalar_tensor_tensor(
                                xres[hc_][:, t_ * 512:(t_ + 1) * 512], b.ap(), 0.5,
                                xres[hc_][:, t_ * 512:(t_ + 1) * 512], ALU.mult, ALU.add))

        def ph_qkv(y_done):
            for dst, src in ((cqrep, cqr), (sqrep, sqr), (ckrep, ckr), (skrep, skr)):
                rp = k.dma("sync", s_rt, dst, src)
            k.wait("vector", s_rt, rp)
            k.wait("tensor", k.sV, y_done[HCN - 1])
            wlv = {}
            blk_free = {}
            mmv = {}
            bank_free = {}
            st_qk = {}
            st_v = {}
            vtr = {}
            vtr_ev = {}
            vst = {}
            rv_prev = None
            prev_qo = {}
            prev_vo = [None]
            OSEQ = []
            for par in range(2):
                for hi in range(8):
                    hh = 2 * hi + par
                    OSEQ += [hh, 16 + hh, 32 + hh]
            for o in range(48):
                if o >= 4:
                    k.wait("gpsimd", k.sP, blk_free[o - 4])
                wlv[o] = k.dma("gpsimd", s_wb[o % 4], wblk[o % 4][:], wqkv[OSEQ[o]])
                k.wait("tensor", s_wb[o % 4], wlv[o])
                if o == 24:
                    emit_gather(0)
                for t in range(2):
                    bank = pb[(2 * o + t) % 4]
                    k.wait("tensor", k.sV, bank_free.get((o - 2, t)))
                    for c in range(HCN):
                        pv = mm(bank.ap(), wblk[o % 4][:, c * 128:(c + 1) * 128],
                                yb[c][:, t * 512:(t + 1) * 512], c == 0, c == HCN - 1,
                                inc=(c == HCN - 1))
                    mmv[(o, t)] = pv
                blk_free[o] = mmv[(o, 1)]
                h = OSEQ[o] % 16
                if OSEQ[o] < 32:
                    isq = OSEQ[o] < 16
                    crep, srep = (cqrep, sqrep) if isq else (ckrep, skrep)
                    dstt = qsb if isq else ksb
                    sect = (h % 2) * 3145728 + (0 if isq else 1) * 1048576 + (h // 2) * 131072
                    po = prev_qo.get(isq)
                    if po is not None:
                        k.wait("vector", st_qk[po][1], st_qk[po][0])
                    rv = None
                    for t in range(2):
                        bank = pb[(2 * o + t) % 4]
                        k.wait("vector", k.sP, mmv[(o, t)])
                        k.op("vector", k.sV, lambda e, b=bank, c_=crep, t_=t: e.tensor_mul(
                            re1, b.ap(), c_[:, t_ * 512:(t_ + 1) * 512]))
                        e2v = k.op("vector", k.sV, lambda e, b=bank, s_=srep, t_=t: e.tensor_mul(
                            re2, b.ap(), s_[:, t_ * 512:(t_ + 1) * 512]))
                        bank_free[(o, t)] = e2v
                        k.wait("sync", k.sV, e2v)
                        if rv_prev:
                            k.wait("sync", k.sV, rv_prev)
                        k.dma("sync", s_sw, re2s[0:64, :], re2[64:128, :])
                        sw1 = k.dma("sync", s_sw, re2s[64:128, :], re2[0:64, :])
                        k.wait("vector", s_sw, sw1)
                        k.wait("vector", k.sV, e2v)
                        if isq:
                            od0 = dstt.ap()[0:64, t * 512:(t + 1) * 512]
                            od1 = dstt.ap()[64:128, t * 512:(t + 1) * 512]
                        else:
                            rs = slice(t * 512 + 511, t * 512 - 1 if t > 0 else None, -1)
                            od0 = dstt.ap()[0:64, rs]
                            od1 = dstt.ap()[64:128, rs]
                        k.op("vector", None, lambda e, d=od0: e.tensor_sub(d, re1[0:64, :], re2s[0:64, :]))
                        rv = k.op("vector", k.sV, lambda e, d=od1: e.tensor_add(
                            d, re1[64:128, :], re2s[64:128, :]))
                        rv_prev = rv
                    k.wait("sync", k.sV, rv)
                    ssem = s_qs[h % 2] if isq else s_ks[h % 2]
                    st_qk[o] = (k.dma("sync", ssem, bass.AP(kvl.tensor, sect, [[TOK, 128], [1, TOK]]),
                                      dstt[:]), ssem)
                    prev_qo[isq] = o
                else:
                    ev = None
                    for t in range(2):
                        bank = pb[(2 * o + t) % 4]
                        k.wait("vector", k.sP, mmv[(o, t)])
                        if prev_vo[0] is not None and t == 0:
                            k.wait("vector", k.sP, vtr.get((prev_vo[0], 7)))
                        rs = slice(t * 512 + 511, t * 512 - 1 if t > 0 else None, -1)
                        ev = k.op("vector", k.sV, lambda e, b=bank, r_=rs: e.tensor_copy(
                            vsb.ap()[:, r_], b.ap()))
                        bank_free[(o, t)] = ev
                    k.wait("tensor", k.sV, ev)
                    sect = (h % 2) * 3145728 + 2 * 1048576 + (h // 2) * 131072
                    for tc in range(8):
                        k.wait("tensor", k.sA, vtr_ev.get((o, tc - 1)))
                        if tc == 0 and prev_vo[0] is not None:
                            k.wait("tensor", k.sA, vtr_ev.get((prev_vo[0], 7)))
                        pv = k.op("tensor", k.sP, lambda e, tc_=tc: e.transpose(
                            pt0.ap(), vsb[:, tc_ * 128:(tc_ + 1) * 128], identb[:]))
                        vtr[(o, tc)] = pv
                        k.wait("scalar", k.sP, pv)
                        pvst = vst.get((o, tc - 2)) or (vst.get((prev_vo[0], tc + 6)) if prev_vo[0] is not None else None)
                        if pvst:
                            k.wait("scalar", pvst[1], pvst[0])
                        ve = evac(vt[tc % 2][:], pt0.ap())
                        vtr_ev[(o, tc)] = ve
                        k.wait("sync", k.sA, ve)
                        vst[(o, tc)] = (k.dma("sync", s_vst[h % 2][tc % 2], bass.AP(
                            kvl.tensor, sect + tc * 128 * 128, [[128, 128], [1, 128]]),
                            vt[tc % 2][:]), s_vst[h % 2][tc % 2])
                    st_v[o] = vst[(o, 7)][0]
                    prev_vo[0] = o

        SECH = 3145728
        def emit_gather(par):
            for c_ in (s_qs[par], s_ks[par], s_vst[par][0], s_vst[par][1]):
                k.wait("gpsimd", c_, c_.n)
            cv = k.op("gpsimd", k.sC, lambda e, p_=par: e.collective_compute(
                "AllGather", ALU.bypass, [list(range(NC))],
                [bass.AP(kvl.tensor, p_ * SECH, [[1024, SECH // 1024], [1, 1024]]).opt()],
                [bass.AP(kvgt[p_], 0, [[1024, NC * SECH // 1024], [1, 1024]]).opt()]))
            k.wait("gpsimd", k.sC, cv)
            k.dma_dyn("gpsimd", s_kvh[par],
                      bass.AP(kvh.tensor, par * 3145728, [[1, 3145728]]),
                      kvgt[par], 0, "kvh",
                      ((SECH, 8), (1048576, 3), (1024, 128), (1, 1024)))

        def ph_gather_kv():
            emit_gather(1)
        

        CHS = CTXSECT // 2
        def emit_gather_ctx(par):
            for c_ in s_cst[par]:
                k.wait("gpsimd", c_, c_.n)
            cv = k.op("gpsimd", k.sC, lambda e, p_=par: e.collective_compute(
                "AllGather", ALU.bypass, [list(range(NC))],
                [bass.AP(ctxl.tensor, p_ * CHS, [[1024, CHS // 1024], [1, 1024]]).opt()],
                [bass.AP(ctxgt[p_], 0, [[1024, NC * CHS // 1024], [1, 1024]]).opt()]))
            k.wait("gpsimd", k.sC, cv)
            k.dma_dyn("gpsimd", s_cxh,
                      bass.AP(ctxh.tensor, par * 131072, [[262144, 8], [1024, 128], [1, 512]]),
                      ctxgt[par], 0, "clo", ((CHS, 8), (S, 128), (1, 512)))
            k.dma_dyn("gpsimd", s_cxh,
                      bass.AP(ctxh.tensor, par * 131072 + 512, [[262144, 8], [1024, 128], [1, 512]]),
                      ctxgt[par], 0, "chi", ((CHS, 8), (S, 128), (1, 512)))

        def ph_gather_ctx():
            emit_gather_ctx(1)
            k.wait("sync", s_cxh, 64)
        

        def ph_flash():
            GC = 0
            KB_ = 0
            qkj = {}
            pvden = {}
            expd = {}
            muld = {}
            stored = {}
            recipd = {}
            bcastd = {}
            bevacd = {}
            pair_end = {}
            blk_qk = {}
            kblk_ld = {}
            vbld = {}
            blk_pv = {}
            kb_of = {}
            JD = 0
            for lh in range(LH):
                tb = k.dma("sync", s_tb[lh], cmb[lh][:],
                           bass.AP(biasd.tensor, lh * 1408 + 127, [[1, 128], [1, 1024]]))
                k.wait("vector", s_tb[lh], tb)
                k.wait("sync", s_kvh[lh], 16)
                k.wait("gpsimd", s_kvh[lh], 16)
                for qt in range(QTN):
                    p = lh * QTN + qt
                    nm = 4 * qt + 4
                    nj = nm // 2
                    rq = qt if qt <= 7 else 15 - qt
                    halfq = 0 if qt <= 7 else 1
                    if p >= 2:
                        k.wait("sync", k.sP, pair_end[p - 2])
                    qtl_ld = k.dma("sync", s_qtl[p % 2], qtl[p % 2],
                                   bass.AP(kvh.tensor,
                                           lh * 3145728 + rq * 3 * 131072 + halfq * 512,
                                           [[1024, 128], [1, 512]]))
                    ctx_bank = (pC0 if p % 2 == 0 else pC1)
                    scb = [pA, pB]
                    base_JD = JD

                    def emit_qk(j, JDj):
                        nonlocal GC, KB_
                        for half in range(2):
                            m = 2 * j + half
                            J0 = 7680 - 512 * qt + 128 * m
                            b = 15 - J0 // 512
                            r = b if b <= 7 else 15 - b
                            hv = 0 if b <= 7 else 1
                            if m % 4 == 0:
                                if KB_ >= 2:
                                    k.wait("sync", k.sP, blk_qk[KB_ - 2])
                                kblk_ld[KB_] = k.dma(
                                    "sync", s_kb[KB_ % 2], kblk[KB_ % 2],
                                    bass.AP(kvh.tensor,
                                            lh * 3145728 + (r * 3 + 1) * 131072 + hv * 512,
                                            [[1024, 128], [1, 512]]))
                                kb_of[GC // 4] = KB_
                                VB = KB_
                                if VB >= 2:
                                    k.wait("sync", k.sP, blk_pv.get(VB - 2, 0))
                                vbld[VB] = k.dma("sync", s_vb[VB % 2],
                                                 w1b[:, (VB % 2) * 512:(VB % 2 + 1) * 512],
                                                 bass.AP(kvh.tensor,
                                                         lh * 3145728 + (r * 3 + 2) * 131072
                                                         + hv * 65536,
                                                         [[128, 128], [16384, 4], [1, 128]]))
                                KB_ += 1
                            if m == 0:
                                k.wait("tensor", s_qtl[p % 2], qtl_ld)
                                k.wait("tensor", k.sV, muld.get(p - 2))
                            if m % 4 == 0:
                                k.wait("tensor", s_kb[kb_of[GC // 4] % 2], kblk_ld[kb_of[GC // 4]])
                            if half == 0:
                                k.wait("tensor", k.sA, expd.get(JDj - 2))
                            sc = scb[JDj % 2].ap()[:, half * 512:(half + 1) * 512]
                            v_ = mm(sc, kblk[kb_of[GC // 4] % 2][:, (m % 4) * 128:(m % 4 + 1) * 128],
                                    qtl[p % 2], True, True, inc=(half == 1))
                            if m % 4 == 3 or m == nm - 1:
                                blk_qk[kb_of[GC // 4]] = k.sP.n
                            GC += 1
                        qkj[JDj] = v_
                        near0, near1 = (2 * j) <= 4, (2 * j + 1) <= 4
                        if near0 or near1:
                            k.wait("vector", k.sP, qkj[JDj])
                            k.wait("vector", k.sA, expd.get(JDj - 2))
                            for half in range(2):
                                m = 2 * j + half
                                if m <= 4:
                                    k.op("vector", k.sV, lambda e, m_=m, lh_=lh, jj=JDj, h_=half:
                                         e.tensor_add(tmpf[jj % 2][:, h_ * 512:(h_ + 1) * 512],
                                                      scb[jj % 2].ap()[:, h_ * 512:(h_ + 1) * 512],
                                                      cmb[lh_][:, m_ * 128:m_ * 128 + 512]))
                            av = k.sV.n
                            k.wait("scalar", k.sV, av)
                            k.wait("scalar", k.sP, pvden.get(JDj - 2))
                            if near0 and near1:
                                expd[JDj] = k.op("scalar", k.sA, lambda e, jj=JDj: e.activation(
                                    psb[jj % 2], tmpf[jj % 2][:, 0:1024], AF.Exp))
                            else:
                                k.op("scalar", k.sA, lambda e, jj=JDj: e.activation(
                                    psb[jj % 2][:, 0:512], tmpf[jj % 2][:, 0:512], AF.Exp))
                                expd[JDj] = k.op("scalar", k.sA, lambda e, jj=JDj, lh_=lh: e.activation(
                                    psb[jj % 2][:, 512:1024],
                                    scb[jj % 2].ap()[:, 512:1024], AF.Exp,
                                    bias=farb.ap()[:, lh_:lh_ + 1]))
                        else:
                            k.wait("scalar", k.sP, qkj[JDj])
                            k.wait("scalar", k.sP, pvden.get(JDj - 2))
                            expd[JDj] = k.op("scalar", k.sA, lambda e, jj=JDj, lh_=lh: e.activation(
                                psb[jj % 2], scb[jj % 2].ap()[:, 0:1024], AF.Exp,
                                bias=farb.ap()[:, lh_:lh_ + 1]))

                    def emit_pvden(j, JDj):
                        k.wait("tensor", k.sA, expd[JDj])
                        if j == 0:
                            k.wait("tensor", k.sA, bevacd.get(p - 1))
                            k.wait("tensor", k.sV, recipd.get(p - 1))
                        for half in range(2):
                            m = 2 * j + half
                            gci = 2 * JDj - 2 * base_JD + half + pair_gc0
                            vb = kb_of[gci // 4]
                            k.wait("tensor", s_vb[vb % 2], vbld[vb])
                            vtile = w1b[:, (vb % 2) * 512 + (gci % 4) * 128:
                                        (vb % 2) * 512 + (gci % 4) * 128 + 128]
                            mm(ctx_bank.ap(), vtile, psb[JDj % 2][:, half * 512:(half + 1) * 512],
                               m == 0, m == nm - 1)
                            v_ = mm(pD.ap()[0:1, :], onescolb[:],
                                    psb[JDj % 2][:, half * 512:(half + 1) * 512],
                                    m == 0, m == nm - 1, inc=(half == 1))
                        pvden[JDj] = v_
                        blk_pv[vb] = v_

                    pair_gc0 = GC
                    emit_qk(0, JD)
                    for j in range(nj):
                        if j + 1 < nj:
                            emit_qk(j + 1, JD + 1)
                        emit_pvden(j, JD)
                        JD += 1
                    denr = R_DR0 if p % 2 == 0 else R_DR1
                    drb = 0
                    k.wait("vector", k.sP, pvden[JD - 1])
                    k.wait("vector", k.sP, bcastd.get(p - 1))
                    recipd[p] = k.op("vector", k.sV, lambda e, d=denr:
                                     e.reciprocal(d[:, 0:512], pD.ap()[0:1, :]))
                    k.wait("tensor", k.sV, recipd[p])
                    bcastd[p] = mm(pD.ap(), onesr(drb), denr[:, 0:512], True, True, inc=True)
                    k.wait("scalar", k.sP, bcastd[p])
                    k.wait("scalar", k.sV, muld.get(p - 2))
                    bevacd[p] = evac(bcsb[p % 2], pD.ap())
                    k.wait("vector", k.sA, bevacd[p])
                    if stored.get(p - 2):
                        k.wait("vector", stored[p - 2][1], stored[p - 2][0])
                    muld[p] = k.op("vector", k.sV, lambda e, p_=p, cb=ctx_bank: e.tensor_mul(
                        ctxsb[p_ % 2][:], cb.ap(), bcsb[p_ % 2]))
                    k.wait("sync", k.sV, muld[p])
                    stored[p] = (k.dma("sync", s_cst[lh][p % 2], bass.AP(
                        ctxl.tensor, lh * HD * S + 512 * qt, [[S, 128], [1, 512]]),
                        ctxsb[p % 2][:]), s_cst[lh][p % 2])
                    pair_end[p] = bcastd[p]
                if lh == 0:
                    emit_gather_ctx(0)

        def ph_dense():
            wlv = {}
            mm_done = {}
            add_d = {}
            ld = {}
            HSEQ = list(range(16))
            for i in range(HCN + 1):
                if i < HCN:
                    hcl = HSEQ[i]
                    if i >= 4:
                        k.wait("gpsimd", k.sP, mm_done[i - 4])
                    wlv[i] = k.dma("gpsimd", s_wb[i % 4], wblk[i % 4][:], wdense[hcl])
                    k.wait("sync", s_cxh, 64)
                    ld[i] = k.dma("sync", s_dld, yb[hcl][:],
                                  bass.AP(ctxh.tensor, hcl * 131072, [[1024, 128], [1, 1024]]))
                if i >= 1:
                    ii = i - 1
                    hc2 = HSEQ[ii]
                    k.wait("tensor", s_wb[ii % 4], wlv[ii])
                    if ii == 0:
                        k.wait("tensor", s_dld, 16 * 16)
                    for t in range(2):
                        bank = pb[(2 * ii + t) % 4]
                        k.wait("tensor", k.sV, add_d.get((ii - 2, t)))
                        for c in range(HCN):
                            pv = mm(bank.ap(), wblk[ii % 4][:, c * 128:(c + 1) * 128],
                                    yb[c][:, t * 512:(t + 1) * 512], c == 0, c == HCN - 1,
                                    inc=(c == HCN - 1))
                        mm_done[ii] = pv
                        k.wait("vector", k.sP, pv)
                        add_d[(ii, t)] = k.op("vector", k.sV,
                            lambda e, h_=hc2, t_=t, b=bank: e.tensor_add(
                                xres[h_][:, t_ * 512:(t_ + 1) * 512],
                                xres[h_][:, t_ * 512:(t_ + 1) * 512], b.ap()))

        def ph_out():
            tr_done = {}
            st = {}
            for tc in range(8):
                for hc in range(HCN):
                    src = xres[hc].ap()[:, tc * 128:(tc + 1) * 128]
                    bank = pb[hc % 2].ap()[:, 0:128]
                    prev = tr_done.get((tc, hc - 2)) or tr_done.get((tc - 1, hc + 14))
                    k.wait("tensor", k.sA, prev)
                    if hc == 0 and tc >= 1:
                        k.wait("tensor", s_out[(tc - 1) % 2], st[(tc - 1, 1)])
                    pv = k.op("tensor", k.sP,
                              lambda e, b=bank, s=src: e.transpose(b, s, identf[:]))
                    k.wait("scalar", k.sP, pv)
                    tr_done[(tc, hc)] = evac(
                        outh[hc // 8][:, (hc % 8) * 128:(hc % 8 + 1) * 128], bank)
                k.wait("sync", k.sA, tr_done[(tc, HCN - 1)])
                st[(tc, 0)] = k.dma("sync", s_out[tc % 2], out[tc * 128:(tc + 1) * 128, 0:1024], outh[0][:])
                st[(tc, 1)] = k.dma("sync", s_out[tc % 2], out[tc * 128:(tc + 1) * 128, 1024:2048], outh[1][:])
            k.wait("sync", s_out[0], s_out[0].n)
            k.wait("sync", s_out[1], s_out[1].n)

        def dump_fm(name, tiles):
            if name not in dbg:
                return
            k.barrier()
            for hc in range(HCN):
                vv = k.op("vector", k.sV, lambda e, h_=hc: e.tensor_copy(cmb[0][:], tiles[h_][:]))
                k.wait("sync", k.sV, vv)
                sv = k.dma("sync", s_dbg, dbg[name][hc * 128:(hc + 1) * 128, :], cmb[0][:])
                k.wait("vector", s_dbg, sv)
            k.barrier()

        def dump_dram(name, tensor, nelem):
            if name not in dbg:
                return
            k.barrier()
            n = min(nelem, H * TOK)
            dv = k.dma("sync", s_dbg, bass.AP(dbg[name].tensor, 0, [[1, n]]), bass.AP(tensor, 0, [[1, n]]))
            k.wait("sync", s_dbg, dv)
            k.barrier()

        def snap(idx, tile, width=128):
            if "gdbg" not in dbg:
                return
            k.wait("sync", k.sV, k.sV.n)
            k.wait("sync", k.sA, k.sA.n)
            k.wait("sync", k.sP, k.sP.n)
            sv = k.dma("sync", s_dbg, dbg["gdbg"][idx * 128:idx * 128 + 128, 0:width], tile)
            k.wait("vector", s_dbg, sv)
            k.wait("scalar", s_dbg, sv)
            k.wait("tensor", s_dbg, sv)

        k.snap = snap

        order = ["prep", "load", "ln1", "router1", "experts1", "shared1",
                 "ln2", "qkv", "gkv", "flash", "gctx", "dense",
                 "ln3", "router2", "experts2", "shared2", "out"]
        stop = order.index(upto)

        def minibar():
            vals = [(k.sP, k.sP.n), (k.sA, k.sA.n), (k.sV, k.sV.n)]
            for eng in KB.ENGS:
                for c, v in vals:
                    k.wait(eng, c, v)

        def phase(nm, fn, bar=True):
            if order.index(nm) <= stop:
                res = fn()
                if bar:
                    k.barrier()
                else:
                    minibar()
                return res
            return None

        phase("prep", ph_prep)
        phase("load", ph_load_x)
        if "xres0" in dbg:
            k.barrier()
            dump_fm("xres0", xres)
        y1 = phase("ln1", ph_ln)
        if y1 is not None:
            dump_fm("ln1", yb)
        if y1 is not None and "lndbg" in dbg:
            k.barrier()
            d = dbg["lndbg"]
            k.dma("sync", s_dbg, d[0:128, :], Asb[:])
            k.dma("sync", s_dbg, d[128:256, :], Bsb[:])
            k.dma("sync", s_dbg, d[256:257, :], R_M)
            k.dma("sync", s_dbg, d[257:258, :], R_V)
            k.dma("sync", s_dbg, d[258:259, :], R_A)
            k.dma("sync", s_dbg, d[259:260, :], R_B)
            sv = k.dma("sync", s_dbg, d[260:261, :], R_T)
            k.wait("sync", s_dbg, sv)
            k.barrier()
        cs1 = phase("router1", lambda: ph_router(y1))
        if cs1 is not None:
            dump_dram("comb1", combd.tensor, E * TOK)
        phase("experts1", lambda: ph_experts(cs1, y1))
        phase("shared1", lambda: ph_shared(y1))
        dump_fm("x1", xres)
        y2 = phase("ln2", ph_ln)
        if y2 is not None:
            dump_fm("ln2", yb)
        phase("qkv", lambda: ph_qkv(y2), bar=False)
        if order.index("qkv") <= stop:
            dump_dram("kvl", kvl.tensor, SEC)
        phase("gkv", ph_gather_kv, bar=False)
        phase("flash", ph_flash, bar=False)
        if order.index("flash") <= stop:
            dump_dram("ctxl", ctxl.tensor, CTXSECT)
        phase("gctx", ph_gather_ctx, bar=False)
        phase("dense", ph_dense)
        dump_fm("x2", xres)
        y3 = phase("ln3", ph_ln)
        cs2 = phase("router2", lambda: ph_router(y3))
        phase("experts2", lambda: ph_experts(cs2, y3))
        phase("shared2", lambda: ph_shared(y3))
        dump_fm("x3", xres)
        phase("out", ph_out)

        block = es.enter_context(nc.Block())
        k.run(block)
    return nc



def _retile(inputs):
    f = np.float32
    gp = np.asarray(inputs["L0_ffn_gate_proj"], f)
    up = np.asarray(inputs["L0_ffn_up_proj"], f)
    dp = np.asarray(inputs["L0_ffn_down_proj"], f)
    sg = np.asarray(inputs["L0_ffn_shared_gate_w"], f)
    su = np.asarray(inputs["L0_ffn_shared_up_w"], f)
    sd = np.asarray(inputs["L0_ffn_shared_down_w"], f)
    qkv = np.asarray(inputs["L0_attn_qkv_w"], f)
    dw = np.asarray(inputs["L0_attn_dense_w"], f)
    gwv = np.asarray(inputs["L0_ffn_gate_weight"], f)
    d = {}
    d["wg"] = np.ascontiguousarray(gp.reshape(E, ICN, 128, HCN, 128).transpose(0, 1, 4, 3, 2).reshape(E, ICN, 128, 2048))
    d["wu"] = np.ascontiguousarray(up.reshape(E, ICN, 128, HCN, 128).transpose(0, 1, 4, 3, 2).reshape(E, ICN, 128, 2048))
    d["wd"] = np.ascontiguousarray(dp.reshape(E, HCN, 128, ICN, 128).transpose(0, 1, 4, 3, 2).reshape(E, HCN, 128, 1408))
    d["wsg"] = np.ascontiguousarray(sg.reshape(SCN, 128, HCN, 128).transpose(0, 3, 2, 1).reshape(SCN, 128, 2048))
    d["wsu"] = np.ascontiguousarray(su.reshape(SCN, 128, HCN, 128).transpose(0, 3, 2, 1).reshape(SCN, 128, 2048))
    d["wsd"] = np.ascontiguousarray(sd.reshape(HCN, 128, SCN, 128).transpose(0, 3, 2, 1).reshape(HCN, 128, 2816))
    d["wqkv"] = np.ascontiguousarray(qkv.reshape(HCN, 128, 48, 128).transpose(2, 1, 0, 3).reshape(48, 128, 2048))
    d["wdense"] = np.ascontiguousarray(dw.reshape(HCN, 128, HCN, 128).transpose(2, 1, 0, 3).reshape(HCN, 128, 2048))
    d["gwt"] = np.ascontiguousarray(gwv.T.reshape(HCN, 128, 16))
    return d


def _bucket_of_delta(n):
    n = np.asarray(n)
    nf = np.maximum(n, 1).astype(np.float64)
    large = (np.log(nf / 16) / np.log(8.0) * 16).astype(np.int64) + 16
    large = np.minimum(large, 31)
    return np.where(n < 16, n, large)


def _consts():
    oh = np.zeros((32, 768), np.float32)
    b = _bucket_of_delta(np.arange(768))
    oh[b, np.arange(768)] = 1.0
    e31v = np.zeros((32, 1), np.float32)
    e31v[31, 0] = 1.0
    identf = np.eye(128, dtype=np.float32)
    nrampv = np.ascontiguousarray(
        np.tile(-(np.arange(16, dtype=np.float32) + 1.0) * 1e-30, 8)[None, :])
    return oh, e31v, identf, nrampv


def _rope_tables(tok_idx):
    inv = 1.0 / 10000.0 ** (np.arange(0, HD, 2, dtype=np.float32) / HD)
    ang = tok_idx[:, None].astype(np.float32) * inv[None, :]
    cos = np.cos(ang).astype(np.float32).T
    sin = np.sin(ang).astype(np.float32).T
    s = np.float32(1.0 / np.sqrt(HD))
    rep = lambda a: np.ascontiguousarray(np.concatenate([a, a], axis=0))
    return rep(cos * s), rep(sin * s), rep(cos), rep(sin)


def prep_in_maps(inputs):
    x = np.ascontiguousarray(np.asarray(inputs["L0_attn_x"], np.float32).reshape(S, H))
    w = _retile(inputs)
    oh, e31v, identf, nrampv = _consts()
    table = np.ascontiguousarray(np.asarray(inputs["L0_attn_rel_attn_table"], np.float32))
    escb_rep = np.ascontiguousarray(
        np.tile(np.asarray(inputs["L0_ffn_e_score_correction_bias"], np.float32), 8)[None, :])
    in_maps = []
    for c in range(NC):
        rows_idx = np.concatenate([np.arange(512 * c, 512 * (c + 1)),
                                   np.arange(512 * (15 - c), 512 * (16 - c))])
        xs = np.ascontiguousarray(x[rows_idx])
        cq, sq, ck, sk = _rope_tables(rows_idx)
        perm = [2 * c, 2 * c + 1] + [hh for hh in range(16) if hh not in (2 * c, 2 * c + 1)]
        tp = np.ascontiguousarray(table[:, perm])
        m = {"xin": xs, "cqr": cq, "sqr": sq, "ckr": ck, "skr": sk, "tablep": tp,
             "escb": escb_rep, "nramp": nrampv, "onehot": oh, "e31": e31v, "identf": identf}
        m.update(w)
        in_maps.append(m)
    return in_maps


def kernel(**inputs):
    in_maps = prep_in_maps(inputs)
    nc = build_program()
    res = run_bass_kernel_spmd(nc, in_maps, core_ids=list(range(NC)))
    full = np.empty((S, H), np.float32)
    for c in range(NC):
        o = np.asarray(res.results[c]["out"])
        full[512 * c:512 * (c + 1)] = o[:512]
        full[512 * (15 - c):512 * (16 - c)] = o[512:]
    return full.reshape(1, S, H)
